# version 01
# baseline (speedup 1.0000x reference)
import numpy as np
import ml_dtypes
import concourse.bass as bass
import concourse.mybir as mybir
from concourse.bass_utils import run_bass_kernel_spmd
from contextlib import ExitStack

F32 = mybir.dt.float32
BF16 = mybir.dt.bfloat16
U8 = mybir.dt.uint8
AF = mybir.ActivationFunctionType
ALU = mybir.AluOpType
AX = mybir.AxisListType.X

D = 2048
NCH = 16
TPC = 1024
HALO = 256
TEXT = TPC + HALO
NCORES = 8
M_MEM = 4096
MSL = M_MEM // NCORES
NFF = 16384
NSC = 32
SCW = NFF // NSC
H = 16
HD = 128
INV_SQRT_D = float(1.0 / np.sqrt(2048.0))
INV_SQRT_HD = float(1.0 / np.sqrt(128.0))

C_M4, C_M6, C_75, C_EPS5, C_EPS6, C_ONE, C_ZERO, C_SPARE = range(8)


def _pi(n):
    idx = np.arange(n)
    i, p = idx // 128, idx % 128
    return 256 * (i // 2) + 2 * p + (i % 2)


PI_D = _pi(2048)
PI_N = _pi(16384)


class S:
    def __init__(self, stack, nc, name):
        self.h = stack.enter_context(nc.semaphore(name))
        self.n = 0


class Prog:
    ENGS = ("pe", "act", "dve", "pool", "sp")

    def __init__(self, nc, stack):
        self.nc = nc
        self.stack = stack
        self.q = {e: [] for e in self.ENGS}
        self.bar = S(stack, nc, "bar")
        self.nbar = 0

    def sem(self, name):
        return S(self.stack, self.nc, name)

    def E(self, eng, method, *args, wait=None, inc=None, **kwargs):
        if wait:
            wait = [(s.h, t) for (s, t) in wait if t > 0]

        def f(h, method=method, args=args, kwargs=kwargs, wait=wait, inc=inc):
            if wait:
                for sh, t in wait:
                    h.wait_ge(sh, t)
            inst = getattr(h, method)(*args, **kwargs)
            if inc is not None:
                inst.then_inc(inc[0].h, inc[1])
        self.q[eng].append(f)
        if inc is not None:
            inc[0].n += inc[1]
            return inc[0].n
        return None

    def W(self, eng, sem, tok=None):
        t = sem.n if tok is None else tok
        if t > 0:
            self.q[eng].append(lambda h, sh=sem.h, t=t: h.wait_ge(sh, t))

    def barrier(self):
        self.nbar += 1
        tgt = 5 * self.nbar
        for e in self.ENGS:
            self.q[e].append(lambda h, sh=self.bar.h: h.sem_inc(sh, 1))
            self.q[e].append(lambda h, sh=self.bar.h, t=tgt: h.wait_ge(sh, t))
        self.bar.n = tgt

    def finalize(self):
        with self.nc.Block() as block:
            @block.tensor
            def _(h):
                for f in self.q["pe"]:
                    f(h)

            @block.scalar
            def _(h):
                for f in self.q["act"]:
                    f(h)

            @block.vector
            def _(h):
                for f in self.q["dve"]:
                    f(h)

            @block.gpsimd
            def _(h):
                for f in self.q["pool"]:
                    f(h)

            @block.sync
            def _(h):
                for f in self.q["sp"]:
                    f(h)


class Ctx:
    pass



def host_prepare(inp):
    out = {}
    asc = np.ascontiguousarray

    def bview(a):
        a = np.asarray(a)
        return a.view(np.uint8)[:, ::4]

    out["wg_pt"] = asc(bview(inp["L0_ffn_gate_W_packed"])[PI_N].T)
    out["wu_pt"] = asc(bview(inp["L0_ffn_up_W_packed"])[PI_N].T)
    out["wd_pt"] = asc(bview(inp["L0_ffn_down_W_packed"])[PI_D].T)
    out["sc_g"] = asc(np.asarray(inp["L0_ffn_gate_scales"], np.float32)[PI_N].T)
    out["sc_u"] = asc(np.asarray(inp["L0_ffn_up_scales"], np.float32)[PI_N].T)
    out["sc_d"] = asc(np.asarray(inp["L0_ffn_down_scales"], np.float32)[PI_D].T)

    def f32(name):
        return np.asarray(inp[name], np.float32)

    out["wqT0"] = asc(f32("L0_attn_wq").T[PI_D])
    out["wkT0"] = asc(f32("L0_attn_wk").T[PI_D])
    out["wvT0"] = asc(f32("L0_attn_wv").T[PI_D])
    out["woT0"] = asc(f32("L0_attn_wo").T[:, PI_D])
    out["wkTm"] = asc(f32("L0_attn_wk").T[:, PI_D])
    out["wvTm"] = asc(f32("L0_attn_wv").T)
    out["gwT0"] = asc(f32("L0_attn_gate_w")[0][PI_D].reshape(2048, 1))
    out["gb0"] = f32("L0_attn_gate_b").reshape(1, 1)
    memT = asc(f32("L0_attn_mem_bank").T)
    out["memT_sl"] = [asc(memT[:, c * MSL:(c + 1) * MSL]) for c in range(NCORES)]

    out["wqT1"] = asc(f32("L1_attn_wq").T[PI_D])
    out["wkT1"] = asc(f32("L1_attn_wk").T[PI_D])
    out["wvT1"] = asc(f32("L1_attn_wv").T[PI_D])
    out["woT1"] = asc(f32("L1_attn_wo").T[:, PI_D])
    out["bq1"] = asc(f32("L1_attn_bq").reshape(2048, 1))
    out["bk1"] = asc(f32("L1_attn_bk").reshape(2048, 1))
    out["bv1"] = asc(f32("L1_attn_bv").reshape(2048, 1))
    out["bo1"] = asc(f32("L1_attn_bo")[PI_D].reshape(2048, 1))
    out["wmqT"] = asc(f32("L1_attn_wmap_q").T)
    out["wmkT"] = asc(f32("L1_attn_wmap_k").T)

    out["bWT1"] = asc(f32("L1_ffn_base_W").T[PI_D][:, PI_D])
    out["rWT1"] = asc(f32("L1_ffn_router_W").T[PI_D])
    out["AT1"] = asc(f32("L1_ffn_A").T[PI_D])
    out["BT1"] = asc(f32("L1_ffn_B").T[:, PI_D])
    out["bias1"] = asc(f32("L1_ffn_biases").reshape(64, 1))

    def sliced(wT, ow):
        nO = wT.shape[1] // ow
        return asc(wT.reshape(16, 128, nO, ow).transpose(2, 1, 0, 3)
                   .reshape(nO, 128, 16 * ow))

    out["wq_l0"] = sliced(out["wqT0"], 128)
    out["wk_l0"] = sliced(out["wkT0"], 128)
    out["wv_l0"] = sliced(out["wvT0"], 128)
    out["wo_l0"] = sliced(out["woT0"], 128)
    out["wkm_l"] = sliced(out["wkTm"], 128)
    out["wvm_l"] = sliced(out["wvTm"], 512)
    out["gw_pc"] = asc(out["gwT0"].reshape(16, 128).T)
    out["bW_l"] = sliced(out["bWT1"], 128)
    out["rw_pc"] = asc(out["rWT1"].reshape(16, 128, 32).transpose(1, 0, 2)
                       .reshape(128, 512))
    out["at_pc"] = asc(out["AT1"].reshape(16, 128, 64).transpose(1, 0, 2)
                       .reshape(128, 1024))
    out["wq_l1"] = sliced(out["wqT1"], 128)
    out["wk_l1"] = sliced(out["wkT1"], 128)
    out["wv_l1"] = sliced(out["wvT1"], 128)
    out["wo_l1"] = sliced(out["woT1"], 128)
    bqkv = np.zeros((128, 48), np.float32)
    for j, nm in enumerate(("bq1", "bk1", "bv1")):
        bqkv[:, j::3] = out[nm].reshape(16, 128).T
    out["bqkv_pc"] = asc(bqkv)
    out["bo_pc"] = asc(out["bo1"].reshape(16, 128).T)

    x0 = np.asarray(inp["L0_attn_hidden_states"], np.float32).reshape(8192, 2048)
    out["x_sl"] = [asc(x0[c * TPC:(c + 1) * TPC]) for c in range(NCORES)]

    consts = np.zeros((128, 8), np.float32)
    consts[:, C_M4] = -4.0
    consts[:, C_M6] = -6.0
    consts[:, C_75] = 7.5
    consts[:, C_EPS5] = 1e-5
    consts[:, C_EPS6] = 1e-6
    consts[:, C_ONE] = 1.0
    out["consts"] = consts
    out["ones_row"] = np.ones((1, 128), np.float32)
    out["ident_f"] = np.eye(128, dtype=np.float32)
    out["ident_b"] = np.eye(128).astype(ml_dtypes.bfloat16)
    r64 = np.zeros((32, 64), np.float32)
    r64[np.arange(32), 2 * np.arange(32)] = 1.0
    r64[np.arange(32), 2 * np.arange(32) + 1] = 1.0
    out["r64"] = r64

    NEG = np.float32(-1.0e30)
    masks = np.zeros((NCORES, 8, 128, 384), np.float32)
    a = np.arange(128)[:, None]
    cc = np.arange(128)[None, :]
    tri_up = np.where(cc > a, 0.0, NEG).astype(np.float32)
    tri_lo = np.where(cc <= a, 0.0, NEG).astype(np.float32)
    for core in range(NCORES):
        for qt in range(8):
            m = masks[core, qt]
            m[:, 0:128] = tri_up
            m[:, 128:256] = 0.0
            m[:, 256:384] = tri_lo
            if core == 0 and qt == 0:
                m[:, 0:256] = NEG
            elif core == 0 and qt == 1:
                m[:, 0:128] = NEG
    out["masks"] = masks
    return out



def ap2(t, p0, pn, f0, fn, fstep=1):
    rowlen = t.shape[-1]
    return bass.AP(t, p0 * rowlen + f0, [[rowlen, pn], [fstep, fn]])


def cb(st, col):
    return st.consts[:, col:col + 1]


def xch(st, i, f0, fn):
    return st.x[:, i * TPC + f0: i * TPC + f0 + fn]


def ych(st, i, f0, fn):
    t = st.text
    return st.y[:, i * t + f0: i * t + f0 + fn]


def yown(st, i, f0, fn):
    return ych(st, i, (st.text - 1024) + f0, fn)


def setup_common(P, nc, *, text=TPC, want_identb=True, nps=6, npsT=0):
    st = Ctx()
    st.text = text
    st.consts_d = nc.dram_tensor("consts", [128, 8], F32, kind="ExternalInput").ap()
    st.ones_row_d = nc.dram_tensor("ones_row", [1, 128], F32, kind="ExternalInput").ap()
    st.ident_f_d = nc.dram_tensor("ident_f", [128, 128], F32, kind="ExternalInput").ap()
    ec = P.stack.enter_context
    st.consts = ec(nc.sbuf_tensor("consts_sb", [128, 8], F32))
    st.ones_row = ec(nc.sbuf_tensor("ones_row_sb", [1, 128], F32))
    st.ident_f = ec(nc.sbuf_tensor("ident_f_sb", [128, 128], F32))
    if want_identb:
        st.ident_b_d = nc.dram_tensor("ident_b", [128, 128], BF16,
                                      kind="ExternalInput").ap()
        st.ident_b = ec(nc.sbuf_tensor("ident_b_sb", [128, 128], BF16))

    st.ld = P.sem("ld_misc")
    P.E("sp", "dma_start", out=st.consts[:], in_=st.consts_d[:], inc=(st.ld, 16))
    P.E("sp", "dma_start", out=st.ones_row[:], in_=st.ones_row_d[:], inc=(st.ld, 16))
    P.E("sp", "dma_start", out=st.ident_f[:], in_=st.ident_f_d[:], inc=(st.ld, 16))
    if want_identb:
        P.E("sp", "dma_start", out=st.ident_b[:], in_=st.ident_b_d[:], inc=(st.ld, 16))

    st.x = ec(nc.sbuf_tensor("x_sb", [128, NCH * TPC], F32))
    st.y = ec(nc.sbuf_tensor("y_sb", [128, NCH * text], BF16))

    st.ps = [ec(nc.psum_tensor(f"ps{i}", [128, 512], F32)) for i in range(nps)]
    st.psS = ec(nc.psum_tensor("psS", [128, 512], F32))
    st.psT = [ec(nc.psum_tensor(f"psT{i}", [128, 1024], BF16)) for i in range(npsT)]
    return st


def emit_load_x(P, st, x_d, sem_done, halo_d=None, halo_sb=None):
    nc = P.nc
    _ph = ExitStack()
    xin = _ph.enter_context(nc.sbuf_tensor("xin_tm", [128, 2048], F32))
    ld = P.sem("ld_x")
    pv = P.sem("pe_xload")
    av = P.sem("act_xload")
    jobs = [(x_d, lambda i, t0: xch(st, i, t0, 128), TPC // 128)]
    if halo_d is not None:
        jobs.append((halo_d,
                     lambda i, t0: halo_sb[:, i * 256 + t0: i * 256 + t0 + 128],
                     HALO // 128))
    slot = 0
    for src_d, dst_fn, ntt in jobs:
        for tt in range(ntt):
            w = [(st.ld, st.ld.n)]
            if slot > 0:
                w.append((av, max(0, av.n - 0)))
            t_ld = P.E("sp", "dma_start", out=xin[:], in_=src_d[tt * 128:(tt + 1) * 128, :],
                       wait=w, inc=(ld, 16))
            for i in range(NCH):
                c, s_ = i // 2, i % 2
                src = ap2(xin, 0, 128, 256 * c + s_, 128, fstep=2)
                k = slot * NCH + i
                pbank = st.psS if k % 2 == 0 else st.ps[0]
                col = ((k // 2) % 2) * 128
                pw = [(ld, t_ld), (st.ld, st.ld.n)]
                if k >= 2:
                    pw.append((av, av.n - 1))
                P.E("pe", "transpose", pbank[:, col:col + 128], src,
                    st.ident_f[:], wait=pw, inc=(pv, 1))
                P.E("act", "copy", dst_fn(i, tt * 128), pbank[:, col:col + 128],
                    wait=[(pv, pv.n)], inc=(av, 1))
            slot += 1
    P.W("act", av)
    P.E("act", "nop", inc=(sem_done, 1))
    _ph.close()


def emit_store_x(P, st, out_d, sem_done):
    nc = P.nc
    _ph = ExitStack()
    xout = _ph.enter_context(nc.sbuf_tensor("xout_tm", [128, 2048], F32))
    pv = P.sem("pe_xst")
    av = P.sem("act_xst")
    sv = P.sem("st_x")
    t_id = P.E("sp", "dma_start", out=st.ident_f[:], in_=st.ident_f_d[:],
               inc=(st.ld, 16))
    P.W("pe", st.ld, t_id)
    for tt in range(8):
        for i in range(NCH):
            c, s_ = i // 2, i % 2
            k = tt * NCH + i
            pbank = st.psS if k % 2 == 0 else st.ps[0]
            col = ((k // 2) % 2) * 128
            pw = []
            if k >= 2:
                pw.append((av, av.n - 1))
            if tt:
                pw.append((sv, tt * 16))
            P.E("pe", "transpose", pbank[:, col:col + 128],
                xch(st, i, tt * 128, 128), st.ident_f[:],
                wait=pw if pw else None, inc=(pv, 1))
            dst = ap2(xout, 0, 128, 256 * c + s_, 128, fstep=2)
            P.E("act", "copy", dst, pbank[:, col:col + 128],
                wait=[(pv, pv.n)], inc=(av, 1))
        P.E("sp", "dma_start", out=out_d[tt * 128:(tt + 1) * 128, :], in_=xout[:],
            wait=[(av, av.n)], inc=(sv, 16))
    P.W("sp", sv)
    P.E("sp", "nop", inc=(sem_done, 1))
    _ph.close()


def emit_ln(P, st, ext_f0, n, sem_ready, sem_done, x_src=None, tag=""):
    nc = P.nc
    if x_src is None:
        x_src = lambda i, f0, w: xch(st, i, f0, w)
    _ph = ExitStack()
    ec = _ph.enter_context
    sq = ec(nc.sbuf_tensor(f"ln_sq{tag}", [128, 1024], F32))
    srow = ec(nc.sbuf_tensor(f"ln_srow{tag}", [1, 1024], F32))
    rr = ec(nc.sbuf_tensor(f"ln_rr{tag}", [1, 1024], F32))
    rb = ec(nc.sbuf_tensor(f"ln_rb{tag}", [128, 1024], F32))
    pv = P.sem(f"ln_pe{tag}")
    av = P.sem(f"ln_act{tag}")
    dv = P.sem(f"ln_dve{tag}")
    wid = min(n, 512)
    for u in range(max(1, n // 512)):
        f0 = u * wid
        for i in range(NCH):
            w = None
            if i == 0:
                w = [(sem_ready, sem_ready.n)]
                if u:
                    w.append((dv, dv.n))
            P.E("pe", "matmul", st.psS[0:1, 0:wid], cb(st, C_ONE),
                x_src(i, f0, wid), start=(i == 0), stop=(i == NCH - 1), wait=w,
                inc=(pv, 1) if i == NCH - 1 else None)
        for i in range(NCH):
            aw = [(sem_ready, sem_ready.n)] if i == 0 else []
            if u and i < 2:
                aw.append((dv, dv.n))
            if i >= 2:
                aw.append((pv, pv.n))
            P.E("act", "activation", sq[:, (i % 2) * 512:(i % 2) * 512 + wid],
                x_src(i, f0, wid), AF.Square, wait=aw if aw else None, inc=(av, 1))
            P.E("pe", "matmul", st.ps[4][0:1, 0:wid], cb(st, C_ONE),
                sq[:, (i % 2) * 512:(i % 2) * 512 + wid],
                start=(i == 0), stop=(i == NCH - 1),
                wait=[(av, av.n)], inc=(pv, 1))
        t_stats = pv.n
        P.E("dve", "tensor_scalar_mul", srow[0:1, 0:wid], st.psS[0:1, 0:wid],
            1.0 / 2048.0, wait=[(pv, t_stats)])
        P.E("dve", "tensor_scalar_mul", srow[0:1, 512:512 + wid], st.ps[4][0:1, 0:wid],
            1.0 / 2048.0)
        P.E("dve", "tensor_mul", rr[0:1, 512:512 + wid], srow[0:1, 0:wid],
            srow[0:1, 0:wid])
        t_d = P.E("dve", "tensor_sub", rr[0:1, 0:wid], srow[0:1, 512:512 + wid],
                  rr[0:1, 512:512 + wid], inc=(dv, 1))
        t_a = P.E("act", "activation", srow[0:1, 512:512 + wid], rr[0:1, 0:wid],
                  AF.Sqrt, bias=st.consts[0:1, C_EPS5:C_EPS5 + 1],
                  wait=[(dv, t_d)], inc=(av, 1))
        P.E("dve", "reciprocal", rr[0:1, 0:wid], srow[0:1, 512:512 + wid],
            wait=[(av, t_a)])
        t_d = P.E("dve", "scalar_tensor_tensor", rr[0:1, 512:512 + wid],
                  srow[0:1, 0:wid], -1.0, rr[0:1, 0:wid], ALU.mult, ALU.mult,
                  inc=(dv, 1))
        P.E("pe", "matmul", st.psS[:, 0:wid], st.ones_row[:], rr[0:1, 0:wid],
            start=True, stop=True, wait=[(dv, t_d)])
        t_pe = P.E("pe", "matmul", st.ps[4][:, 0:wid], st.ones_row[:],
                   rr[0:1, 512:512 + wid], start=True, stop=True, inc=(pv, 1))
        P.E("dve", "tensor_copy", rb[:, 0:wid], st.psS[:, 0:wid], wait=[(pv, t_pe)])
        t_d = P.E("dve", "tensor_copy", rb[:, 512:512 + wid], st.ps[4][:, 0:wid],
                  inc=(dv, 1))
        for i in range(NCH):
            P.E("dve", "tensor_mul", sq[:, (i % 2) * 512:(i % 2) * 512 + wid],
                x_src(i, f0, wid), rb[:, 0:wid],
                wait=[(dv, t_d), (av, av.n), (pv, pv.n)] if i < 2 else None)
            P.E("dve", "tensor_add", ych(st, i, ext_f0 + u * wid, wid),
                sq[:, (i % 2) * 512:(i % 2) * 512 + wid], rb[:, 512:512 + wid],
                inc=(dv, 1))
    P.W("dve", dv)
    P.E("dve", "nop", inc=(sem_done, 1))
    _ph.close()



def emit_ffn0(P, st, wd, sem_y, sem_done):
    nc = P.nc
    _ph = ExitStack()
    ec = _ph.enter_context
    wg_sb = [ec(nc.sbuf_tensor(f"wg{b}", [128, NCH * 512], BF16)) for b in range(2)]
    wu_sb = [ec(nc.sbuf_tensor(f"wu{b}", [128, NCH * 512], BF16)) for b in range(2)]
    wdn_sb = [ec(nc.sbuf_tensor(f"wdn{b}", [128, 4 * 2048], BF16)) for b in range(2)]
    pk_sb = ec(nc.sbuf_tensor("pk_sb", [128, 2 * 512], U8))
    vdec = ec(nc.sbuf_tensor("vdec_sb", [128, 3 * 512], U8))
    scs_sb = ec(nc.sbuf_tensor("scs_sb", [128, 512], F32))
    tmp = ec(nc.sbuf_tensor("dec_tmp", [128, 8 * 512], BF16))
    gel = ec(nc.sbuf_tensor("gelu_sb", [128, 512], BF16))
    inter = ec(nc.sbuf_tensor("inter_sb", [128, 4 * 1024], BF16))

    s_ld = P.sem("f0_ld")
    s_dd = P.sem("f0_dd")
    s_da = P.sem("f0_da")
    s_pe = P.sem("f0_pe")
    s_ga = P.sem("f0_ga")
    s_iv = P.sem("f0_iv")
    s_xa = P.sem("f0_xa")

    dec_done = {}
    pe_done = {}
    slot_free_tok = {}
    last_mphase_tok = {}
    last_sgread_tok = {}
    last_relu = [0]
    last_gelu_tok = {}
    last_inter_tok = {}
    down_xa_tok = {}
    gu_ctr = [0]

    def unit_list(j):
        nsl = j * SCW
        units = []
        for c in range(8):
            units.append(("g",
                          wd["wg_pt"][c * 128:(c + 1) * 128, nsl:nsl + 512],
                          bass.AP(wd["sc_g"].tensor, (8 * c) * NFF + nsl,
                                  [[NFF, 8], [0, 16], [1, 512]]),
                          wg_sb[j % 2], 2 * c, 512, 1.0))
            units.append(("u",
                          wd["wu_pt"][c * 128:(c + 1) * 128, nsl:nsl + 512],
                          bass.AP(wd["sc_u"].tensor, (8 * c) * NFF + nsl,
                                  [[NFF, 8], [0, 16], [1, 512]]),
                          wu_sb[j % 2], 2 * c, 512, 1.0))
        for cp in range(2):
            cpr = 2 * j + cp
            for strip in range(4):
                units.append(("d",
                              wd["wd_pt"][cpr * 128:(cpr + 1) * 128,
                                          strip * 512:(strip + 1) * 512],
                              bass.AP(wd["sc_d"].tensor,
                                      (8 * cpr) * 2048 + strip * 512,
                                      [[2048, 8], [0, 16], [1, 512]]),
                              wdn_sb[j % 2], 2 * cp, strip * 512, 0.5))
        return units

    def stream_sc(j):
        first_dec_w = [(s_pe, pe_done[j - 2])] if j >= 2 else []
        for iu, unit in enumerate(unit_list(j)):
            kind, pk_src, scs_src, dst, chb, extra, fold = unit
            gu = gu_ctr[0]
            gu_ctr[0] += 1
            slot = gu % 2
            so = slot * 512
            lw = [(s_dd, slot_free_tok[slot])] if slot in slot_free_tok else None
            P.E("sp", "dma_start", out=pk_sb[:, so:so + 512], in_=pk_src,
                wait=lw, inc=(s_ld, 16))
            sw = [(s_dd, last_mphase_tok[1])] if 1 in last_mphase_tok else None
            t_l2 = P.E("sp", "dma_start", out=scs_sb[:, 0:512], in_=scs_src,
                       wait=sw, inc=(s_ld, 16))
            up = gu % 2
            v_t = vdec[:, up * 512: up * 512 + 512]
            r_t = vdec[:, 1024:1536]
            rb_t = tmp[:, 0:512]
            m_t = tmp[:, 512:1024]
            for s_i in (0, 1):
                npar = s_i
                sg_t = tmp[:, 1024 + npar * 512: 1024 + npar * 512 + 512]
                a_t = tmp[:, 2048 + npar * 512: 2048 + npar * 512 + 512]
                b_t = tmp[:, 3072 + npar * 512: 3072 + npar * 512 + 512]
                vw = [(s_ld, t_l2)]
                if s_i == 0:
                    vw += (first_dec_w if iu == 0 else [])
                    if up in last_sgread_tok:
                        vw.append((s_da, last_sgread_tok[up]))
                    P.E("dve", "tensor_single_scalar", v_t, pk_sb[:, so:so + 512],
                        15, ALU.bitwise_and, wait=vw)
                else:
                    if last_relu[0]:
                        vw.append((s_da, last_relu[0]))
                    P.E("dve", "tensor_single_scalar", v_t, pk_sb[:, so:so + 512],
                        4, ALU.logical_shift_right, wait=vw)
                rw = [(s_da, last_relu[0])] if last_relu[0] else None
                P.E("dve", "tensor_single_scalar", r_t, v_t, 7, ALU.bitwise_and,
                    wait=rw)
                t_r = P.E("dve", "tensor_copy", rb_t, r_t, inc=(s_dd, 1))
                aw = [(s_dd, t_r)]
                if npar in last_mphase_tok:
                    aw.append((s_dd, last_mphase_tok[npar]))
                P.E("act", "activation", sg_t, v_t, AF.Sign, scale=-1.0,
                    bias=cb(st, C_75), wait=aw)
                P.E("act", "activation", a_t, r_t, AF.Relu, bias=cb(st, C_M4))
                t_a = P.E("act", "activation", b_t, r_t, AF.Relu, bias=cb(st, C_M6),
                          inc=(s_da, 1))
                last_relu[0] = t_a
                if s_i == 1:
                    last_sgread_tok[up] = t_a
                P.E("dve", "tensor_add", m_t, rb_t, a_t, wait=[(s_da, t_a)])
                P.E("dve", "scalar_tensor_tensor", m_t, m_t, 0.5, b_t, ALU.mult, ALU.add)
                P.E("dve", "tensor_mul", m_t, m_t, sg_t)
                if kind == "d":
                    dsto = (chb + s_i) * 2048 + extra
                else:
                    dsto = (chb + s_i) * 512
                t_m = P.E("dve", "scalar_tensor_tensor", dst[:, dsto:dsto + 512],
                          scs_sb[:, 0:512], fold, m_t, ALU.mult, ALU.mult,
                          inc=(s_dd, 1))
                last_mphase_tok[npar] = t_m
            slot_free_tok[slot] = last_mphase_tok[1]
        dec_done[j] = s_dd.n

    def matmul_sc(j):
        b = j % 2
        ib = j % 2
        for un in range(8):
            ln, tt2 = un // 2, un % 2
            gup = (j * 8 + un) % 2
            gps, ups = st.ps[gup], st.ps[2 + gup]
            w0 = [(s_dd, dec_done[j]), (sem_y, sem_y.n)]
            if gup in last_gelu_tok:
                w0.append((s_ga, last_gelu_tok[gup]))
            if gup in last_inter_tok:
                w0.append((s_iv, last_inter_tok[gup]))
            for i in range(NCH):
                P.E("pe", "matmul", gps[:],
                    wg_sb[b][:, i * 512 + ln * 128: i * 512 + ln * 128 + 128],
                    yown(st, i, tt2 * 512, 512),
                    start=(i == 0), stop=(i == NCH - 1), wait=w0 if i == 0 else None)
            t_pe = None
            for i in range(NCH):
                t_pe = P.E("pe", "matmul", ups[:],
                           wu_sb[b][:, i * 512 + ln * 128: i * 512 + ln * 128 + 128],
                           yown(st, i, tt2 * 512, 512),
                           start=(i == 0), stop=(i == NCH - 1),
                           inc=(s_pe, 1) if i == NCH - 1 else None)
            gw = [(s_pe, t_pe)]
            if last_inter_tok:
                gw.append((s_iv, max(last_inter_tok.values())))
            t_g = P.E("act", "activation", gel[:, 0:512], gps[:],
                      AF.Gelu_apprx_tanh, wait=gw, inc=(s_ga, 1))
            last_gelu_tok[gup] = t_g
            t_i = P.E("dve", "tensor_mul",
                      inter[:, ln * 1024 + tt2 * 512: ln * 1024 + tt2 * 512 + 512],
                      gel[:, 0:512], ups[:],
                      wait=[(s_ga, t_g), (s_pe, t_pe)], inc=(s_iv, 1))
            last_inter_tok[gup] = t_i
        t_inter_all = s_iv.n
        for dch in range(NCH):
            for tt2 in range(2):
                dp = (dch * 2 + tt2) % 2
                dps = st.ps[4 + dp]
                w0 = [(s_iv, t_inter_all)]
                if dp in down_xa_tok:
                    w0.append((s_xa, down_xa_tok[dp]))
                t_pe = None
                for ln in range(4):
                    t_pe = P.E("pe", "matmul", dps[:],
                               wdn_sb[b][:, ln * 2048 + dch * 128: ln * 2048 + dch * 128 + 128],
                               inter[:, ln * 1024 + tt2 * 512: ln * 1024 + tt2 * 512 + 512],
                               start=(ln == 0), stop=(ln == 3),
                               wait=w0 if ln == 0 else None,
                               inc=(s_pe, 1) if ln == 3 else None)
                t_x = P.E("dve", "tensor_add", xch(st, dch, tt2 * 512, 512),
                          xch(st, dch, tt2 * 512, 512), dps[:],
                          wait=[(s_pe, t_pe)], inc=(s_xa, 1))
                down_xa_tok[dp] = t_x
        pe_done[j] = s_pe.n

    for j in range(NSC):
        stream_sc(j)
        matmul_sc(j)
    P.W("dve", s_xa)
    P.E("dve", "nop", inc=(sem_done, 1))
    _ph.close()



def emit_attn0(P, st, wd, sem_y, sem_done, stop_after=None):
    nc = P.nc
    ec = P.stack.enter_context
    s_ld = P.sem("a0_ld")
    s_cl = P.sem("a0_cast")
    s_pe = P.sem("a0_pe")
    s_av = P.sem("a0_act")
    s_dv = P.sem("a0_dve")
    s_cc = P.sem("a0_cc")
    s_st = P.sem("a0_st")

    gw_sb = ec(nc.sbuf_tensor("gw_sb", [128, 16], BF16))
    gwf_sb = ec(nc.sbuf_tensor("gwf_sb", [128, 16], F32))
    gb_sb = ec(nc.sbuf_tensor("gb_sb", [128, 1], F32))
    byp_sb = ec(nc.sbuf_tensor("byp_sb", [128, 8], F32))

    agk_in = nc.dram_tensor("agk_in", [2048, MSL], BF16)
    agk_out = nc.dram_tensor("agk_out", [8 * 2048, MSL], BF16)
    agv_in = nc.dram_tensor("agv_in", [MSL, 2048], BF16)
    agv_out = nc.dram_tensor("agv_out", [M_MEM, 2048], BF16)
    with ExitStack() as ph:
        memT_sb = ph.enter_context(nc.sbuf_tensor("memT_sb", [128, 16 * MSL], BF16))
        wk_sb = ph.enter_context(nc.sbuf_tensor("wkm_sb", [128, 16 * 128], BF16))
        kms = ph.enter_context(nc.sbuf_tensor("kms_sb", [128, 16 * MSL], BF16))
        wvm_sb = ph.enter_context(nc.sbuf_tensor("wvm_sb", [128, 16 * 512], BF16))
        vms = ph.enter_context(nc.sbuf_tensor("vms_sb", [128, 4 * 2048], BF16))
        t_mem = P.E("pool", "dma_start", out=memT_sb[:],
                    in_=wd["memT_sl"].rearrange("(c p) m -> p c m", p=128),
                    inc=(s_cl, 16))
        if stop_after == "p0ld":
            P.W("pool", s_cl)
            P.E("pool", "nop", inc=(sem_done, 1))
        for dk in range(0 if stop_after == 'p0ld' else 16):
            t_w = P.E("pool", "dma_start", out=wk_sb[:],
                      in_=wd["wkm_l"][dk, :, :],
                      wait=[(s_av, s_av.n)] if dk else None, inc=(s_cl, 16))
            if stop_after == "p0w":
                continue
            for u in range(MSL // 512):
                pp = st.ps[(dk + u) % 2]
                t_pe = None
                for i in range(NCH):
                    t_pe = P.E("pe", "matmul", pp[:],
                               wk_sb[:, i * 128:(i + 1) * 128],
                               memT_sb[:, i * MSL + u * 512: i * MSL + u * 512 + 512],
                               start=(i == 0), stop=(i == NCH - 1),
                               wait=[(s_cl, s_cl.n), (s_av, max(0, s_av.n - 1))] if i == 0 else None,
                               inc=(s_pe, 1) if i == NCH - 1 else None)
                P.E("act", "activation", kms[:, dk * MSL + u * 512: dk * MSL + u * 512 + 512],
                    pp[:], AF.Copy, scale=INV_SQRT_D, wait=[(s_pe, t_pe)], inc=(s_av, 1))
        t_kst = None
        if stop_after not in ('p0ld', 'p0k', 'p0w'):
            t_kst = P.E("sp", "dma_start",
                        out=agk_in.ap().rearrange("(c p) m -> p c m", p=128),
                        in_=kms[:], wait=[(s_av, s_av.n)], inc=(s_ld, 16))
        for dvs in range(4 if stop_after not in ('p0ld', 'p0k', 'p0w') else 0):
            t_w = P.E("pool", "dma_start", out=wvm_sb[:],
                      in_=wd["wvm_l"][dvs, :, :],
                      wait=[(s_av, s_av.n)] if dvs else None, inc=(s_cl, 16))
            for mb in range(4):
                pp = st.ps[2 + mb % 2]
                t_pe = None
                for i in range(NCH):
                    t_pe = P.E("pe", "matmul", pp[:],
                               memT_sb[:, i * MSL + mb * 128: i * MSL + mb * 128 + 128],
                               wvm_sb[:, i * 512:(i + 1) * 512],
                               start=(i == 0), stop=(i == NCH - 1),
                               wait=[(s_cl, s_cl.n), (s_av, max(0, s_av.n - 1))] if i == 0 else None,
                               inc=(s_pe, 1) if i == NCH - 1 else None)
                P.E("act", "activation",
                    vms[:, mb * 2048 + dvs * 512: mb * 2048 + dvs * 512 + 512],
                    pp[:], AF.Copy, wait=[(s_pe, t_pe)], inc=(s_av, 1))
        if stop_after not in ('p0ld', 'p0k', 'p0w'):
            t_vst = P.E("sp", "dma_start",
                        out=agv_in.ap().rearrange("(c p) m -> p c m", p=128),
                        in_=vms[:], wait=[(s_av, s_av.n)], inc=(s_ld, 16))
        P.W("pool", s_cl)
        P.W("pool", s_ld)
        if stop_after not in ('p0nc', 'p0ld', 'p0k', 'p0w'):
            P.E("pool", "collective_compute", "AllGather", ALU.bypass,
                replica_groups=[list(range(NCORES))],
                ins=[agk_in.ap().opt()], outs=[agk_out.ap().opt()], inc=(s_cc, 1))
            P.E("pool", "collective_compute", "AllGather", ALU.bypass,
                replica_groups=[list(range(NCORES))],
                ins=[agv_in.ap().opt()], outs=[agv_out.ap().opt()], inc=(s_cc, 1))
        P.barrier()
    if stop_after in ('p0nc', 'p0ld', 'p0k', 'p0w'):
        P.W("pool", s_cl)
        P.E("pool", "nop", inc=(sem_done, 1))
        return
    if stop_after == "p0":
        P.W("dve", s_cc)
        P.E("dve", "nop", inc=(sem_done, 1))
        return

    ph_ctx = ExitStack()
    ctx_sb = ph_ctx.enter_context(nc.sbuf_tensor("ctx_sb", [128, 16 * 1024], BF16))
    with ExitStack() as ph:
        st.masks_d = nc.dram_tensor("masks", [8, 128, 384], F32,
                                    kind="ExternalInput").ap()
        st.masks = ph.enter_context(nc.sbuf_tensor("masks_sb", [128, 8 * 384], F32))
        for qt in range(8):
            P.E("sp", "dma_start", out=st.masks[:, qt * 384:(qt + 1) * 384],
                in_=st.masks_d[qt, :, :], inc=(st.ld, 16))
        wq_sb = ph.enter_context(nc.sbuf_tensor("wq_sb", [128, 16 * 128], BF16))
        wk_sb = ph.enter_context(nc.sbuf_tensor("wk_sb", [128, 16 * 128], BF16))
        wv_sb = ph.enter_context(nc.sbuf_tensor("wv_sb", [128, 16 * 128], BF16))
        q_sb = ph.enter_context(nc.sbuf_tensor("q_sb", [128, 1024], BF16))
        k_sb = ph.enter_context(nc.sbuf_tensor("k_sb", [128, TEXT], BF16))
        v_sb = ph.enter_context(nc.sbuf_tensor("v_sb", [128, TEXT], BF16))
        vt_sb = ph.enter_context(nc.sbuf_tensor("vt_sb", [128, 10 * 128], BF16))
        sc_sb = ph.enter_context(nc.sbuf_tensor("sc_sb", [128, 2 * 384], F32))
        sm_sb = ph.enter_context(nc.sbuf_tensor("sm_sb", [128, 8], F32))
        w_sb = ph.enter_context(nc.sbuf_tensor("w_sb", [128, 2 * 384], BF16))
        wt_sb = ph.enter_context(nc.sbuf_tensor("wt_sb", [128, 2 * 384], BF16))
        for h in range(H):
            hs = slice(h * 128, (h + 1) * 128)
            t_w = None
            for nm, sb in (("wq_l0", wq_sb), ("wk_l0", wk_sb), ("wv_l0", wv_sb)):
                t_w = P.E("pool", "dma_start", out=sb[:],
                          in_=wd[nm][h, :, :],
                          wait=[(s_pe, s_pe.n)] if h else None, inc=(s_cl, 16))
            for (wsb, dst, rng_, scl) in ((wq_sb, q_sb, (256, 1024), INV_SQRT_HD),
                                          (wk_sb, k_sb, (0, 1280), None),
                                          (wv_sb, v_sb, (0, 1280), None)):
                f0, ntok = rng_
                for u in range((ntok + 511) // 512):
                    wid = min(512, ntok - u * 512)
                    pp = st.ps[u % 2]
                    t_pe = None
                    for i in range(NCH):
                        t_pe = P.E("pe", "matmul", pp[:, 0:wid],
                                   wsb[:, i * 128:(i + 1) * 128],
                                   ych(st, i, f0 + u * 512, wid),
                                   start=(i == 0), stop=(i == NCH - 1),
                                   wait=[(s_cl, t_w), (sem_y, sem_y.n),
                                         (s_av, s_av.n)] if i == 0 else None,
                                   inc=(s_pe, 1) if i == NCH - 1 else None)
                    if scl is None:
                        P.E("act", "activation", dst[:, u * 512:u * 512 + wid],
                            pp[:, 0:wid], AF.Copy, wait=[(s_pe, t_pe)], inc=(s_av, 1))
                    else:
                        P.E("act", "activation", dst[:, u * 512:u * 512 + wid],
                            pp[:, 0:wid], AF.Copy, scale=scl,
                            wait=[(s_pe, t_pe)], inc=(s_av, 1))
            if stop_after == "p1q" and h == 0:
                cpv = P.sem("a0_dbgq")
                P.W("dve", s_av)
                P.E("dve", "tensor_copy", xch(st, 0, 0, 1024), q_sb[:, 0:1024],
                    inc=(cpv, 1))
                P.E("dve", "tensor_copy", xch(st, 1, 0, 1024), k_sb[:, 0:1024],
                    inc=(cpv, 1))
                P.E("dve", "tensor_copy", xch(st, 2, 0, 1024), k_sb[:, 256:1280],
                    inc=(cpv, 1))
                P.E("dve", "tensor_copy", xch(st, 3, 0, 1024), v_sb[:, 256:1280],
                    inc=(cpv, 1))
                P.W("dve", cpv)
                P.E("dve", "nop", inc=(sem_done, 1))
                break
            for ts_ in range(10):
                pb = st.psT[ts_ % 2]
                col = ((ts_ // 2) % 4) * 128
                w = [(s_av, s_av.n), (s_dv, s_dv.n)] if ts_ == 0 else (
                    [(s_dv, s_dv.n - 1)] if ts_ >= 2 else None)
                t_pe = P.E("pe", "transpose", pb[:, col:col + 128],
                           v_sb[:, ts_ * 128:(ts_ + 1) * 128], st.ident_b[:],
                           wait=w, inc=(s_pe, 1))
                P.E("dve", "tensor_copy", vt_sb[:, ts_ * 128:(ts_ + 1) * 128],
                    pb[:, col:col + 128],
                    wait=[(s_pe, t_pe)], inc=(s_dv, 1))
            if stop_after == "p1s" and h == 1:
                cpv = P.sem("a0_dbgs")
                P.W("dve", s_av)
                P.E("dve", "tensor_copy", xch(st, 0, 0, 1024),
                    ctx_sb[:, 0:1024], inc=(cpv, 1))
                P.E("dve", "tensor_copy", xch(st, 1, 0, 768),
                    st.x[:, 1 * TPC: 1 * TPC + 768] if False else sc_sb[:, 0:768],
                    inc=(cpv, 1))
                P.E("dve", "tensor_copy", xch(st, 2, 0, 768), w_sb[:, 0:768],
                    inc=(cpv, 1))
                P.E("dve", "tensor_copy", xch(st, 3, 0, 768), wt_sb[:, 0:768],
                    inc=(cpv, 1))
                P.W("dve", cpv)
                P.E("dve", "nop", inc=(sem_done, 1))
                break
            wt_tok = {}
            for qt in range(8):
                sp_ = qt % 2
                pp = st.ps[2 + sp_]
                t_pe = P.E("pe", "matmul", pp[:, 0:384],
                           q_sb[:, qt * 128:(qt + 1) * 128],
                           k_sb[:, qt * 128:qt * 128 + 384],
                           start=True, stop=True,
                           wait=[(s_av, s_av.n), (s_dv, s_dv.n)] if qt == 0 else
                           [(s_dv, s_dv.n - 4)],
                           inc=(s_pe, 1))
                P.E("dve", "tensor_add", sc_sb[:, sp_ * 384:sp_ * 384 + 384],
                    pp[:, 0:384], st.masks[:, qt * 384:(qt + 1) * 384],
                    wait=[(s_pe, t_pe), (st.ld, st.ld.n)])
                t_d = P.E("dve", "reduce_max", sm_sb[:, sp_:sp_ + 1],
                          sc_sb[:, sp_ * 384:sp_ * 384 + 384], axis=AX,
                          inc=(s_dv, 1))
                t_d = P.E("dve", "tensor_scalar_mul", sm_sb[:, 4 + sp_:5 + sp_],
                          sm_sb[:, sp_:sp_ + 1], -1.0, wait=[(s_dv, t_d)],
                          inc=(s_dv, 1))
                t_a = P.E("act", "activation", sc_sb[:, sp_ * 384:sp_ * 384 + 384],
                          sc_sb[:, sp_ * 384:sp_ * 384 + 384], AF.Exp,
                          bias=sm_sb[:, 4 + sp_:5 + sp_],
                          wait=[(s_dv, t_d)], inc=(s_av, 1))
                t_d = P.E("dve", "reduce_sum", sm_sb[:, 2 + sp_:3 + sp_],
                          sc_sb[:, sp_ * 384:sp_ * 384 + 384], axis=AX,
                          wait=[(s_av, t_a)], inc=(s_dv, 1))
                t_d = P.E("dve", "reciprocal", sm_sb[:, 6 + sp_:7 + sp_],
                          sm_sb[:, 2 + sp_:3 + sp_], wait=[(s_dv, t_d)],
                          inc=(s_dv, 1))
                ww = [(s_dv, t_d)]
                if sp_ in wt_tok:
                    ww.append((s_pe, wt_tok[sp_]))
                t_d = P.E("dve", "tensor_scalar_mul",
                          w_sb[:, sp_ * 384:sp_ * 384 + 384],
                          sc_sb[:, sp_ * 384:sp_ * 384 + 384],
                          sm_sb[:, 6 + sp_:7 + sp_],
                          wait=ww, inc=(s_dv, 1))
                for kt in range(3):
                    pb = st.psT[kt % 2]
                    col = 512 + (kt // 2) * 128
                    t_pe = P.E("pe", "transpose", pb[:, col:col + 128],
                               w_sb[:, sp_ * 384 + kt * 128: sp_ * 384 + kt * 128 + 128],
                               st.ident_b[:],
                               wait=[(s_dv, t_d)] if kt == 0 else
                               ([(s_dv, s_dv.n - 1)] if kt >= 2 else None),
                               inc=(s_pe, 1))
                    P.E("dve", "tensor_copy",
                        wt_sb[:, sp_ * 384 + kt * 128: sp_ * 384 + kt * 128 + 128],
                        pb[:, col:col + 128],
                        wait=[(s_pe, t_pe)], inc=(s_dv, 1))
                    wt_tok[sp_] = t_pe
                t_pe = None
                for kt in range(3):
                    t_pe = P.E("pe", "matmul", st.psS[:, 0:128],
                               vt_sb[:, (qt + kt) * 128:(qt + kt) * 128 + 128],
                               wt_sb[:, sp_ * 384 + kt * 128: sp_ * 384 + kt * 128 + 128],
                               start=(kt == 0), stop=(kt == 2),
                               wait=[(s_dv, s_dv.n), (s_av, max(0, s_av.n - 0))] if kt == 0 else None,
                               inc=(s_pe, 1) if kt == 2 else None)
                P.E("act", "activation", ctx_sb[:, h * 1024 + qt * 128: h * 1024 + qt * 128 + 128],
                    st.psS[:, 0:128], AF.Copy, wait=[(s_pe, t_pe)], inc=(s_av, 1))
        P.barrier()
    if stop_after == "p1s":
        ph_ctx.close()
        return
    if stop_after == "p1q":
        ph_ctx.close()
        return
    if stop_after == "p1":
        P.W("dve", s_av)
        cpv = P.sem("a0_dbgcp")
        for i in range(NCH):
            P.E("dve", "tensor_copy", xch(st, i, 0, 1024),
                ctx_sb[:, i * 1024:(i + 1) * 1024], inc=(cpv, 1))
        P.W("dve", cpv)
        P.E("dve", "nop", inc=(sem_done, 1))
        ph_ctx.close()
        return

    ph_ol = ExitStack()
    ol_sb = ph_ol.enter_context(nc.sbuf_tensor("ol_sb", [128, 16 * 1024], BF16))
    with ExitStack() as ph:
        wo_sb = [ph.enter_context(nc.sbuf_tensor(f"wo_sb{b}", [128, 16 * 128], BF16))
                 for b in range(2)]
        for do in range(16):
            b = do % 2
            t_w = P.E("pool", "dma_start", out=wo_sb[b][:],
                      in_=wd["wo_l0"][do, :, :],
                      wait=[(s_pe, max(0, s_pe.n - 1))] if do >= 2 else None,
                      inc=(s_cl, 16))
            for tt2 in range(2):
                pp = st.ps[(do * 2 + tt2) % 4]
                t_pe = None
                for i in range(NCH):
                    t_pe = P.E("pe", "matmul", pp[:],
                               wo_sb[b][:, i * 128:(i + 1) * 128],
                               ctx_sb[:, i * 1024 + tt2 * 512: i * 1024 + tt2 * 512 + 512],
                               start=(i == 0), stop=(i == NCH - 1),
                               wait=[(s_cl, t_w), (s_dv, max(0, s_dv.n - 2)),
                                     (s_av, max(0, s_av.n - 2))] if i == 0 else None,
                               inc=(s_pe, 1) if i == NCH - 1 else None)
                t_x = P.E("dve", "tensor_add", xch(st, do, tt2 * 512, 512),
                          xch(st, do, tt2 * 512, 512), pp[:],
                          wait=[(s_pe, t_pe)], inc=(s_dv, 1))
                P.E("act", "activation",
                    ol_sb[:, do * 1024 + tt2 * 512: do * 1024 + tt2 * 512 + 512],
                    pp[:], AF.Copy, wait=[(s_pe, t_pe), (s_dv, t_x)], inc=(s_av, 1))
        P.barrier()
    if stop_after == "p2":
        P.W("dve", s_dv)
        P.E("dve", "nop", inc=(sem_done, 1))
        ph_ol.close()
        ph_ctx.close()
        return

    s_cp = P.sem("a0_cp")
    t_gwf = P.E("sp", "dma_start", out=gwf_sb[:], in_=wd["gw_pc"][:, :],
                inc=(s_ld, 16))
    P.E("pool", "tensor_copy", gw_sb[:], gwf_sb[:],
        wait=[(s_ld, t_gwf)], inc=(s_cp, 1))
    P.E("sp", "dma_start", out=gb_sb[:],
        in_=bass.AP(wd["gb0"].tensor, 0, [[0, 128], [1, 1]]), inc=(s_ld, 16))
    t_gw = s_cp.n
    t_gb = s_ld.n
    for tt in range(8):
        t_pe = None
        for i in range(NCH):
            t_pe = P.E("pe", "matmul", st.psS[0:128, 0:1],
                       ol_sb[:, i * 1024 + tt * 128: i * 1024 + tt * 128 + 128],
                       gw_sb[:, i:i + 1],
                       start=(i == 0), stop=(i == NCH - 1),
                       wait=[(s_cp, t_gw), (s_av, s_av.n)] if i == 0 and tt == 0 else
                       ([(s_av, s_av.n)] if i == 0 else None),
                       inc=(s_pe, 1) if i == NCH - 1 else None)
        P.E("act", "activation", byp_sb[:, tt:tt + 1], st.psS[0:128, 0:1],
            AF.Sigmoid, bias=gb_sb[:, 0:1],
            wait=[(s_pe, t_pe), (st.ld, st.ld.n), (s_ld, t_gb)], inc=(s_av, 1))
    P.barrier()
    ph_ol.close()
    ph_ctx.close()
    if stop_after == "byp":
        P.W("dve", s_av)
        P.E("dve", "nop", inc=(sem_done, 1))
        return

    with ExitStack() as ph:
        km_sb = [ph.enter_context(nc.sbuf_tensor(f"km_sb{b}", [128, 16 * 512], BF16))
                 for b in range(2)]
        vm_sb = [ph.enter_context(nc.sbuf_tensor(f"vm_sb{b}", [128, 512], BF16))
                 for b in range(2)]
        wo2_sb = [ph.enter_context(nc.sbuf_tensor(f"wo2_sb{b}", [128, 16 * 128], BF16))
                  for b in range(2)]
        sim_sb = ph.enter_context(nc.sbuf_tensor("sim_sb", [128, 4096], F32))
        wn_sb = ph.enter_context(nc.sbuf_tensor("wn_sb", [128, 4096], BF16))
        wnt_sb = ph.enter_context(nc.sbuf_tensor("wnt_sb", [128, 32 * 256], BF16))
        t8_sb = ph.enter_context(nc.sbuf_tensor("t8_sb", [128, 16], F32))
        opre_sb = ph.enter_context(nc.sbuf_tensor("opre_sb", [128, 16 * 256], BF16))
        t_k_ld = {}
        for grp in range(4):
            for lt in range(2):
                tt = grp * 2 + lt
                for ms in range(8):
                    b = ms % 2
                    t_k = P.E("sp", "dma_start", out=km_sb[b][:],
                              in_=agk_out.ap()[ms * 2048:(ms + 1) * 2048, :]
                              .rearrange("(c p) m -> p c m", p=128),
                              wait=[(s_cc, 1), (s_pe, max(0, s_pe.n - 1))],
                              inc=(s_ld, 16))
                    t_k_ld[ms] = t_k
                    pp = st.ps[ms % 2]
                    t_pe = None
                    for i in range(NCH):
                        t_pe = P.E("pe", "matmul", pp[:],
                                   yown(st, i, tt * 128, 128),
                                   km_sb[b][:, i * 512:(i + 1) * 512],
                                   start=(i == 0), stop=(i == NCH - 1),
                                   wait=[(s_ld, t_k_ld[ms]),
                                         (s_av, max(0, s_av.n - 1))] if i == 0 else None,
                                   inc=(s_pe, 1) if i == NCH - 1 else None)
                    P.E("act", "activation", sim_sb[:, ms * 512:(ms + 1) * 512],
                        pp[:], AF.Copy, wait=[(s_pe, t_pe)], inc=(s_av, 1))
                t_sim = s_av.n
                t_d = P.E("dve", "max", t8_sb[:, 0:8], sim_sb[:],
                          wait=[(s_av, t_sim)], inc=(s_dv, 1))
                t_d = P.E("dve", "tensor_scalar_mul", t8_sb[:, 8:9], t8_sb[:, 0:1],
                          -1.0, wait=[(s_dv, t_d)], inc=(s_dv, 1))
                t_d = P.E("dve", "tensor_add", t8_sb[:, 9:10], t8_sb[:, 7:8],
                          t8_sb[:, 8:9], wait=[(s_dv, t_d)], inc=(s_dv, 1))
                t_a = P.E("act", "activation", t8_sb[:, 10:11], t8_sb[:, 9:10],
                          AF.Exp, wait=[(s_dv, t_d)], inc=(s_av, 1))
                t_a2 = P.E("act", "activation", sim_sb[:], sim_sb[:], AF.Exp,
                           bias=t8_sb[:, 8:9], wait=[(s_dv, t_d)], inc=(s_av, 1))
                t_d = P.E("dve", "scalar_tensor_tensor",
                          wn_sb[:, 0:4096], sim_sb[:],
                          t8_sb[:, 10:11], sim_sb[:], ALU.is_ge, ALU.mult,
                          wait=[(s_av, s_av.n)], inc=(s_dv, 1))
                t_d = P.E("dve", "reduce_sum", t8_sb[:, 11:12],
                          wn_sb[:, 0:4096], axis=AX, inc=(s_dv, 1))
                t_d = P.E("dve", "reciprocal", t8_sb[:, 12:13], t8_sb[:, 11:12],
                          wait=[(s_dv, t_d)], inc=(s_dv, 1))
                t_d = P.E("dve", "tensor_scalar",
                          wn_sb[:, 0:4096],
                          wn_sb[:, 0:4096],
                          t8_sb[:, 12:13], byp_sb[:, tt:tt + 1],
                          ALU.mult, ALU.mult, wait=[(s_dv, t_d)], inc=(s_dv, 1))
                for ms2 in range(32):
                    pb = st.psT[ms2 % 2]
                    col = ((ms2 // 2) % 4) * 128
                    t_pe = P.E("pe", "transpose", pb[:, col:col + 128],
                               wn_sb[:, ms2 * 128: ms2 * 128 + 128],
                               st.ident_b[:],
                               wait=[(s_dv, t_d)] if ms2 == 0 else
                               ([(s_dv, s_dv.n - 1)] if ms2 >= 2 else None),
                               inc=(s_pe, 1))
                    P.E("dve", "tensor_copy",
                        wnt_sb[:, ms2 * 256 + lt * 128: ms2 * 256 + lt * 128 + 128],
                        pb[:, col:col + 128],
                        wait=[(s_pe, t_pe)], inc=(s_dv, 1))
            t_wnt = s_dv.n
            for pass_ in range(4):
                t_pe_last = {}
                mc_tok = {}
                for mc in range(32):
                    vw = [(s_cc, 2)]
                    if mc - 2 in mc_tok:
                        vw.append((s_pe, mc_tok[mc - 2]))
                    elif pass_ > 0:
                        vw.append((s_pe, s_pe.n))
                    t_v = P.E("sp", "dma_start", out=vm_sb[mc % 2][:],
                              in_=agv_out.ap()[mc * 128:(mc + 1) * 128,
                                               pass_ * 512:(pass_ + 1) * 512],
                              wait=vw, inc=(s_ld, 16))
                    for ldv in range(4):
                        dv_ = pass_ * 4 + ldv
                        pp = st.ps[ldv]
                        w = None
                        if mc == 0:
                            w = [(s_dv, t_wnt), (s_ld, t_v),
                                 (s_av, s_av.n)]
                        elif ldv == 0:
                            w = [(s_ld, t_v)]
                        t_pe_last[ldv] = P.E("pe", "matmul", pp[:, 0:256],
                                             vm_sb[mc % 2][:, ldv * 128:(ldv + 1) * 128],
                                             wnt_sb[:, mc * 256:(mc + 1) * 256],
                                             start=(mc == 0), stop=(mc == 31), wait=w,
                                             inc=(s_pe, 1))
                        mc_tok[mc] = t_pe_last[ldv]
                for ldv in range(4):
                    dv_ = pass_ * 4 + ldv
                    P.E("act", "activation", opre_sb[:, dv_ * 256:(dv_ + 1) * 256],
                        st.ps[ldv][:, 0:256], AF.Copy,
                        wait=[(s_pe, t_pe_last[ldv])], inc=(s_av, 1))
            t_opre = s_av.n
            for do in range(16):
                b2 = do % 2
                t_w = P.E("pool", "dma_start", out=wo2_sb[b2][:],
                          in_=wd["wo_l0"][do, :, :],
                          wait=[(s_pe, max(0, s_pe.n - 1))] if (grp or do >= 2) else None,
                          inc=(s_cl, 16))
                pp = st.ps[4] if do % 2 == 0 else st.psS
                t_pe = None
                for i in range(NCH):
                    t_pe = P.E("pe", "matmul", pp[:, 0:256],
                               wo2_sb[b2][:, i * 128:(i + 1) * 128],
                               opre_sb[:, i * 256:(i + 1) * 256],
                               start=(i == 0), stop=(i == NCH - 1),
                               wait=[(s_av, t_opre), (s_cl, t_w),
                                     (s_dv, max(0, s_dv.n - 1))] if i == 0 else None,
                               inc=(s_pe, 1) if i == NCH - 1 else None)
                P.E("dve", "tensor_add", xch(st, do, grp * 256, 256),
                    xch(st, do, grp * 256, 256), pp[:, 0:256],
                    wait=[(s_pe, t_pe)], inc=(s_dv, 1))
        P.W("dve", s_dv)
        P.E("dve", "nop", inc=(sem_done, 1))



def emit_ffn1(P, st, wd, sem_y, sem_done):
    nc = P.nc
    ec = P.stack.enter_context
    s_ld = P.sem("f1_ld")
    s_cl = P.sem("f1_cl")
    s_cp = P.sem("f1_cp")
    s_pe = P.sem("f1_pe")
    s_av = P.sem("f1_av")
    s_dv = P.sem("f1_dv")
    with ExitStack() as ph:
        rw_sb = ph.enter_context(nc.sbuf_tensor("rw_sb", [128, 16 * 32], BF16))
        at_sb = ph.enter_context(nc.sbuf_tensor("at_sb", [128, 16 * 64], BF16))
        bt_sb = ph.enter_context(nc.sbuf_tensor("bt_sb", [64, 2048], BF16))
        r64_sb = ph.enter_context(nc.sbuf_tensor("r64_sb", [32, 64], F32))
        bia_sb = ph.enter_context(nc.sbuf_tensor("bia_sb", [64, 1], F32))
        sg_sb = ph.enter_context(nc.sbuf_tensor("sg_sb", [32, 1024], F32))
        gl_sb = ph.enter_context(nc.sbuf_tensor("gl_sb", [64, 1024], F32))
        gt_sb = ph.enter_context(nc.sbuf_tensor("gt_sb", [128, 8 * 64], F32))
        tk_sb = ph.enter_context(nc.sbuf_tensor("tk_sb", [128, 12], F32))
        gat_sb = ph.enter_context(nc.sbuf_tensor("gat_sb", [64, 1024], BF16))
        mid_sb = ph.enter_context(nc.sbuf_tensor("mid_sb", [64, 1024], BF16))
        mod_sb = ph.enter_context(nc.sbuf_tensor("mod_sb", [64, 1024], BF16))
        bw_sb = [ph.enter_context(nc.sbuf_tensor(f"bw_sb{b}", [128, 16 * 128], BF16))
                 for b in range(2)]
        rwf_sb = ph.enter_context(nc.sbuf_tensor("rwf_sb", [128, 512], F32))
        atf_sb = ph.enter_context(nc.sbuf_tensor("atf_sb", [128, 1024], F32))
        t_f = P.E("sp", "dma_start", out=rwf_sb[:], in_=wd["rw_pc"][:, :],
                  inc=(s_ld, 16))
        P.E("pool", "tensor_copy", rw_sb[:], rwf_sb[:], wait=[(s_ld, t_f)],
            inc=(s_cp, 1))
        t_f = P.E("sp", "dma_start", out=atf_sb[:], in_=wd["at_pc"][:, :],
                  inc=(s_ld, 16))
        P.E("pool", "tensor_copy", at_sb[:], atf_sb[:], wait=[(s_ld, t_f)],
            inc=(s_cp, 1))
        P.E("pool", "dma_start", out=bt_sb[:], in_=wd["BT1"][:, :], inc=(s_cl, 16))
        P.E("sp", "dma_start", out=r64_sb[:], in_=wd["r64"][:], inc=(s_ld, 16))
        P.E("sp", "dma_start", out=bia_sb[:], in_=wd["bias1"][:], inc=(s_ld, 16))
        t_cl = s_cl.n
        t_cp = s_cp.n
        t_ld = s_ld.n
        for u in range(2):
            pp = st.ps[u]
            t_pe = None
            for i in range(NCH):
                t_pe = P.E("pe", "matmul", pp[0:32, 0:512],
                           rw_sb[:, i * 32:(i + 1) * 32],
                           yown(st, i, u * 512, 512),
                           start=(i == 0), stop=(i == NCH - 1),
                           wait=[(s_cl, t_cl), (s_cp, t_cp),
                                 (sem_y, sem_y.n)] if i == 0 else None,
                           inc=(s_pe, 1) if i == NCH - 1 else None)
            P.E("act", "activation", sg_sb[:, u * 512:(u + 1) * 512], pp[0:32, 0:512],
                AF.Sigmoid, wait=[(s_pe, t_pe)], inc=(s_av, 1))
        for u in range(2):
            pp = st.ps[2 + u]
            t_pe = P.E("pe", "matmul", pp[0:64, 0:512], r64_sb[:],
                       sg_sb[:, u * 512:(u + 1) * 512], start=True, stop=True,
                       wait=[(s_av, s_av.n), (st.ld, st.ld.n), (s_ld, t_ld)],
                       inc=(s_pe, 1))
            P.E("dve", "tensor_scalar", gl_sb[:, u * 512:(u + 1) * 512],
                pp[0:64, 0:512], bia_sb[:, 0:1], None, ALU.add,
                wait=[(s_pe, t_pe), (s_ld, t_ld)], inc=(s_dv, 1))
        for tt in range(8):
            pb = st.psS if tt % 2 == 0 else st.ps[4]
            col = ((tt // 2) % 2) * 64
            t_pe = P.E("pe", "transpose", pb[:, col:col + 64],
                       gl_sb[:, tt * 128:(tt + 1) * 128], st.ident_f[0:64, 0:64],
                       wait=[(s_av, s_av.n), (s_dv, s_dv.n)] if tt == 0 else
                       ([(s_dv, s_dv.n - 1)] if tt >= 2 else None), inc=(s_pe, 1))
            P.E("dve", "tensor_copy", gt_sb[:, tt * 64:(tt + 1) * 64],
                pb[:, col:col + 64],
                wait=[(s_pe, t_pe)], inc=(s_dv, 1))
        for tt in range(8):
            g_ap = gt_sb[:, tt * 64:(tt + 1) * 64]
            t_d = P.E("dve", "max", tk_sb[:, 0:8], g_ap, wait=[(s_dv, s_dv.n)],
                      inc=(s_dv, 1))
            t_d = P.E("dve", "scalar_tensor_tensor", g_ap, g_ap, tk_sb[:, 7:8],
                      g_ap, ALU.is_ge, ALU.mult, wait=[(s_dv, t_d)], inc=(s_dv, 1))
            t_d = P.E("dve", "reduce_sum", tk_sb[:, 8:9], g_ap, axis=AX,
                      wait=[(s_dv, t_d)], inc=(s_dv, 1))
            t_d = P.E("dve", "reciprocal", tk_sb[:, 9:10], tk_sb[:, 8:9],
                      wait=[(s_dv, t_d)], inc=(s_dv, 1))
            t_d = P.E("dve", "tensor_scalar", g_ap, g_ap, tk_sb[:, 9:10], 16.0,
                      ALU.mult, ALU.mult, wait=[(s_dv, t_d)], inc=(s_dv, 1))
            pb2 = st.psS if tt % 2 == 0 else st.ps[4]
            t_pe = P.E("pe", "transpose", pb2[0:64, 256:384], g_ap, st.ident_f[:],
                       wait=[(s_dv, t_d), (s_av, max(0, s_av.n - 1))], inc=(s_pe, 1))
            P.E("act", "activation", gat_sb[:, tt * 128:(tt + 1) * 128],
                pb2[0:64, 256:384], AF.Copy,
                wait=[(s_pe, t_pe)], inc=(s_av, 1))
        for u in range(2):
            pp = st.ps[u]
            t_pe = None
            for i in range(NCH):
                t_pe = P.E("pe", "matmul", pp[0:64, 0:512],
                           at_sb[:, i * 64:(i + 1) * 64],
                           yown(st, i, u * 512, 512),
                           start=(i == 0), stop=(i == NCH - 1),
                           wait=[(s_av, s_av.n), (s_cp, t_cp)] if i == 0 else None,
                           inc=(s_pe, 1) if i == NCH - 1 else None)
            P.E("act", "activation", mid_sb[:, u * 512:(u + 1) * 512], pp[0:64, 0:512],
                AF.Copy, wait=[(s_pe, t_pe)], inc=(s_av, 1))
        t_d = P.E("dve", "tensor_mul", mod_sb[:], mid_sb[:], gat_sb[:],
                  wait=[(s_av, s_av.n)], inc=(s_dv, 1))
        for nchk in range(NCH):
            b = nchk % 2
            t_w = P.E("pool", "dma_start", out=bw_sb[b][:],
                      in_=wd["bW_l"][nchk, :, :],
                      wait=[(s_pe, max(0, s_pe.n - 1))] if nchk >= 2 else None,
                      inc=(s_cl, 16))
            for tt2 in range(2):
                pp = st.ps[2 + (nchk * 2 + tt2) % 2]
                for i in range(NCH):
                    P.E("pe", "matmul", pp[:],
                        bw_sb[b][:, i * 128:(i + 1) * 128],
                        yown(st, i, tt2 * 512, 512),
                        start=(i == 0), stop=False,
                        wait=[(s_cl, t_w), (s_dv, t_d),
                              (s_dv, max(0, s_dv.n - 1))] if i == 0 else None)
                t_pe = P.E("pe", "matmul", pp[:],
                           bt_sb[:, nchk * 128:(nchk + 1) * 128],
                           mod_sb[:, tt2 * 512:(tt2 + 1) * 512],
                           start=False, stop=True, inc=(s_pe, 1))
                P.E("dve", "scalar_tensor_tensor", xch(st, nchk, tt2 * 512, 512),
                    pp[:], 0.5, xch(st, nchk, tt2 * 512, 512), ALU.mult, ALU.add,
                    wait=[(s_pe, t_pe)], inc=(s_dv, 1))
        P.W("dve", s_dv)
        P.E("dve", "nop", inc=(sem_done, 1))



def emit_attn1(P, st, wd, sem_y, sem_done):
    nc = P.nc
    ec = P.stack.enter_context
    s_ld = P.sem("a1_ld")
    s_cl = P.sem("a1_cl")
    s_cp = P.sem("a1_cp")
    s_pe = P.sem("a1_pe")
    s_av = P.sem("a1_av")
    s_dv = P.sem("a1_dv")
    s_cc = P.sem("a1_cc")
    ar_in = nc.dram_tensor("kv_in", [64, 16 * 129], F32)
    ar_out = nc.dram_tensor("kv_out", [64, 16 * 129], F32)

    qm_sb = ec(nc.sbuf_tensor("qm_sb", [64, 16 * 1024], BF16))
    ctx1_sb = ec(nc.sbuf_tensor("ctx1_sb", [128, 16 * 1024], BF16))
    stg_sb = ec(nc.sbuf_tensor("stg_sb", [64, 16 * 129], F32))
    kvb_sb = ec(nc.sbuf_tensor("kvb_sb", [64, 16 * 130], BF16))
    rn_sb = ec(nc.sbuf_tensor("rn_sb", [1, 2048], F32))
    rnb_sb = ec(nc.sbuf_tensor("rnb_sb", [128, 1024], F32))
    with ExitStack() as ph:
        wk_sb = ph.enter_context(nc.sbuf_tensor("wk1_sb", [128, 16 * 128], BF16))
        wmap_sb = ph.enter_context(nc.sbuf_tensor("wmap_sb", [128, 2 * 64], BF16))
        bia_sb = ph.enter_context(nc.sbuf_tensor("bia1_sb", [128, 16 * 3], F32))
        kf_sb = ph.enter_context(nc.sbuf_tensor("kf_sb", [128, 1024], BF16))
        vt1_sb = ph.enter_context(nc.sbuf_tensor("vt1_sb", [128, 8 * 128], BF16))
        km_sb = ph.enter_context(nc.sbuf_tensor("km1_sb", [128, 8 * 64], BF16))
        el_sb = ph.enter_context(nc.sbuf_tensor("el_sb", [128, 6 * 512], F32))
        onesb = ph.enter_context(nc.sbuf_tensor("onesb", [128, 1], BF16))

        wmf_sb = ph.enter_context(nc.sbuf_tensor("wmf_sb", [128, 128], F32))
        t_f = P.E("sp", "dma_start", out=wmf_sb[:, 0:64], in_=wd["wmqT"][:, :],
                  inc=(s_ld, 16))
        t_f = P.E("sp", "dma_start", out=wmf_sb[:, 64:128], in_=wd["wmkT"][:, :],
                  inc=(s_ld, 16))
        P.E("pool", "tensor_copy", wmap_sb[:], wmf_sb[:], wait=[(s_ld, t_f)],
            inc=(s_cp, 1))
        P.E("sp", "dma_start", out=bia_sb[:, 0:48], in_=wd["bqkv_pc"][:, :],
            inc=(s_ld, 16))
        P.E("act", "activation", onesb[:], cb(st, C_ONE), AF.Copy,
            wait=[(st.ld, st.ld.n)], inc=(s_av, 1))

        def proj_head(h, wname, bcol, dst, dst_off, scale_sb=None):
            t_w = P.E("pool", "dma_start", out=wk_sb[:],
                      in_=wd[wname][h, :, :],
                      wait=[(s_pe, s_pe.n)], inc=(s_cl, 16))
            for u in range(2):
                pp = st.ps[u]
                t_pe = None
                for i in range(NCH):
                    t_pe = P.E("pe", "matmul", pp[:],
                               wk_sb[:, i * 128:(i + 1) * 128],
                               yown(st, i, u * 512, 512),
                               start=(i == 0), stop=(i == NCH - 1),
                               wait=[(s_cl, t_w), (sem_y, sem_y.n),
                                     (s_av, max(0, s_av.n - 1)),
                                     (s_dv, max(0, s_dv.n - 1))] if i == 0 else None,
                               inc=(s_pe, 1) if i == NCH - 1 else None)
                P.E("dve", "tensor_scalar",
                    dst[:, dst_off + u * 512: dst_off + u * 512 + 512],
                    pp[:], bia_sb[:, h * 3 + bcol: h * 3 + bcol + 1], None,
                    ALU.add,
                    wait=[(s_pe, t_pe), (s_ld, s_ld.n)], inc=(s_dv, 1))

        def elu1(src_ap, dst_ap, tmpo, first_wait=None, w=512, p=128):
            a1 = el_sb[0:p, tmpo:tmpo + w]
            a2 = el_sb[0:p, tmpo + 512:tmpo + 512 + w]
            P.E("act", "activation", a1, src_ap, AF.Relu, scale=-1.0, wait=first_wait)
            P.E("act", "activation", a1, a1, AF.Exp, scale=-1.0)
            t_a = P.E("act", "activation", a2, src_ap, AF.Relu, inc=(s_av, 1))
            return P.E("dve", "tensor_add", dst_ap, a1, a2,
                       wait=[(s_av, t_a)], inc=(s_dv, 1))

        for h in range(H):
            proj_head(h, "wk_l1", 1, kf_sb, 0)
            t_kf = s_dv.n
            for ts_ in range(8):
                pbk = st.psS if ts_ % 2 == 0 else st.ps[4]
                t_pe = P.E("pe", "matmul", pbk[:, 0:64],
                           kf_sb[:, ts_ * 128:(ts_ + 1) * 128], wmap_sb[:, 64:128],
                           start=True, stop=True,
                           wait=[(s_dv, t_kf), (s_cp, s_cp.n)] if ts_ == 0 else
                           ([(s_dv, s_dv.n - 1)] if ts_ >= 2 else None),
                           inc=(s_pe, 1))
                elu1(pbk[:, 0:64],
                     km_sb[:, ts_ * 64:(ts_ + 1) * 64], (ts_ % 2) * 1024,
                     first_wait=[(s_pe, t_pe)] if ts_ < 2 else
                     [(s_pe, t_pe), (s_dv, s_dv.n - 1)], w=64)
            proj_head(h, "wv_l1", 2, kf_sb, 0)
            for ts_ in range(8):
                pb = st.psT[ts_ % 2]
                col = ((ts_ // 2) % 4) * 128
                t_pe = P.E("pe", "transpose", pb[:, col:col + 128],
                           kf_sb[:, ts_ * 128:(ts_ + 1) * 128], st.ident_b[:],
                           wait=[(s_dv, s_dv.n)] if ts_ == 0 else
                           ([(s_dv, s_dv.n - 1)] if ts_ >= 2 else None), inc=(s_pe, 1))
                P.E("dve", "tensor_copy", vt1_sb[:, ts_ * 128:(ts_ + 1) * 128],
                    pb[:, col:col + 128],
                    wait=[(s_pe, t_pe)], inc=(s_dv, 1))
            t_pe = None
            for ts_ in range(8):
                t_pe = P.E("pe", "matmul", st.ps[2][0:64, 0:128],
                           km_sb[:, ts_ * 64:(ts_ + 1) * 64],
                           vt1_sb[:, ts_ * 128:(ts_ + 1) * 128],
                           start=(ts_ == 0), stop=(ts_ == 7),
                           wait=[(s_dv, s_dv.n), (s_av, s_av.n)] if ts_ == 0 else None,
                           inc=(s_pe, 1) if ts_ == 7 else None)
            for ts_ in range(8):
                t_pe2 = P.E("pe", "matmul", st.ps[3][0:64, 0:1],
                            km_sb[:, ts_ * 64:(ts_ + 1) * 64], onesb[:],
                            start=(ts_ == 0), stop=(ts_ == 7),
                            inc=(s_pe, 1) if ts_ == 7 else None)
            P.E("act", "activation", stg_sb[:, h * 129: h * 129 + 128],
                st.ps[2][0:64, 0:128], AF.Copy, wait=[(s_pe, s_pe.n)], inc=(s_av, 1))
            P.E("act", "activation", stg_sb[:, h * 129 + 128: h * 129 + 129],
                st.ps[3][0:64, 0:1], AF.Copy, inc=(s_av, 1))
        P.E("pool", "dma_start", out=ar_in[:, :], in_=stg_sb[:],
            wait=[(s_av, s_av.n)], inc=(s_cl, 16))
        P.W("pool", s_cl)
        P.E("pool", "collective_compute", "AllReduce", ALU.add,
            replica_groups=[list(range(NCORES))],
            ins=[ar_in.ap().opt()], outs=[ar_out.ap().opt()], inc=(s_cc, 1))
        t_bk = P.E("pool", "dma_start", out=stg_sb[:], in_=ar_out[:, :],
                   wait=[(s_cc, 1)], inc=(s_cl, 16))
        t_kv = P.E("act", "activation", kvb_sb[:, 0:16 * 129], stg_sb[:, 0:16 * 129],
                   AF.Copy, wait=[(s_cl, t_bk)], inc=(s_av, 1))
        for h in range(H):
            proj_head(h, "wq_l1", 0, kf_sb, 0)
            t_kf = s_dv.n
            t_qm = None
            for u in range(2):
                pp = st.ps[u]
                t_pe = P.E("pe", "matmul", pp[0:64, 0:512],
                           wmap_sb[:, 0:64],
                           kf_sb[:, u * 512:(u + 1) * 512],
                           start=True, stop=True,
                           wait=[(s_dv, t_kf), (s_dv, max(0, s_dv.n - 1))],
                           inc=(s_pe, 1))
                t_qm = elu1(pp[0:64, 0:512],
                            qm_sb[:, h * 1024 + u * 512: h * 1024 + u * 512 + 512],
                            1024 + u * 1024,
                            first_wait=[(s_pe, t_pe), (s_dv, s_dv.n)], p=64)
            t_nr = None
            for u in range(2):
                t_pe = P.E("pe", "matmul", st.ps[2 + u][0:1, 0:512],
                           kvb_sb[:, h * 129 + 128: h * 129 + 129],
                           qm_sb[:, h * 1024 + u * 512: h * 1024 + u * 512 + 512],
                           start=True, stop=True,
                           wait=[(s_av, t_kv), (s_dv, t_qm),
                                 (s_dv, max(0, s_dv.n - 1))], inc=(s_pe, 1))
                t_nr = P.E("dve", "tensor_scalar", rn_sb[0:1, u * 512:(u + 1) * 512],
                           st.ps[2 + u][0:1, 0:512],
                           st.consts[0:1, C_EPS6:C_EPS6 + 1], None, ALU.add,
                           wait=[(s_pe, t_pe)], inc=(s_dv, 1))
            t_rn = P.E("dve", "reciprocal", rn_sb[0:1, 1024:2048],
                       rn_sb[0:1, 0:1024],
                       wait=[(s_dv, t_nr)], inc=(s_dv, 1))
            for u in range(2):
                t_pe = P.E("pe", "matmul", st.ps[2 + u][:, 0:512], st.ones_row[:],
                           rn_sb[0:1, 1024 + u * 512:1024 + (u + 1) * 512],
                           start=True, stop=True,
                           wait=[(s_dv, t_rn)], inc=(s_pe, 1))
                P.E("dve", "tensor_copy", rnb_sb[:, u * 512:(u + 1) * 512],
                    st.ps[2 + u][:, 0:512], wait=[(s_pe, t_pe)], inc=(s_dv, 1))
            for u in range(2):
                t_pe = P.E("pe", "matmul", st.ps[u][:, 0:512],
                           kvb_sb[:, h * 129: h * 129 + 128],
                           qm_sb[:, h * 1024 + u * 512: h * 1024 + u * 512 + 512],
                           start=True, stop=True,
                           wait=[(s_av, t_kv), (s_dv, s_dv.n)], inc=(s_pe, 1))
                P.E("dve", "tensor_mul",
                    ctx1_sb[:, h * 1024 + u * 512: h * 1024 + u * 512 + 512],
                    st.ps[u][:, 0:512], rnb_sb[:, u * 512:(u + 1) * 512],
                    wait=[(s_pe, t_pe)], inc=(s_dv, 1))
        P.barrier()

    with ExitStack() as ph:
        wo_sb = [ph.enter_context(nc.sbuf_tensor(f"wo1_sb{b}", [128, 16 * 128], BF16))
                 for b in range(2)]
        bo_sb = ph.enter_context(nc.sbuf_tensor("bo1_sb", [128, 16], F32))
        P.E("sp", "dma_start", out=bo_sb[:], in_=wd["bo_pc"][:, :], inc=(s_ld, 16))
        t_bo = s_ld.n
        for do in range(16):
            b = do % 2
            t_w = P.E("pool", "dma_start", out=wo_sb[b][:],
                      in_=wd["wo_l1"][do, :, :],
                      wait=[(s_pe, max(0, s_pe.n - 1))] if do >= 2 else None,
                      inc=(s_cl, 16))
            for tt2 in range(2):
                pp = st.ps[(do * 2 + tt2) % 4]
                t_pe = None
                for i in range(NCH):
                    t_pe = P.E("pe", "matmul", pp[:],
                               wo_sb[b][:, i * 128:(i + 1) * 128],
                               ctx1_sb[:, i * 1024 + tt2 * 512: i * 1024 + tt2 * 512 + 512],
                               start=(i == 0), stop=(i == NCH - 1),
                               wait=[(s_cl, t_w), (s_dv, max(0, s_dv.n - 3))] if i == 0 else None,
                               inc=(s_pe, 1) if i == NCH - 1 else None)
                P.E("dve", "scalar_tensor_tensor", xch(st, do, tt2 * 512, 512),
                    pp[:], bo_sb[:, do:do + 1], xch(st, do, tt2 * 512, 512),
                    ALU.add, ALU.add,
                    wait=[(s_pe, t_pe), (s_ld, t_bo)], inc=(s_dv, 1))
    P.W("dve", s_dv)
    P.E("dve", "nop", inc=(sem_done, 1))



def _din(nc, name, shape, dtype):
    return nc.dram_tensor(name, list(shape), dtype, kind="ExternalInput").ap()


def build_prog(kind):
    scratch = 8192 if kind == "ffn0" else 16384
    nc = bass.Bass("TRN2", target_bir_lowering=False, debug=False,
                   dynamic_dma_scratch_size=scratch, detect_race_conditions=False)
    stack = ExitStack()
    with stack:
        P = Prog(nc, stack)
        text = TEXT if kind == "attn0" else TPC
        x_in = _din(nc, "x_in", [TPC, 2048], F32)
        x_out = nc.dram_tensor("x_out", [TPC, 2048], F32, kind="ExternalOutput").ap()
        wd = {}
        if kind == "ffn0":
            wd["wg_pt"] = _din(nc, "wg_pt", [1024, NFF], U8)
            wd["wu_pt"] = _din(nc, "wu_pt", [1024, NFF], U8)
            wd["wd_pt"] = _din(nc, "wd_pt", [8192, 2048], U8)
            wd["sc_g"] = _din(nc, "sc_g", [64, NFF], F32)
            wd["sc_u"] = _din(nc, "sc_u", [64, NFF], F32)
            wd["sc_d"] = _din(nc, "sc_d", [512, 2048], F32)
        elif kind == "attn0":
            for nm in ("wq_l0", "wk_l0", "wv_l0", "wo_l0", "wkm_l"):
                wd[nm] = _din(nc, nm, [16, 128, 2048], F32)
            wd["wvm_l"] = _din(nc, "wvm_l", [4, 128, 8192], F32)
            wd["gw_pc"] = _din(nc, "gw_pc", [128, 16], F32)
            wd["gb0"] = _din(nc, "gb0", [1, 1], F32)
            wd["memT_sl"] = _din(nc, "memT_sl", [2048, MSL], F32)
            halo_in = _din(nc, "halo_in", [HALO, 2048], F32)
        elif kind == "ffn1":
            wd["bW_l"] = _din(nc, "bW_l", [16, 128, 2048], F32)
            wd["rw_pc"] = _din(nc, "rw_pc", [128, 512], F32)
            wd["at_pc"] = _din(nc, "at_pc", [128, 1024], F32)
            wd["BT1"] = _din(nc, "BT1", [64, 2048], F32)
            wd["bias1"] = _din(nc, "bias1", [64, 1], F32)
            wd["r64"] = _din(nc, "r64", [32, 64], F32)
        elif kind == "attn1":
            for nm in ("wq_l1", "wk_l1", "wv_l1", "wo_l1"):
                wd[nm] = _din(nc, nm, [16, 128, 2048], F32)
            wd["bqkv_pc"] = _din(nc, "bqkv_pc", [128, 48], F32)
            wd["bo_pc"] = _din(nc, "bo_pc", [128, 16], F32)
            wd["wmqT"] = _din(nc, "wmqT", [128, 64], F32)
            wd["wmkT"] = _din(nc, "wmkT", [128, 64], F32)

        is_attn = kind in ("attn0", "attn1")
        st = setup_common(P, nc, text=text, want_identb=is_attn,
                          nps=5 if is_attn else 6, npsT=2 if is_attn else 0)
        s_x = P.sem("sx")
        s_y = P.sem("sy")
        s_f = P.sem("sf")
        s_o = P.sem("so")
        if kind == "attn0":
            ph_h = ExitStack()
            halo_sb = ph_h.enter_context(
                nc.sbuf_tensor("halo_sb", [128, NCH * HALO], F32))
            emit_load_x(P, st, x_in, s_x, halo_d=halo_in, halo_sb=halo_sb)
            P.barrier()
            emit_ln(P, st, HALO, TPC, s_x, s_y, tag="m")
            P.barrier()
            emit_ln(P, st, 0, HALO, s_x, s_y,
                    x_src=lambda i, f0, w: halo_sb[:, i * HALO + f0: i * HALO + f0 + w],
                    tag="h")
            P.barrier()
            ph_h.close()
            emit_attn0(P, st, wd, s_y, s_f, stop_after=A0_STOP)
        else:
            emit_load_x(P, st, x_in, s_x)
            P.barrier()
            emit_ln(P, st, 0, TPC, s_x, s_y, tag="m")
            P.barrier()
            if kind == "ffn0":
                emit_ffn0(P, st, wd, s_y, s_f)
            elif kind == "ffn1":
                emit_ffn1(P, st, wd, s_y, s_f)
            else:
                emit_attn1(P, st, wd, s_y, s_f)
        P.barrier()
        emit_store_x(P, st, x_out, s_o)
        P.finalize()
    return nc


A0_STOP = None
_PROGS = {}


def _get_prog(kind):
    if kind not in _PROGS:
        _PROGS[kind] = build_prog(kind)
    return _PROGS[kind]


def kernel(**inputs):
    wd = host_prepare(inputs)
    cores = list(range(NCORES))
    common = {k: wd[k] for k in ("consts", "ones_row", "ident_f", "ident_b")}

    def launch(kind, x_cur, extra_per_core=None, shared=None):
        nc = _get_prog(kind)
        maps = []
        for c in cores:
            m = dict(common)
            m["x_in"] = x_cur[c]
            if shared:
                m.update(shared)
            if extra_per_core:
                m.update(extra_per_core[c])
            maps.append(m)
        res = run_bass_kernel_spmd(nc, maps, core_ids=cores)
        return [np.asarray(res.results[c]["x_out"]) for c in cores]

    sh_ffn0 = {k: wd[k] for k in ("wg_pt", "wu_pt", "wd_pt", "sc_g", "sc_u", "sc_d")}
    sh_attn0 = {k: wd[k] for k in ("wq_l0", "wk_l0", "wv_l0", "wo_l0", "wkm_l",
                                   "wvm_l", "gw_pc", "gb0")}
    sh_ffn1 = {k: wd[k] for k in ("bW_l", "rw_pc", "at_pc", "BT1", "bias1", "r64")}
    sh_attn1 = {k: wd[k] for k in ("wq_l1", "wk_l1", "wv_l1", "wo_l1",
                                   "bqkv_pc", "bo_pc", "wmqT", "wmkT")}

    x = wd["x_sl"]
    x = launch("ffn0", x, shared=sh_ffn0)
    halo = [np.zeros((HALO, 2048), np.float32) if c == 0 else
            np.ascontiguousarray(x[c - 1][-HALO:]) for c in cores]
    extra = [{"halo_in": halo[c], "memT_sl": wd["memT_sl"][c],
              "masks": wd["masks"][c]} for c in cores]
    x = launch("attn0", x, extra_per_core=extra, shared=sh_attn0)
    x = launch("ffn0", x, shared=sh_ffn0)
    x = launch("ffn1", x, shared=sh_ffn1)
    x = launch("attn1", x, shared=sh_attn1)
    x = launch("ffn1", x, shared=sh_ffn1)
    out = np.concatenate(x, axis=0).reshape(1, 8192, 2048).astype(np.float32)
    return out


# version 02
# speedup vs baseline: 1.1854x; 1.1854x over previous
import numpy as np
import ml_dtypes
import concourse.bass as bass
import concourse.mybir as mybir
from concourse.bass_utils import run_bass_kernel_spmd
from contextlib import ExitStack

F32 = mybir.dt.float32
BF16 = mybir.dt.bfloat16
U8 = mybir.dt.uint8
AF = mybir.ActivationFunctionType
ALU = mybir.AluOpType
AX = mybir.AxisListType.X

D = 2048
NCH = 16
TPC = 1024
HALO = 256
TEXT = TPC + HALO
NCORES = 8
M_MEM = 4096
MSL = M_MEM // NCORES
NFF = 16384
NSC = 32
SCW = NFF // NSC
H = 16
HD = 128
INV_SQRT_D = float(1.0 / np.sqrt(2048.0))
INV_SQRT_HD = float(1.0 / np.sqrt(128.0))

C_M4, C_M6, C_75, C_EPS5, C_EPS6, C_ONE, C_ZERO, C_SPARE = range(8)


def _pi(n):
    idx = np.arange(n)
    i, p = idx // 128, idx % 128
    return 256 * (i // 2) + 2 * p + (i % 2)


PI_D = _pi(2048)
PI_N = _pi(16384)


class S:
    def __init__(self, stack, nc, name):
        self.h = stack.enter_context(nc.semaphore(name))
        self.n = 0


class Prog:
    ENGS = ("pe", "act", "dve", "pool", "sp")

    def __init__(self, nc, stack):
        self.nc = nc
        self.stack = stack
        self.q = {e: [] for e in self.ENGS}
        self.bar = S(stack, nc, "bar")
        self.nbar = 0

    def sem(self, name):
        self._uid = getattr(self, "_uid", 0) + 1
        return S(self.stack, self.nc, f"{name}_{self._uid}")

    def sbt(self, name, shape, dtype):
        self._uid = getattr(self, "_uid", 0) + 1
        return self.nc.sbuf_tensor(f"{name}_{self._uid}", shape, dtype)

    def E(self, eng, method, *args, wait=None, inc=None, **kwargs):
        if wait:
            wait = [(s.h, t) for (s, t) in wait if t > 0]

        def f(h, method=method, args=args, kwargs=kwargs, wait=wait, inc=inc):
            if wait:
                for sh, t in wait:
                    h.wait_ge(sh, t)
            inst = getattr(h, method)(*args, **kwargs)
            if inc is not None:
                inst.then_inc(inc[0].h, inc[1])
        self.q[eng].append(f)
        if inc is not None:
            inc[0].n += inc[1]
            return inc[0].n
        return None

    def W(self, eng, sem, tok=None):
        t = sem.n if tok is None else tok
        if t > 0:
            self.q[eng].append(lambda h, sh=sem.h, t=t: h.wait_ge(sh, t))

    def barrier(self):
        self.nbar += 1
        tgt = 5 * self.nbar
        for e in self.ENGS:
            self.q[e].append(lambda h, sh=self.bar.h: h.sem_inc(sh, 1))
            self.q[e].append(lambda h, sh=self.bar.h, t=tgt: h.wait_ge(sh, t))
        self.bar.n = tgt

    def finalize(self):
        with self.nc.Block() as block:
            @block.tensor
            def _(h):
                for f in self.q["pe"]:
                    f(h)

            @block.scalar
            def _(h):
                for f in self.q["act"]:
                    f(h)

            @block.vector
            def _(h):
                for f in self.q["dve"]:
                    f(h)

            @block.gpsimd
            def _(h):
                for f in self.q["pool"]:
                    f(h)

            @block.sync
            def _(h):
                for f in self.q["sp"]:
                    f(h)


class Ctx:
    pass



def host_prepare(inp):
    out = {}
    asc = np.ascontiguousarray

    def bview(a):
        a = np.asarray(a)
        return a.view(np.uint8)[:, ::4]

    out["wg_pt"] = asc(bview(inp["L0_ffn_gate_W_packed"])[PI_N].T)
    out["wu_pt"] = asc(bview(inp["L0_ffn_up_W_packed"])[PI_N].T)
    out["wd_pt"] = asc(bview(inp["L0_ffn_down_W_packed"])[PI_D].T)
    out["sc_g"] = asc(np.asarray(inp["L0_ffn_gate_scales"], np.float32)[PI_N].T)
    out["sc_u"] = asc(np.asarray(inp["L0_ffn_up_scales"], np.float32)[PI_N].T)
    out["sc_d"] = asc(np.asarray(inp["L0_ffn_down_scales"], np.float32)[PI_D].T)

    def f32(name):
        return np.asarray(inp[name], np.float32)

    out["wqT0"] = asc(f32("L0_attn_wq").T[PI_D])
    out["wkT0"] = asc(f32("L0_attn_wk").T[PI_D])
    out["wvT0"] = asc(f32("L0_attn_wv").T[PI_D])
    out["woT0"] = asc(f32("L0_attn_wo").T[:, PI_D])
    out["wkTm"] = asc(f32("L0_attn_wk").T[:, PI_D])
    out["wvTm"] = asc(f32("L0_attn_wv").T)
    out["gwT0"] = asc(f32("L0_attn_gate_w")[0][PI_D].reshape(2048, 1))
    out["gb0"] = f32("L0_attn_gate_b").reshape(1, 1)
    memT = asc(f32("L0_attn_mem_bank").T)
    out["memT_sl"] = [asc(memT[:, c * MSL:(c + 1) * MSL]) for c in range(NCORES)]

    out["wqT1"] = asc(f32("L1_attn_wq").T[PI_D])
    out["wkT1"] = asc(f32("L1_attn_wk").T[PI_D])
    out["wvT1"] = asc(f32("L1_attn_wv").T[PI_D])
    out["woT1"] = asc(f32("L1_attn_wo").T[:, PI_D])
    out["bq1"] = asc(f32("L1_attn_bq").reshape(2048, 1))
    out["bk1"] = asc(f32("L1_attn_bk").reshape(2048, 1))
    out["bv1"] = asc(f32("L1_attn_bv").reshape(2048, 1))
    out["bo1"] = asc(f32("L1_attn_bo")[PI_D].reshape(2048, 1))
    out["wmqT"] = asc(f32("L1_attn_wmap_q").T)
    out["wmkT"] = asc(f32("L1_attn_wmap_k").T)

    out["bWT1"] = asc(f32("L1_ffn_base_W").T[PI_D][:, PI_D])
    out["rWT1"] = asc(f32("L1_ffn_router_W").T[PI_D])
    out["AT1"] = asc(f32("L1_ffn_A").T[PI_D])
    out["BT1"] = asc(f32("L1_ffn_B").T[:, PI_D])
    out["bias1"] = asc(f32("L1_ffn_biases").reshape(64, 1))

    def sliced(wT, ow):
        nO = wT.shape[1] // ow
        return asc(wT.reshape(16, 128, nO, ow).transpose(2, 1, 0, 3)
                   .reshape(nO, 128, 16 * ow))

    out["wq_l0"] = sliced(out["wqT0"], 128)
    out["wk_l0"] = sliced(out["wkT0"], 128)
    out["wv_l0"] = sliced(out["wvT0"], 128)
    out["wo_l0"] = sliced(out["woT0"], 128)
    out["wkm_l"] = sliced(out["wkTm"], 128)
    out["wvm_l"] = sliced(out["wvTm"], 512)
    out["gw_pc"] = asc(out["gwT0"].reshape(16, 128).T)
    out["bW_l"] = sliced(out["bWT1"], 128)
    out["rw_pc"] = asc(out["rWT1"].reshape(16, 128, 32).transpose(1, 0, 2)
                       .reshape(128, 512))
    out["at_pc"] = asc(out["AT1"].reshape(16, 128, 64).transpose(1, 0, 2)
                       .reshape(128, 1024))
    out["wq_l1"] = sliced(out["wqT1"], 128)
    out["wk_l1"] = sliced(out["wkT1"], 128)
    out["wv_l1"] = sliced(out["wvT1"], 128)
    out["wo_l1"] = sliced(out["woT1"], 128)
    bqkv = np.zeros((128, 48), np.float32)
    for j, nm in enumerate(("bq1", "bk1", "bv1")):
        bqkv[:, j::3] = out[nm].reshape(16, 128).T
    out["bqkv_pc"] = asc(bqkv)
    out["bo_pc"] = asc(out["bo1"].reshape(16, 128).T)

    x0 = np.asarray(inp["L0_attn_hidden_states"], np.float32).reshape(8192, 2048)
    out["x_sl"] = [asc(x0[c * TPC:(c + 1) * TPC]) for c in range(NCORES)]

    consts = np.zeros((128, 8), np.float32)
    consts[:, C_M4] = -4.0
    consts[:, C_M6] = -6.0
    consts[:, C_75] = 7.5
    consts[:, C_EPS5] = 1e-5
    consts[:, C_EPS6] = 1e-6
    consts[:, C_ONE] = 1.0
    out["consts"] = consts
    out["ones_row"] = np.ones((1, 128), np.float32)
    out["ident_f"] = np.eye(128, dtype=np.float32)
    out["ident_b"] = np.eye(128).astype(ml_dtypes.bfloat16)
    r64 = np.zeros((32, 64), np.float32)
    r64[np.arange(32), 2 * np.arange(32)] = 1.0
    r64[np.arange(32), 2 * np.arange(32) + 1] = 1.0
    out["r64"] = r64

    NEG = np.float32(-1.0e30)
    masks = np.zeros((NCORES, 8, 128, 384), np.float32)
    a = np.arange(128)[:, None]
    cc = np.arange(128)[None, :]
    tri_up = np.where(cc > a, 0.0, NEG).astype(np.float32)
    tri_lo = np.where(cc <= a, 0.0, NEG).astype(np.float32)
    for core in range(NCORES):
        for qt in range(8):
            m = masks[core, qt]
            m[:, 0:128] = tri_up
            m[:, 128:256] = 0.0
            m[:, 256:384] = tri_lo
            if core == 0 and qt == 0:
                m[:, 0:256] = NEG
            elif core == 0 and qt == 1:
                m[:, 0:128] = NEG
    out["masks"] = masks
    sel = np.zeros((NCORES, 128, 8), np.float32)
    for c in range(1, NCORES):
        sel[c, :, c - 1] = 1.0
    out["sel_pc"] = sel
    return out



def ap2(t, p0, pn, f0, fn, fstep=1):
    rowlen = t.shape[-1]
    return bass.AP(t, p0 * rowlen + f0, [[rowlen, pn], [fstep, fn]])


def cb(st, col):
    return st.consts[:, col:col + 1]


def xch(st, i, f0, fn):
    return st.x[:, i * TPC + f0: i * TPC + f0 + fn]


def ych(st, i, f0, fn):
    t = st.text
    return st.y[:, i * t + f0: i * t + f0 + fn]


def yown(st, i, f0, fn):
    return ych(st, i, (st.text - 1024) + f0, fn)


def setup_common(P, nc, *, text=TPC, want_identb=True, nps=6, npsT=0):
    st = Ctx()
    st.text = text
    st.consts_d = nc.dram_tensor("consts", [128, 8], F32, kind="ExternalInput").ap()
    st.ones_row_d = nc.dram_tensor("ones_row", [1, 128], F32, kind="ExternalInput").ap()
    st.ident_f_d = nc.dram_tensor("ident_f", [128, 128], F32, kind="ExternalInput").ap()
    ec = P.stack.enter_context
    st.consts = ec(P.sbt("consts_sb", [128, 8], F32))
    st.ones_row = ec(P.sbt("ones_row_sb", [1, 128], F32))
    st.ident_f = ec(P.sbt("ident_f_sb", [128, 128], F32))
    if want_identb:
        st.ident_b_d = nc.dram_tensor("ident_b", [128, 128], BF16,
                                      kind="ExternalInput").ap()
        st.ident_b = ec(P.sbt("ident_b_sb", [128, 128], BF16))

    st.ld = P.sem("ld_misc")
    P.E("sp", "dma_start", out=st.consts[:], in_=st.consts_d[:], inc=(st.ld, 16))
    P.E("sp", "dma_start", out=st.ones_row[:], in_=st.ones_row_d[:], inc=(st.ld, 16))
    P.E("sp", "dma_start", out=st.ident_f[:], in_=st.ident_f_d[:], inc=(st.ld, 16))
    if want_identb:
        P.E("sp", "dma_start", out=st.ident_b[:], in_=st.ident_b_d[:], inc=(st.ld, 16))

    st.x = ec(P.sbt("x_sb", [128, NCH * TPC], F32))
    st.y = ec(P.sbt("y_sb", [128, NCH * text], BF16))

    st.ps = [ec(nc.psum_tensor(f"ps{i}", [128, 512], F32)) for i in range(nps)]
    st.psS = ec(nc.psum_tensor("psS", [128, 512], F32))
    st.psT = [ec(nc.psum_tensor(f"psT{i}", [128, 1024], BF16)) for i in range(npsT)]
    return st


def emit_load_x(P, st, x_d, sem_done, into_fn=None):
    nc = P.nc
    _ph = ExitStack()
    xin = _ph.enter_context(P.sbt("xin_tm", [128, 2048], F32))
    ld = P.sem("ld_x")
    pv = P.sem("pe_xload")
    av = P.sem("act_xload")
    if into_fn is None:
        into_fn = lambda i, t0: xch(st, i, t0, 128)
    ntt = x_d.shape[0] // 128
    for tt in range(ntt):
        w = [(st.ld, st.ld.n)]
        if tt > 0:
            w.append((av, av.n))
        t_ld = P.E("sp", "dma_start", out=xin[:], in_=x_d[tt * 128:(tt + 1) * 128, :],
                   wait=w, inc=(ld, 16))
        for i in range(NCH):
            c, s_ = i // 2, i % 2
            src = ap2(xin, 0, 128, 256 * c + s_, 128, fstep=2)
            k = tt * NCH + i
            pbank = st.psS if k % 2 == 0 else st.ps[0]
            col = ((k // 2) % 2) * 128
            pw = [(ld, t_ld), (st.ld, st.ld.n)]
            if k >= 2:
                pw.append((av, av.n - 1))
            P.E("pe", "transpose", pbank[:, col:col + 128], src,
                st.ident_f[:], wait=pw, inc=(pv, 1))
            P.E("act", "copy", into_fn(i, tt * 128), pbank[:, col:col + 128],
                wait=[(pv, pv.n)], inc=(av, 1))
    P.W("act", av)
    P.E("act", "nop", inc=(sem_done, 1))
    _ph.close()


def emit_store_x(P, st, out_d, sem_done):
    nc = P.nc
    _ph = ExitStack()
    xout = _ph.enter_context(P.sbt("xout_tm", [128, 2048], F32))
    pv = P.sem("pe_xst")
    av = P.sem("act_xst")
    sv = P.sem("st_x")
    t_id = P.E("sp", "dma_start", out=st.ident_f[:], in_=st.ident_f_d[:],
               inc=(st.ld, 16))
    P.W("pe", st.ld, t_id)
    for tt in range(8):
        for i in range(NCH):
            c, s_ = i // 2, i % 2
            k = tt * NCH + i
            pbank = st.psS if k % 2 == 0 else st.ps[0]
            col = ((k // 2) % 2) * 128
            pw = []
            if k >= 2:
                pw.append((av, av.n - 1))
            if tt:
                pw.append((sv, tt * 16))
            P.E("pe", "transpose", pbank[:, col:col + 128],
                xch(st, i, tt * 128, 128), st.ident_f[:],
                wait=pw if pw else None, inc=(pv, 1))
            dst = ap2(xout, 0, 128, 256 * c + s_, 128, fstep=2)
            P.E("act", "copy", dst, pbank[:, col:col + 128],
                wait=[(pv, pv.n)], inc=(av, 1))
        P.E("sp", "dma_start", out=out_d[tt * 128:(tt + 1) * 128, :], in_=xout[:],
            wait=[(av, av.n)], inc=(sv, 16))
    P.W("sp", sv)
    P.E("sp", "nop", inc=(sem_done, 1))
    _ph.close()


def emit_ln(P, st, ext_f0, n, sem_ready, sem_done, x_src=None, tag="",
            dst_fn=None):
    nc = P.nc
    if x_src is None:
        x_src = lambda i, f0, w: xch(st, i, f0, w)
    if dst_fn is None:
        dst_fn = lambda i, f0, w: ych(st, i, ext_f0 + f0, w)
    _ph = ExitStack()
    ec = _ph.enter_context
    sq = ec(P.sbt(f"ln_sq{tag}", [128, 1024], F32))
    srow = ec(P.sbt(f"ln_srow{tag}", [1, 1024], F32))
    rr = ec(P.sbt(f"ln_rr{tag}", [1, 1024], F32))
    rb = ec(P.sbt(f"ln_rb{tag}", [128, 1024], F32))
    pv = P.sem(f"ln_pe{tag}")
    av = P.sem(f"ln_act{tag}")
    dv = P.sem(f"ln_dve{tag}")
    wid = min(n, 512)
    for u in range(max(1, n // 512)):
        f0 = u * wid
        for i in range(NCH):
            w = None
            if i == 0:
                w = [(sem_ready, sem_ready.n)]
                if u:
                    w.append((dv, dv.n))
            P.E("pe", "matmul", st.psS[0:1, 0:wid], cb(st, C_ONE),
                x_src(i, f0, wid), start=(i == 0), stop=(i == NCH - 1), wait=w,
                inc=(pv, 1) if i == NCH - 1 else None)
        for i in range(NCH):
            aw = [(sem_ready, sem_ready.n)] if i == 0 else []
            if u and i < 2:
                aw.append((dv, dv.n))
            if i >= 2:
                aw.append((pv, pv.n))
            P.E("act", "activation", sq[:, (i % 2) * 512:(i % 2) * 512 + wid],
                x_src(i, f0, wid), AF.Square, wait=aw if aw else None, inc=(av, 1))
            P.E("pe", "matmul", st.ps[4][0:1, 0:wid], cb(st, C_ONE),
                sq[:, (i % 2) * 512:(i % 2) * 512 + wid],
                start=(i == 0), stop=(i == NCH - 1),
                wait=[(av, av.n)], inc=(pv, 1))
        t_stats = pv.n
        P.E("dve", "tensor_scalar_mul", srow[0:1, 0:wid], st.psS[0:1, 0:wid],
            1.0 / 2048.0, wait=[(pv, t_stats)])
        P.E("dve", "tensor_scalar_mul", srow[0:1, 512:512 + wid], st.ps[4][0:1, 0:wid],
            1.0 / 2048.0)
        P.E("dve", "tensor_mul", rr[0:1, 512:512 + wid], srow[0:1, 0:wid],
            srow[0:1, 0:wid])
        t_d = P.E("dve", "tensor_sub", rr[0:1, 0:wid], srow[0:1, 512:512 + wid],
                  rr[0:1, 512:512 + wid], inc=(dv, 1))
        t_a = P.E("act", "activation", srow[0:1, 512:512 + wid], rr[0:1, 0:wid],
                  AF.Sqrt, bias=st.consts[0:1, C_EPS5:C_EPS5 + 1],
                  wait=[(dv, t_d)], inc=(av, 1))
        P.E("dve", "reciprocal", rr[0:1, 0:wid], srow[0:1, 512:512 + wid],
            wait=[(av, t_a)])
        t_d = P.E("dve", "scalar_tensor_tensor", rr[0:1, 512:512 + wid],
                  srow[0:1, 0:wid], -1.0, rr[0:1, 0:wid], ALU.mult, ALU.mult,
                  inc=(dv, 1))
        P.E("pe", "matmul", st.psS[:, 0:wid], st.ones_row[:], rr[0:1, 0:wid],
            start=True, stop=True, wait=[(dv, t_d)])
        t_pe = P.E("pe", "matmul", st.ps[4][:, 0:wid], st.ones_row[:],
                   rr[0:1, 512:512 + wid], start=True, stop=True, inc=(pv, 1))
        P.E("dve", "tensor_copy", rb[:, 0:wid], st.psS[:, 0:wid], wait=[(pv, t_pe)])
        t_d = P.E("dve", "tensor_copy", rb[:, 512:512 + wid], st.ps[4][:, 0:wid],
                  inc=(dv, 1))
        for i in range(NCH):
            P.E("dve", "tensor_mul", sq[:, (i % 2) * 512:(i % 2) * 512 + wid],
                x_src(i, f0, wid), rb[:, 0:wid],
                wait=[(dv, t_d), (av, av.n), (pv, pv.n)] if i < 2 else None)
            P.E("dve", "tensor_add", dst_fn(i, u * wid, wid),
                sq[:, (i % 2) * 512:(i % 2) * 512 + wid], rb[:, 512:512 + wid],
                inc=(dv, 1))
    P.W("dve", dv)
    P.E("dve", "nop", inc=(sem_done, 1))
    _ph.close()


def emit_halo_gather(P, st, sel_d, halo_sb, sem_x_ready, sem_done):
    nc = P.nc
    ag_in = nc.dram_tensor("hx_in", [2048, HALO], F32)
    ag_out = nc.dram_tensor("hx_out", [8 * 2048, HALO], F32)
    s_st = P.sem("hx_st")
    s_cc = P.sem("hx_cc")
    s_ld = P.sem("hx_ld")
    s_dv = P.sem("hx_dv")
    with ExitStack() as ph:
        stg = ph.enter_context(P.sbt("hx_stg", [128, NCH * HALO], F32))
        slab = [ph.enter_context(P.sbt(f"hx_sl{b}", [128, NCH * HALO], F32))
                for b in range(2)]
        sel_sb = ph.enter_context(P.sbt("hx_sel", [128, 8], F32))
        P.E("sp", "dma_start", out=sel_sb[:], in_=sel_d[:, :], inc=(s_ld, 16))
        t_d = None
        for i in range(NCH):
            t_d = P.E("dve", "tensor_copy", stg[:, i * HALO:(i + 1) * HALO],
                      xch(st, i, TPC - HALO, HALO),
                      wait=[(sem_x_ready, sem_x_ready.n)] if i == 0 else None,
                      inc=(s_dv, 1))
        t_st = P.E("sp", "dma_start",
                   out=ag_in.ap().rearrange("(c p) m -> p c m", p=128),
                   in_=stg[:], wait=[(s_dv, t_d)], inc=(s_st, 16))
        P.W("pool", s_st, t_st)
        P.E("pool", "collective_compute", "AllGather", ALU.bypass,
            replica_groups=[list(range(NCORES))],
            ins=[ag_in.ap().opt()], outs=[ag_out.ap().opt()], inc=(s_cc, 1))
        P.E("dve", "memset", halo_sb[:], 0.0, inc=(s_dv, 1))
        for s in range(8):
            b = s % 2
            t_l = P.E("sp", "dma_start", out=slab[b][:],
                      in_=ag_out.ap()[s * 2048:(s + 1) * 2048, :]
                      .rearrange("(c p) m -> p c m", p=128),
                      wait=[(s_cc, 1)] if s < 2 else [(s_dv, s_dv.n)],
                      inc=(s_ld, 16))
            t_d = P.E("dve", "scalar_tensor_tensor", halo_sb[:],
                      slab[b][:], sel_sb[:, s:s + 1], halo_sb[:],
                      ALU.mult, ALU.add,
                      wait=[(s_ld, t_l), (s_ld, s_ld.n if s == 0 else 0)],
                      inc=(s_dv, 1))
    P.W("dve", s_dv)
    P.E("dve", "nop", inc=(sem_done, 1))



def emit_ffn0(P, st, wd, sem_y, sem_done):
    nc = P.nc
    _ph = ExitStack()
    ec = _ph.enter_context
    wg_sb = [ec(P.sbt(f"wg{b}", [128, NCH * 512], BF16)) for b in range(2)]
    wu_sb = [ec(P.sbt(f"wu{b}", [128, NCH * 512], BF16)) for b in range(2)]
    wdn_sb = [ec(P.sbt(f"wdn{b}", [128, 4 * 2048], BF16)) for b in range(2)]
    pk_sb = ec(P.sbt("pk_sb", [128, 2 * 512], U8))
    vdec = ec(P.sbt("vdec_sb", [128, 3 * 512], U8))
    scs_sb = ec(P.sbt("scs_sb", [128, 512], F32))
    tmp = ec(P.sbt("dec_tmp", [128, 5 * 512], BF16))
    gel = ec(P.sbt("gelu_sb", [128, 512], BF16))
    inter = ec(P.sbt("inter_sb", [128, 4 * 512], BF16))

    s_ld = P.sem("f0_ld")
    s_dd = P.sem("f0_dd")
    s_da = P.sem("f0_da")
    s_pe = P.sem("f0_pe")
    s_ga = P.sem("f0_ga")
    s_iv = P.sem("f0_iv")
    s_xa = P.sem("f0_xa")

    dec_done = {}
    pe_done = {}
    slot_free_tok = {}
    last_mphase_tok = {}
    last_sgread_tok = {}
    last_relu = [0]
    last_gelu_tok = {}
    last_inter_tok = {}
    down_xa_tok = {}
    gu_ctr = [0]

    def unit_list(j):
        nsl = j * SCW
        units = []
        for c in range(8):
            units.append(("g",
                          wd["wg_pt"][c * 128:(c + 1) * 128, nsl:nsl + 512],
                          bass.AP(wd["sc_g"].tensor, (8 * c) * NFF + nsl,
                                  [[NFF, 8], [0, 16], [1, 512]]),
                          wg_sb[j % 2], 2 * c, 512, 1.0))
            units.append(("u",
                          wd["wu_pt"][c * 128:(c + 1) * 128, nsl:nsl + 512],
                          bass.AP(wd["sc_u"].tensor, (8 * c) * NFF + nsl,
                                  [[NFF, 8], [0, 16], [1, 512]]),
                          wu_sb[j % 2], 2 * c, 512, 1.0))
        for cp in range(2):
            cpr = 2 * j + cp
            for strip in range(4):
                units.append(("d",
                              wd["wd_pt"][cpr * 128:(cpr + 1) * 128,
                                          strip * 512:(strip + 1) * 512],
                              bass.AP(wd["sc_d"].tensor,
                                      (8 * cpr) * 2048 + strip * 512,
                                      [[2048, 8], [0, 16], [1, 512]]),
                              wdn_sb[j % 2], 2 * cp, strip * 512, 0.5))
        return units

    def stream_sc(j):
        first_dec_w = [(s_pe, pe_done[j - 2])] if j >= 2 else []
        for iu, unit in enumerate(unit_list(j)):
            kind, pk_src, scs_src, dst, chb, extra, fold = unit
            gu = gu_ctr[0]
            gu_ctr[0] += 1
            slot = gu % 2
            so = slot * 512
            lw = [(s_dd, slot_free_tok[slot])] if slot in slot_free_tok else None
            P.E("sp", "dma_start", out=pk_sb[:, so:so + 512], in_=pk_src,
                wait=lw, inc=(s_ld, 16))
            sw = [(s_dd, last_mphase_tok[1])] if 1 in last_mphase_tok else None
            t_l2 = P.E("sp", "dma_start", out=scs_sb[:, 0:512], in_=scs_src,
                       wait=sw, inc=(s_ld, 16))
            up = gu % 2
            v_t = vdec[:, up * 512: up * 512 + 512]
            r_t = vdec[:, 1024:1536]
            rb_t = tmp[:, 0:512]
            m_t = tmp[:, 512:1024]
            for s_i in (0, 1):
                npar = s_i
                sg_t = tmp[:, 1024:1536]
                a_t = tmp[:, 1536:2048]
                b_t = tmp[:, 2048:2560]
                vw = [(s_ld, t_l2)]
                if s_i == 0:
                    vw += (first_dec_w if iu == 0 else [])
                    if up in last_sgread_tok:
                        vw.append((s_da, last_sgread_tok[up]))
                    P.E("dve", "tensor_single_scalar", v_t, pk_sb[:, so:so + 512],
                        15, ALU.bitwise_and, wait=vw)
                else:
                    if last_relu[0]:
                        vw.append((s_da, last_relu[0]))
                    P.E("dve", "tensor_single_scalar", v_t, pk_sb[:, so:so + 512],
                        4, ALU.logical_shift_right, wait=vw)
                rw = [(s_da, last_relu[0])] if last_relu[0] else None
                P.E("dve", "tensor_single_scalar", r_t, v_t, 7, ALU.bitwise_and,
                    wait=rw)
                t_r = P.E("dve", "tensor_copy", rb_t, r_t, inc=(s_dd, 1))
                aw = [(s_dd, t_r)]
                if last_mphase_tok:
                    aw.append((s_dd, max(last_mphase_tok.values())))
                P.E("act", "activation", sg_t, v_t, AF.Sign, scale=-1.0,
                    bias=cb(st, C_75), wait=aw)
                P.E("act", "activation", a_t, r_t, AF.Relu, bias=cb(st, C_M4))
                t_a = P.E("act", "activation", b_t, r_t, AF.Relu, bias=cb(st, C_M6),
                          inc=(s_da, 1))
                last_relu[0] = t_a
                if s_i == 1:
                    last_sgread_tok[up] = t_a
                P.E("dve", "tensor_add", m_t, rb_t, a_t, wait=[(s_da, t_a)])
                P.E("dve", "scalar_tensor_tensor", m_t, m_t, 0.5, b_t, ALU.mult, ALU.add)
                P.E("dve", "tensor_mul", m_t, m_t, sg_t)
                if kind == "d":
                    dsto = (chb + s_i) * 2048 + extra
                else:
                    dsto = (chb + s_i) * 512
                t_m = P.E("dve", "scalar_tensor_tensor", dst[:, dsto:dsto + 512],
                          scs_sb[:, 0:512], fold, m_t, ALU.mult, ALU.mult,
                          inc=(s_dd, 1))
                last_mphase_tok[npar] = t_m
            slot_free_tok[slot] = last_mphase_tok[1]
        dec_done[j] = s_dd.n

    def matmul_sc(j):
        b = j % 2
        for tt2 in range(2):
            for ln in range(4):
                un = tt2 * 4 + ln
                gup = (j * 8 + un) % 2
                gps, ups = st.ps[gup], st.ps[2 + gup]
                w0 = [(s_dd, dec_done[j]), (sem_y, sem_y.n)]
                if gup in last_gelu_tok:
                    w0.append((s_ga, last_gelu_tok[gup]))
                if gup in last_inter_tok:
                    w0.append((s_iv, last_inter_tok[gup]))
                for i in range(NCH):
                    P.E("pe", "matmul", gps[:],
                        wg_sb[b][:, i * 512 + ln * 128: i * 512 + ln * 128 + 128],
                        yown(st, i, tt2 * 512, 512),
                        start=(i == 0), stop=(i == NCH - 1), wait=w0 if i == 0 else None)
                t_pe = None
                for i in range(NCH):
                    t_pe = P.E("pe", "matmul", ups[:],
                               wu_sb[b][:, i * 512 + ln * 128: i * 512 + ln * 128 + 128],
                               yown(st, i, tt2 * 512, 512),
                               start=(i == 0), stop=(i == NCH - 1),
                               inc=(s_pe, 1) if i == NCH - 1 else None)
                gw = [(s_pe, t_pe)]
                if last_inter_tok:
                    gw.append((s_iv, max(last_inter_tok.values())))
                t_g = P.E("act", "activation", gel[:, 0:512], gps[:],
                          AF.Gelu_apprx_tanh, wait=gw, inc=(s_ga, 1))
                last_gelu_tok[gup] = t_g
                t_i = P.E("dve", "tensor_mul",
                          inter[:, ln * 512:(ln + 1) * 512],
                          gel[:, 0:512], ups[:],
                          wait=[(s_ga, t_g), (s_pe, t_pe)], inc=(s_iv, 1))
                last_inter_tok[gup] = t_i
            t_inter_half = s_iv.n
            for dch in range(NCH):
                dp = (dch + tt2) % 2
                dps = st.ps[4] if dp == 0 else st.psS
                w0 = [(s_iv, t_inter_half)]
                if dp in down_xa_tok:
                    w0.append((s_xa, down_xa_tok[dp]))
                t_pe = None
                for ln in range(4):
                    t_pe = P.E("pe", "matmul", dps[:],
                               wdn_sb[b][:, ln * 2048 + dch * 128: ln * 2048 + dch * 128 + 128],
                               inter[:, ln * 512:(ln + 1) * 512],
                               start=(ln == 0), stop=(ln == 3),
                               wait=w0 if ln == 0 else None,
                               inc=(s_pe, 1) if ln == 3 else None)
                t_x = P.E("dve", "tensor_add", xch(st, dch, tt2 * 512, 512),
                          xch(st, dch, tt2 * 512, 512), dps[:],
                          wait=[(s_pe, t_pe)], inc=(s_xa, 1))
                down_xa_tok[dp] = t_x
        pe_done[j] = s_pe.n

    for j in range(NSC):
        stream_sc(j)
        matmul_sc(j)
    P.W("dve", s_xa)
    P.E("dve", "nop", inc=(sem_done, 1))
    _ph.close()



def emit_attn0(P, st, wd, sem_y, sem_done, stop_after=None, yh=None,
               yh_stack=None, byp_sb=None):
    nc = P.nc
    ec = P.stack.enter_context
    s_ld = P.sem("a0_ld")
    s_cl = P.sem("a0_cast")
    s_pe = P.sem("a0_pe")
    s_av = P.sem("a0_act")
    s_dv = P.sem("a0_dve")
    s_cc = P.sem("a0_cc")
    s_st = P.sem("a0_st")

    agk_in = nc.dram_tensor("agk_in", [2048, MSL], BF16)
    agk_out = nc.dram_tensor("agk_out", [8 * 2048, MSL], BF16)
    agv_in = nc.dram_tensor("agv_in", [MSL, 2048], BF16)
    agv_out = nc.dram_tensor("agv_out", [M_MEM, 2048], BF16)
    with ExitStack() as ph:
        memT_sb = ph.enter_context(P.sbt("memT_sb", [128, 16 * MSL], BF16))
        wk_sb = ph.enter_context(P.sbt("wkm_sb", [128, 16 * 128], BF16))
        kms = ph.enter_context(P.sbt("kms_sb", [128, 16 * MSL], BF16))
        wvm_sb = ph.enter_context(P.sbt("wvm_sb", [128, 16 * 512], BF16))
        vms = ph.enter_context(P.sbt("vms_sb", [128, 4 * 2048], BF16))
        t_mem = P.E("pool", "dma_start", out=memT_sb[:],
                    in_=wd["memT_sl"].rearrange("(c p) m -> p c m", p=128),
                    inc=(s_cl, 16))
        if stop_after == "p0ld":
            P.W("pool", s_cl)
            P.E("pool", "nop", inc=(sem_done, 1))
        for dk in range(0 if stop_after == 'p0ld' else 16):
            t_w = P.E("pool", "dma_start", out=wk_sb[:],
                      in_=wd["wkm_l"][dk, :, :],
                      wait=[(s_av, s_av.n)] if dk else None, inc=(s_cl, 16))
            if stop_after == "p0w":
                continue
            for u in range(MSL // 512):
                pp = st.ps[(dk + u) % 2]
                t_pe = None
                for i in range(NCH):
                    t_pe = P.E("pe", "matmul", pp[:],
                               wk_sb[:, i * 128:(i + 1) * 128],
                               memT_sb[:, i * MSL + u * 512: i * MSL + u * 512 + 512],
                               start=(i == 0), stop=(i == NCH - 1),
                               wait=[(s_cl, s_cl.n), (s_av, max(0, s_av.n - 1))] if i == 0 else None,
                               inc=(s_pe, 1) if i == NCH - 1 else None)
                P.E("act", "activation", kms[:, dk * MSL + u * 512: dk * MSL + u * 512 + 512],
                    pp[:], AF.Copy, scale=INV_SQRT_D, wait=[(s_pe, t_pe)], inc=(s_av, 1))
        t_kst = None
        if stop_after not in ('p0ld', 'p0k', 'p0w'):
            t_kst = P.E("sp", "dma_start",
                        out=agk_in.ap().rearrange("(c p) m -> p c m", p=128),
                        in_=kms[:], wait=[(s_av, s_av.n)], inc=(s_ld, 16))
        for dvs in range(4 if stop_after not in ('p0ld', 'p0k', 'p0w') else 0):
            t_w = P.E("pool", "dma_start", out=wvm_sb[:],
                      in_=wd["wvm_l"][dvs, :, :],
                      wait=[(s_av, s_av.n)] if dvs else None, inc=(s_cl, 16))
            for mb in range(4):
                pp = st.ps[2 + mb % 2]
                t_pe = None
                for i in range(NCH):
                    t_pe = P.E("pe", "matmul", pp[:],
                               memT_sb[:, i * MSL + mb * 128: i * MSL + mb * 128 + 128],
                               wvm_sb[:, i * 512:(i + 1) * 512],
                               start=(i == 0), stop=(i == NCH - 1),
                               wait=[(s_cl, s_cl.n), (s_av, max(0, s_av.n - 1))] if i == 0 else None,
                               inc=(s_pe, 1) if i == NCH - 1 else None)
                P.E("act", "activation",
                    vms[:, mb * 2048 + dvs * 512: mb * 2048 + dvs * 512 + 512],
                    pp[:], AF.Copy, wait=[(s_pe, t_pe)], inc=(s_av, 1))
        if stop_after not in ('p0ld', 'p0k', 'p0w'):
            t_vst = P.E("sp", "dma_start",
                        out=agv_in.ap().rearrange("(c p) m -> p c m", p=128),
                        in_=vms[:], wait=[(s_av, s_av.n)], inc=(s_ld, 16))
        P.W("pool", s_cl)
        P.W("pool", s_ld)
        if stop_after not in ('p0nc', 'p0ld', 'p0k', 'p0w'):
            P.E("pool", "collective_compute", "AllGather", ALU.bypass,
                replica_groups=[list(range(NCORES))],
                ins=[agk_in.ap().opt()], outs=[agk_out.ap().opt()], inc=(s_cc, 1))
            P.E("pool", "collective_compute", "AllGather", ALU.bypass,
                replica_groups=[list(range(NCORES))],
                ins=[agv_in.ap().opt()], outs=[agv_out.ap().opt()], inc=(s_cc, 1))
        P.barrier()
    if stop_after in ('p0nc', 'p0ld', 'p0k', 'p0w'):
        P.W("pool", s_cl)
        P.E("pool", "nop", inc=(sem_done, 1))
        return
    if stop_after == "p0":
        P.W("dve", s_cc)
        P.E("dve", "nop", inc=(sem_done, 1))
        return

    ph_ctx = ExitStack()
    ctx_sb = ph_ctx.enter_context(P.sbt("ctx_sb", [128, 16 * 1024], BF16))
    with ExitStack() as ph:
        st.masks_d = nc.dram_tensor("masks", [8, 128, 384], F32,
                                    kind="ExternalInput").ap()
        st.masks = ph.enter_context(P.sbt("masks_sb", [128, 8 * 384], F32))
        for qt in range(8):
            P.E("sp", "dma_start", out=st.masks[:, qt * 384:(qt + 1) * 384],
                in_=st.masks_d[qt, :, :], inc=(st.ld, 16))
        wq_sb = ph.enter_context(P.sbt("wq_sb", [128, 16 * 128], BF16))
        wk_sb = ph.enter_context(P.sbt("wk_sb", [128, 16 * 128], BF16))
        wv_sb = ph.enter_context(P.sbt("wv_sb", [128, 16 * 128], BF16))
        q_sb = ph.enter_context(P.sbt("q_sb", [128, 1024], BF16))
        k_sb = ph.enter_context(P.sbt("k_sb", [128, TEXT], BF16))
        v_sb = ph.enter_context(P.sbt("v_sb", [128, TEXT], BF16))
        vt_sb = ph.enter_context(P.sbt("vt_sb", [128, 10 * 128], BF16))
        sc_sb = ph.enter_context(P.sbt("sc_sb", [128, 2 * 384], F32))
        sm_sb = ph.enter_context(P.sbt("sm_sb", [128, 8], F32))
        w_sb = ph.enter_context(P.sbt("w_sb", [128, 2 * 384], BF16))
        wt_sb = ph.enter_context(P.sbt("wt_sb", [128, 2 * 384], BF16))
        for h in range(H):
            hs = slice(h * 128, (h + 1) * 128)
            t_w = None
            for nm, sb in (("wq_l0", wq_sb), ("wk_l0", wk_sb), ("wv_l0", wv_sb)):
                t_w = P.E("pool", "dma_start", out=sb[:],
                          in_=wd[nm][h, :, :],
                          wait=[(s_pe, s_pe.n)] if h else None, inc=(s_cl, 16))
            qu = [(lambda i, o=o: yown(st, i, o, 512), 512, 256 + o)
                  for o in (0, 512)]
            ku = [(lambda i: yh(i, 0, 256), 256, 0)] + \
                 [(lambda i, o=o: yown(st, i, o, 512), 512, 256 + o)
                  for o in (0, 512)]
            for (wsb, dst, units, scl) in ((wq_sb, q_sb, qu, INV_SQRT_HD),
                                           (wk_sb, k_sb, ku, None),
                                           (wv_sb, v_sb, ku, None)):
                for u, (src_fn, wid, dsto) in enumerate(units):
                    if dst is q_sb:
                        dsto -= 256
                    pp = st.ps[u % 2]
                    t_pe = None
                    for i in range(NCH):
                        t_pe = P.E("pe", "matmul", pp[:, 0:wid],
                                   wsb[:, i * 128:(i + 1) * 128],
                                   src_fn(i),
                                   start=(i == 0), stop=(i == NCH - 1),
                                   wait=[(s_cl, t_w), (sem_y, sem_y.n),
                                         (s_av, s_av.n)] if i == 0 else None,
                                   inc=(s_pe, 1) if i == NCH - 1 else None)
                    if scl is None:
                        P.E("act", "activation", dst[:, dsto:dsto + wid],
                            pp[:, 0:wid], AF.Copy, wait=[(s_pe, t_pe)], inc=(s_av, 1))
                    else:
                        P.E("act", "activation", dst[:, dsto:dsto + wid],
                            pp[:, 0:wid], AF.Copy, scale=scl,
                            wait=[(s_pe, t_pe)], inc=(s_av, 1))
            if stop_after == "p1q" and h == 0:
                cpv = P.sem("a0_dbgq")
                P.W("dve", s_av)
                P.E("dve", "tensor_copy", xch(st, 0, 0, 1024), q_sb[:, 0:1024],
                    inc=(cpv, 1))
                P.E("dve", "tensor_copy", xch(st, 1, 0, 1024), k_sb[:, 0:1024],
                    inc=(cpv, 1))
                P.E("dve", "tensor_copy", xch(st, 2, 0, 1024), k_sb[:, 256:1280],
                    inc=(cpv, 1))
                P.E("dve", "tensor_copy", xch(st, 3, 0, 1024), v_sb[:, 256:1280],
                    inc=(cpv, 1))
                P.W("dve", cpv)
                P.E("dve", "nop", inc=(sem_done, 1))
                break
            for ts_ in range(10):
                pb = st.psT[ts_ % 2]
                col = ((ts_ // 2) % 4) * 128
                w = [(s_av, s_av.n), (s_dv, s_dv.n)] if ts_ == 0 else (
                    [(s_dv, s_dv.n - 1)] if ts_ >= 2 else None)
                t_pe = P.E("pe", "transpose", pb[:, col:col + 128],
                           v_sb[:, ts_ * 128:(ts_ + 1) * 128], st.ident_b[:],
                           wait=w, inc=(s_pe, 1))
                P.E("dve", "tensor_copy", vt_sb[:, ts_ * 128:(ts_ + 1) * 128],
                    pb[:, col:col + 128],
                    wait=[(s_pe, t_pe)], inc=(s_dv, 1))
            if stop_after == "p1s" and h == 1:
                cpv = P.sem("a0_dbgs")
                P.W("dve", s_av)
                P.E("dve", "tensor_copy", xch(st, 0, 0, 1024),
                    ctx_sb[:, 0:1024], inc=(cpv, 1))
                P.E("dve", "tensor_copy", xch(st, 1, 0, 768),
                    st.x[:, 1 * TPC: 1 * TPC + 768] if False else sc_sb[:, 0:768],
                    inc=(cpv, 1))
                P.E("dve", "tensor_copy", xch(st, 2, 0, 768), w_sb[:, 0:768],
                    inc=(cpv, 1))
                P.E("dve", "tensor_copy", xch(st, 3, 0, 768), wt_sb[:, 0:768],
                    inc=(cpv, 1))
                P.W("dve", cpv)
                P.E("dve", "nop", inc=(sem_done, 1))
                break
            wt_tok = {}
            for qt in range(8):
                sp_ = qt % 2
                pp = st.ps[2 + sp_]
                t_pe = P.E("pe", "matmul", pp[:, 0:384],
                           q_sb[:, qt * 128:(qt + 1) * 128],
                           k_sb[:, qt * 128:qt * 128 + 384],
                           start=True, stop=True,
                           wait=[(s_av, s_av.n), (s_dv, s_dv.n)] if qt == 0 else
                           [(s_dv, s_dv.n - 4)],
                           inc=(s_pe, 1))
                P.E("dve", "tensor_add", sc_sb[:, sp_ * 384:sp_ * 384 + 384],
                    pp[:, 0:384], st.masks[:, qt * 384:(qt + 1) * 384],
                    wait=[(s_pe, t_pe), (st.ld, st.ld.n)])
                t_d = P.E("dve", "reduce_max", sm_sb[:, sp_:sp_ + 1],
                          sc_sb[:, sp_ * 384:sp_ * 384 + 384], axis=AX,
                          inc=(s_dv, 1))
                t_d = P.E("dve", "tensor_scalar_mul", sm_sb[:, 4 + sp_:5 + sp_],
                          sm_sb[:, sp_:sp_ + 1], -1.0, wait=[(s_dv, t_d)],
                          inc=(s_dv, 1))
                t_a = P.E("act", "activation", sc_sb[:, sp_ * 384:sp_ * 384 + 384],
                          sc_sb[:, sp_ * 384:sp_ * 384 + 384], AF.Exp,
                          bias=sm_sb[:, 4 + sp_:5 + sp_],
                          wait=[(s_dv, t_d)], inc=(s_av, 1))
                t_d = P.E("dve", "reduce_sum", sm_sb[:, 2 + sp_:3 + sp_],
                          sc_sb[:, sp_ * 384:sp_ * 384 + 384], axis=AX,
                          wait=[(s_av, t_a)], inc=(s_dv, 1))
                t_d = P.E("dve", "reciprocal", sm_sb[:, 6 + sp_:7 + sp_],
                          sm_sb[:, 2 + sp_:3 + sp_], wait=[(s_dv, t_d)],
                          inc=(s_dv, 1))
                ww = [(s_dv, t_d)]
                if sp_ in wt_tok:
                    ww.append((s_pe, wt_tok[sp_]))
                t_d = P.E("dve", "tensor_scalar_mul",
                          w_sb[:, sp_ * 384:sp_ * 384 + 384],
                          sc_sb[:, sp_ * 384:sp_ * 384 + 384],
                          sm_sb[:, 6 + sp_:7 + sp_],
                          wait=ww, inc=(s_dv, 1))
                for kt in range(3):
                    pb = st.psT[kt % 2]
                    col = 512 + (kt // 2) * 128
                    t_pe = P.E("pe", "transpose", pb[:, col:col + 128],
                               w_sb[:, sp_ * 384 + kt * 128: sp_ * 384 + kt * 128 + 128],
                               st.ident_b[:],
                               wait=[(s_dv, t_d)] if kt == 0 else
                               ([(s_dv, s_dv.n - 1)] if kt >= 2 else None),
                               inc=(s_pe, 1))
                    P.E("dve", "tensor_copy",
                        wt_sb[:, sp_ * 384 + kt * 128: sp_ * 384 + kt * 128 + 128],
                        pb[:, col:col + 128],
                        wait=[(s_pe, t_pe)], inc=(s_dv, 1))
                    wt_tok[sp_] = t_pe
                t_pe = None
                for kt in range(3):
                    t_pe = P.E("pe", "matmul", st.psS[:, 0:128],
                               vt_sb[:, (qt + kt) * 128:(qt + kt) * 128 + 128],
                               wt_sb[:, sp_ * 384 + kt * 128: sp_ * 384 + kt * 128 + 128],
                               start=(kt == 0), stop=(kt == 2),
                               wait=[(s_dv, s_dv.n), (s_av, max(0, s_av.n - 0))] if kt == 0 else None,
                               inc=(s_pe, 1) if kt == 2 else None)
                P.E("act", "activation", ctx_sb[:, h * 1024 + qt * 128: h * 1024 + qt * 128 + 128],
                    st.psS[:, 0:128], AF.Copy, wait=[(s_pe, t_pe)], inc=(s_av, 1))
        P.barrier()
    if stop_after == "p1s":
        ph_ctx.close()
        return
    if stop_after == "p1q":
        ph_ctx.close()
        return
    if stop_after == "p1":
        P.W("dve", s_av)
        cpv = P.sem("a0_dbgcp")
        for i in range(NCH):
            P.E("dve", "tensor_copy", xch(st, i, 0, 1024),
                ctx_sb[:, i * 1024:(i + 1) * 1024], inc=(cpv, 1))
        P.W("dve", cpv)
        P.E("dve", "nop", inc=(sem_done, 1))
        ph_ctx.close()
        return

    ph_ol = ExitStack()
    ol_sb = ph_ol.enter_context(P.sbt("ol_sb", [128, 16 * 1024], BF16))
    gw_sb = ph_ol.enter_context(P.sbt("gw_sb", [128, 16], BF16))
    gwf_sb = ph_ol.enter_context(P.sbt("gwf_sb", [128, 16], F32))
    gb_sb = ph_ol.enter_context(P.sbt("gb_sb", [128, 1], F32))
    with ExitStack() as ph:
        wo_sb = [ph.enter_context(P.sbt(f"wo_sb{b}", [128, 16 * 128], BF16))
                 for b in range(2)]
        for do in range(16):
            b = do % 2
            t_w = P.E("pool", "dma_start", out=wo_sb[b][:],
                      in_=wd["wo_l0"][do, :, :],
                      wait=[(s_pe, max(0, s_pe.n - 1))] if do >= 2 else None,
                      inc=(s_cl, 16))
            for tt2 in range(2):
                pp = st.ps[(do * 2 + tt2) % 4]
                t_pe = None
                for i in range(NCH):
                    t_pe = P.E("pe", "matmul", pp[:],
                               wo_sb[b][:, i * 128:(i + 1) * 128],
                               ctx_sb[:, i * 1024 + tt2 * 512: i * 1024 + tt2 * 512 + 512],
                               start=(i == 0), stop=(i == NCH - 1),
                               wait=[(s_cl, t_w), (s_dv, max(0, s_dv.n - 2)),
                                     (s_av, max(0, s_av.n - 2))] if i == 0 else None,
                               inc=(s_pe, 1) if i == NCH - 1 else None)
                t_x = P.E("dve", "tensor_add", xch(st, do, tt2 * 512, 512),
                          xch(st, do, tt2 * 512, 512), pp[:],
                          wait=[(s_pe, t_pe)], inc=(s_dv, 1))
                P.E("act", "activation",
                    ol_sb[:, do * 1024 + tt2 * 512: do * 1024 + tt2 * 512 + 512],
                    pp[:], AF.Copy, wait=[(s_pe, t_pe), (s_dv, t_x)], inc=(s_av, 1))
        P.barrier()
    if stop_after == "p2":
        P.W("dve", s_dv)
        P.E("dve", "nop", inc=(sem_done, 1))
        ph_ol.close()
        ph_ctx.close()
        return

    s_cp = P.sem("a0_cp")
    t_gwf = P.E("sp", "dma_start", out=gwf_sb[:], in_=wd["gw_pc"][:, :],
                inc=(s_ld, 16))
    P.E("pool", "tensor_copy", gw_sb[:], gwf_sb[:],
        wait=[(s_ld, t_gwf)], inc=(s_cp, 1))
    P.E("sp", "dma_start", out=gb_sb[:],
        in_=bass.AP(wd["gb0"].tensor, 0, [[0, 128], [1, 1]]), inc=(s_ld, 16))
    t_gw = s_cp.n
    t_gb = s_ld.n
    for tt in range(8):
        t_pe = None
        for i in range(NCH):
            t_pe = P.E("pe", "matmul", st.psS[0:128, 0:1],
                       ol_sb[:, i * 1024 + tt * 128: i * 1024 + tt * 128 + 128],
                       gw_sb[:, i:i + 1],
                       start=(i == 0), stop=(i == NCH - 1),
                       wait=[(s_cp, t_gw), (s_av, s_av.n)] if i == 0 and tt == 0 else
                       ([(s_av, s_av.n)] if i == 0 else None),
                       inc=(s_pe, 1) if i == NCH - 1 else None)
        P.E("act", "activation", byp_sb[:, tt:tt + 1], st.psS[0:128, 0:1],
            AF.Sigmoid, bias=gb_sb[:, 0:1],
            wait=[(s_pe, t_pe), (st.ld, st.ld.n), (s_ld, t_gb)], inc=(s_av, 1))
    P.barrier()
    ph_ol.close()
    ph_ctx.close()
    if yh_stack is not None:
        yh_stack.close()
    if stop_after == "byp":
        P.W("dve", s_av)
        P.E("dve", "nop", inc=(sem_done, 1))
        return

    with ExitStack() as ph:
        km_sb = [ph.enter_context(P.sbt(f"km_sb{b}", [128, 16 * 512], BF16))
                 for b in range(2)]
        vm_sb = [ph.enter_context(P.sbt(f"vm_sb{b}", [128, 512], BF16))
                 for b in range(2)]
        wo2_sb = [ph.enter_context(P.sbt(f"wo2_sb{b}", [128, 16 * 128], BF16))
                  for b in range(2)]
        sim_sb = ph.enter_context(P.sbt("sim_sb", [128, 4096], F32))
        wn_sb = ph.enter_context(P.sbt("wn_sb", [128, 4096], BF16))
        wnt_sb = ph.enter_context(P.sbt("wnt_sb", [128, 32 * 256], BF16))
        t8_sb = ph.enter_context(P.sbt("t8_sb", [128, 16], F32))
        opre_sb = ph.enter_context(P.sbt("opre_sb", [128, 16 * 256], BF16))
        t_k_ld = {}
        for grp in range(4):
            for lt in range(2):
                tt = grp * 2 + lt
                for ms in range(8):
                    b = ms % 2
                    t_k = P.E("sp", "dma_start", out=km_sb[b][:],
                              in_=agk_out.ap()[ms * 2048:(ms + 1) * 2048, :]
                              .rearrange("(c p) m -> p c m", p=128),
                              wait=[(s_cc, 1), (s_pe, max(0, s_pe.n - 1))],
                              inc=(s_ld, 16))
                    t_k_ld[ms] = t_k
                    pp = st.ps[ms % 2]
                    t_pe = None
                    for i in range(NCH):
                        t_pe = P.E("pe", "matmul", pp[:],
                                   yown(st, i, tt * 128, 128),
                                   km_sb[b][:, i * 512:(i + 1) * 512],
                                   start=(i == 0), stop=(i == NCH - 1),
                                   wait=[(s_ld, t_k_ld[ms]),
                                         (s_av, max(0, s_av.n - 1))] if i == 0 else None,
                                   inc=(s_pe, 1) if i == NCH - 1 else None)
                    P.E("act", "activation", sim_sb[:, ms * 512:(ms + 1) * 512],
                        pp[:], AF.Copy, wait=[(s_pe, t_pe)], inc=(s_av, 1))
                t_sim = s_av.n
                t_d = P.E("dve", "max", t8_sb[:, 0:8], sim_sb[:],
                          wait=[(s_av, t_sim)], inc=(s_dv, 1))
                t_d = P.E("dve", "tensor_scalar_mul", t8_sb[:, 8:9], t8_sb[:, 0:1],
                          -1.0, wait=[(s_dv, t_d)], inc=(s_dv, 1))
                t_d = P.E("dve", "tensor_add", t8_sb[:, 9:10], t8_sb[:, 7:8],
                          t8_sb[:, 8:9], wait=[(s_dv, t_d)], inc=(s_dv, 1))
                t_a = P.E("act", "activation", t8_sb[:, 10:11], t8_sb[:, 9:10],
                          AF.Exp, wait=[(s_dv, t_d)], inc=(s_av, 1))
                t_a2 = P.E("act", "activation", sim_sb[:], sim_sb[:], AF.Exp,
                           bias=t8_sb[:, 8:9], wait=[(s_dv, t_d)], inc=(s_av, 1))
                t_d = P.E("dve", "scalar_tensor_tensor",
                          wn_sb[:, 0:4096], sim_sb[:],
                          t8_sb[:, 10:11], sim_sb[:], ALU.is_ge, ALU.mult,
                          wait=[(s_av, s_av.n)], inc=(s_dv, 1))
                t_d = P.E("dve", "reduce_sum", t8_sb[:, 11:12],
                          wn_sb[:, 0:4096], axis=AX, inc=(s_dv, 1))
                t_d = P.E("dve", "reciprocal", t8_sb[:, 12:13], t8_sb[:, 11:12],
                          wait=[(s_dv, t_d)], inc=(s_dv, 1))
                t_d = P.E("dve", "tensor_scalar",
                          wn_sb[:, 0:4096],
                          wn_sb[:, 0:4096],
                          t8_sb[:, 12:13], byp_sb[:, tt:tt + 1],
                          ALU.mult, ALU.mult, wait=[(s_dv, t_d)], inc=(s_dv, 1))
                for ms2 in range(32):
                    pb = st.psT[ms2 % 2]
                    col = ((ms2 // 2) % 4) * 128
                    t_pe = P.E("pe", "transpose", pb[:, col:col + 128],
                               wn_sb[:, ms2 * 128: ms2 * 128 + 128],
                               st.ident_b[:],
                               wait=[(s_dv, t_d)] if ms2 == 0 else
                               ([(s_dv, s_dv.n - 1)] if ms2 >= 2 else None),
                               inc=(s_pe, 1))
                    P.E("dve", "tensor_copy",
                        wnt_sb[:, ms2 * 256 + lt * 128: ms2 * 256 + lt * 128 + 128],
                        pb[:, col:col + 128],
                        wait=[(s_pe, t_pe)], inc=(s_dv, 1))
            t_wnt = s_dv.n
            for pass_ in range(4):
                t_pe_last = {}
                mc_tok = {}
                for mc in range(32):
                    vw = [(s_cc, 2)]
                    if mc - 2 in mc_tok:
                        vw.append((s_pe, mc_tok[mc - 2]))
                    elif pass_ > 0:
                        vw.append((s_pe, s_pe.n))
                    t_v = P.E("sp", "dma_start", out=vm_sb[mc % 2][:],
                              in_=agv_out.ap()[mc * 128:(mc + 1) * 128,
                                               pass_ * 512:(pass_ + 1) * 512],
                              wait=vw, inc=(s_ld, 16))
                    for ldv in range(4):
                        dv_ = pass_ * 4 + ldv
                        pp = st.ps[ldv]
                        w = None
                        if mc == 0:
                            w = [(s_dv, t_wnt), (s_ld, t_v),
                                 (s_av, s_av.n)]
                        elif ldv == 0:
                            w = [(s_ld, t_v)]
                        t_pe_last[ldv] = P.E("pe", "matmul", pp[:, 0:256],
                                             vm_sb[mc % 2][:, ldv * 128:(ldv + 1) * 128],
                                             wnt_sb[:, mc * 256:(mc + 1) * 256],
                                             start=(mc == 0), stop=(mc == 31), wait=w,
                                             inc=(s_pe, 1))
                        mc_tok[mc] = t_pe_last[ldv]
                for ldv in range(4):
                    dv_ = pass_ * 4 + ldv
                    P.E("act", "activation", opre_sb[:, dv_ * 256:(dv_ + 1) * 256],
                        st.ps[ldv][:, 0:256], AF.Copy,
                        wait=[(s_pe, t_pe_last[ldv])], inc=(s_av, 1))
            t_opre = s_av.n
            for do in range(16):
                b2 = do % 2
                t_w = P.E("pool", "dma_start", out=wo2_sb[b2][:],
                          in_=wd["wo_l0"][do, :, :],
                          wait=[(s_pe, max(0, s_pe.n - 1))] if (grp or do >= 2) else None,
                          inc=(s_cl, 16))
                pp = st.ps[4] if do % 2 == 0 else st.psS
                t_pe = None
                for i in range(NCH):
                    t_pe = P.E("pe", "matmul", pp[:, 0:256],
                               wo2_sb[b2][:, i * 128:(i + 1) * 128],
                               opre_sb[:, i * 256:(i + 1) * 256],
                               start=(i == 0), stop=(i == NCH - 1),
                               wait=[(s_av, t_opre), (s_cl, t_w),
                                     (s_dv, max(0, s_dv.n - 1))] if i == 0 else None,
                               inc=(s_pe, 1) if i == NCH - 1 else None)
                P.E("dve", "tensor_add", xch(st, do, grp * 256, 256),
                    xch(st, do, grp * 256, 256), pp[:, 0:256],
                    wait=[(s_pe, t_pe)], inc=(s_dv, 1))
        P.W("dve", s_dv)
        P.E("dve", "nop", inc=(sem_done, 1))



def emit_ffn1(P, st, wd, sem_y, sem_done):
    nc = P.nc
    ec = P.stack.enter_context
    s_ld = P.sem("f1_ld")
    s_cl = P.sem("f1_cl")
    s_cp = P.sem("f1_cp")
    s_pe = P.sem("f1_pe")
    s_av = P.sem("f1_av")
    s_dv = P.sem("f1_dv")
    with ExitStack() as ph:
        rw_sb = ph.enter_context(P.sbt("rw_sb", [128, 16 * 32], BF16))
        at_sb = ph.enter_context(P.sbt("at_sb", [128, 16 * 64], BF16))
        bt_sb = ph.enter_context(P.sbt("bt_sb", [64, 2048], BF16))
        r64_sb = ph.enter_context(P.sbt("r64_sb", [32, 64], F32))
        bia_sb = ph.enter_context(P.sbt("bia_sb", [64, 1], F32))
        gl_sb = ph.enter_context(P.sbt("gl_sb", [64, 1024], F32))
        gt_sb = ph.enter_context(P.sbt("gt_sb", [128, 8 * 64], F32))
        tk_sb = ph.enter_context(P.sbt("tk_sb", [128, 12], F32))
        gat_sb = ph.enter_context(P.sbt("gat_sb", [64, 1024], BF16))
        mid_sb = ph.enter_context(P.sbt("mid_sb", [64, 1024], BF16))
        bw_sb = [ph.enter_context(P.sbt(f"bw_sb{b}", [128, 16 * 128], BF16))
                 for b in range(2)]
        with ExitStack() as ph2:
            rwf_sb = ph2.enter_context(P.sbt("rwf_sb", [128, 512], F32))
            atf_sb = ph2.enter_context(P.sbt("atf_sb", [128, 1024], F32))
            t_f = P.E("sp", "dma_start", out=rwf_sb[:], in_=wd["rw_pc"][:, :],
                      inc=(s_ld, 16))
            P.E("pool", "tensor_copy", rw_sb[:], rwf_sb[:], wait=[(s_ld, t_f)],
                inc=(s_cp, 1))
            t_f = P.E("sp", "dma_start", out=atf_sb[:], in_=wd["at_pc"][:, :],
                      inc=(s_ld, 16))
            P.E("pool", "tensor_copy", at_sb[:], atf_sb[:], wait=[(s_ld, t_f)],
                inc=(s_cp, 1))
        P.E("pool", "dma_start", out=bt_sb[:], in_=wd["BT1"][:, :], inc=(s_cl, 16))
        P.E("sp", "dma_start", out=r64_sb[:], in_=wd["r64"][:], inc=(s_ld, 16))
        P.E("sp", "dma_start", out=bia_sb[:], in_=wd["bias1"][:], inc=(s_ld, 16))
        t_cl = s_cl.n
        t_cp = s_cp.n
        t_ld = s_ld.n
        for u in range(2):
            pp = st.ps[u]
            t_pe = None
            for i in range(NCH):
                t_pe = P.E("pe", "matmul", pp[0:32, 0:512],
                           rw_sb[:, i * 32:(i + 1) * 32],
                           yown(st, i, u * 512, 512),
                           start=(i == 0), stop=(i == NCH - 1),
                           wait=[(s_cl, t_cl), (s_cp, t_cp),
                                 (sem_y, sem_y.n)] if i == 0 else None,
                           inc=(s_pe, 1) if i == NCH - 1 else None)
            P.E("act", "activation", gl_sb[0:32, u * 512:(u + 1) * 512],
                pp[0:32, 0:512],
                AF.Sigmoid, wait=[(s_pe, t_pe)], inc=(s_av, 1))
        for u in range(2):
            pp = st.ps[2 + u]
            t_pe = P.E("pe", "matmul", pp[0:64, 0:512], r64_sb[:],
                       gl_sb[0:32, u * 512:(u + 1) * 512], start=True, stop=True,
                       wait=[(s_av, s_av.n), (st.ld, st.ld.n), (s_ld, t_ld)],
                       inc=(s_pe, 1))
            P.E("dve", "tensor_scalar", gl_sb[:, u * 512:(u + 1) * 512],
                pp[0:64, 0:512], bia_sb[:, 0:1], None, ALU.add,
                wait=[(s_pe, t_pe), (s_ld, t_ld)], inc=(s_dv, 1))
        for tt in range(8):
            pb = st.psS if tt % 2 == 0 else st.ps[4]
            col = ((tt // 2) % 2) * 64
            t_pe = P.E("pe", "transpose", pb[:, col:col + 64],
                       gl_sb[:, tt * 128:(tt + 1) * 128], st.ident_f[0:64, 0:64],
                       wait=[(s_av, s_av.n), (s_dv, s_dv.n)] if tt == 0 else
                       ([(s_dv, s_dv.n - 1)] if tt >= 2 else None), inc=(s_pe, 1))
            P.E("dve", "tensor_copy", gt_sb[:, tt * 64:(tt + 1) * 64],
                pb[:, col:col + 64],
                wait=[(s_pe, t_pe)], inc=(s_dv, 1))
        for tt in range(8):
            g_ap = gt_sb[:, tt * 64:(tt + 1) * 64]
            t_d = P.E("dve", "max", tk_sb[:, 0:8], g_ap, wait=[(s_dv, s_dv.n)],
                      inc=(s_dv, 1))
            t_d = P.E("dve", "scalar_tensor_tensor", g_ap, g_ap, tk_sb[:, 7:8],
                      g_ap, ALU.is_ge, ALU.mult, wait=[(s_dv, t_d)], inc=(s_dv, 1))
            t_d = P.E("dve", "reduce_sum", tk_sb[:, 8:9], g_ap, axis=AX,
                      wait=[(s_dv, t_d)], inc=(s_dv, 1))
            t_d = P.E("dve", "reciprocal", tk_sb[:, 9:10], tk_sb[:, 8:9],
                      wait=[(s_dv, t_d)], inc=(s_dv, 1))
            t_d = P.E("dve", "tensor_scalar", g_ap, g_ap, tk_sb[:, 9:10], 16.0,
                      ALU.mult, ALU.mult, wait=[(s_dv, t_d)], inc=(s_dv, 1))
            pb2 = st.psS if tt % 2 == 0 else st.ps[4]
            t_pe = P.E("pe", "transpose", pb2[0:64, 256:384], g_ap, st.ident_f[:],
                       wait=[(s_dv, t_d), (s_av, max(0, s_av.n - 1))], inc=(s_pe, 1))
            P.E("act", "activation", gat_sb[:, tt * 128:(tt + 1) * 128],
                pb2[0:64, 256:384], AF.Copy,
                wait=[(s_pe, t_pe)], inc=(s_av, 1))
        for u in range(2):
            pp = st.ps[u]
            t_pe = None
            for i in range(NCH):
                t_pe = P.E("pe", "matmul", pp[0:64, 0:512],
                           at_sb[:, i * 64:(i + 1) * 64],
                           yown(st, i, u * 512, 512),
                           start=(i == 0), stop=(i == NCH - 1),
                           wait=[(s_av, s_av.n), (s_cp, t_cp)] if i == 0 else None,
                           inc=(s_pe, 1) if i == NCH - 1 else None)
            P.E("act", "activation", mid_sb[:, u * 512:(u + 1) * 512], pp[0:64, 0:512],
                AF.Copy, wait=[(s_pe, t_pe)], inc=(s_av, 1))
        t_d = P.E("dve", "tensor_mul", mid_sb[:], mid_sb[:], gat_sb[:],
                  wait=[(s_av, s_av.n)], inc=(s_dv, 1))
        for nchk in range(NCH):
            b = nchk % 2
            t_w = P.E("pool", "dma_start", out=bw_sb[b][:],
                      in_=wd["bW_l"][nchk, :, :],
                      wait=[(s_pe, max(0, s_pe.n - 1))] if nchk >= 2 else None,
                      inc=(s_cl, 16))
            for tt2 in range(2):
                pp = st.ps[2 + (nchk * 2 + tt2) % 2]
                for i in range(NCH):
                    P.E("pe", "matmul", pp[:],
                        bw_sb[b][:, i * 128:(i + 1) * 128],
                        yown(st, i, tt2 * 512, 512),
                        start=(i == 0), stop=False,
                        wait=[(s_cl, t_w), (s_dv, t_d),
                              (s_dv, max(0, s_dv.n - 1))] if i == 0 else None)
                t_pe = P.E("pe", "matmul", pp[:],
                           bt_sb[:, nchk * 128:(nchk + 1) * 128],
                           mid_sb[:, tt2 * 512:(tt2 + 1) * 512],
                           start=False, stop=True, inc=(s_pe, 1))
                P.E("dve", "scalar_tensor_tensor", xch(st, nchk, tt2 * 512, 512),
                    pp[:], 0.5, xch(st, nchk, tt2 * 512, 512), ALU.mult, ALU.add,
                    wait=[(s_pe, t_pe)], inc=(s_dv, 1))
        P.W("dve", s_dv)
        P.E("dve", "nop", inc=(sem_done, 1))



def emit_attn1(P, st, wd, sem_y, sem_done):
    nc = P.nc
    _ph = ExitStack()
    ec = _ph.enter_context
    s_ld = P.sem("a1_ld")
    s_cl = P.sem("a1_cl")
    s_cp = P.sem("a1_cp")
    s_pe = P.sem("a1_pe")
    s_av = P.sem("a1_av")
    s_dv = P.sem("a1_dv")
    s_cc = P.sem("a1_cc")
    ar_in = nc.dram_tensor("kv_in", [64, 16 * 129], F32)
    ar_out = nc.dram_tensor("kv_out", [64, 16 * 129], F32)

    qm_sb = ec(P.sbt("qm_sb", [64, 16 * 1024], BF16))
    ctx1_sb = ec(P.sbt("ctx1_sb", [128, 16 * 1024], BF16))
    stg_sb = ec(P.sbt("stg_sb", [64, 16 * 129], F32))
    kvb_sb = ec(P.sbt("kvb_sb", [64, 16 * 130], BF16))
    rn_sb = ec(P.sbt("rn_sb", [1, 2048], F32))
    rnb_sb = ec(P.sbt("rnb_sb", [128, 1024], F32))
    with ExitStack() as ph:
        wk_sb = ph.enter_context(P.sbt("wk1_sb", [128, 16 * 128], BF16))
        wmap_sb = ph.enter_context(P.sbt("wmap_sb", [128, 2 * 64], BF16))
        bia_sb = ph.enter_context(P.sbt("bia1_sb", [128, 16 * 3], F32))
        kf_sb = ph.enter_context(P.sbt("kf_sb", [128, 1024], BF16))
        vt1_sb = ph.enter_context(P.sbt("vt1_sb", [128, 8 * 128], BF16))
        km_sb = ph.enter_context(P.sbt("km1_sb", [128, 8 * 64], BF16))
        el_sb = ph.enter_context(P.sbt("el_sb", [128, 6 * 512], F32))
        onesb = ph.enter_context(P.sbt("onesb", [128, 1], BF16))

        wmf_sb = ph.enter_context(P.sbt("wmf_sb", [128, 128], F32))
        t_f = P.E("sp", "dma_start", out=wmf_sb[:, 0:64], in_=wd["wmqT"][:, :],
                  inc=(s_ld, 16))
        t_f = P.E("sp", "dma_start", out=wmf_sb[:, 64:128], in_=wd["wmkT"][:, :],
                  inc=(s_ld, 16))
        P.E("pool", "tensor_copy", wmap_sb[:], wmf_sb[:], wait=[(s_ld, t_f)],
            inc=(s_cp, 1))
        P.E("sp", "dma_start", out=bia_sb[:, 0:48], in_=wd["bqkv_pc"][:, :],
            inc=(s_ld, 16))
        P.E("act", "activation", onesb[:], cb(st, C_ONE), AF.Copy,
            wait=[(st.ld, st.ld.n)], inc=(s_av, 1))

        def proj_head(h, wname, bcol, dst, dst_off, scale_sb=None):
            t_w = P.E("pool", "dma_start", out=wk_sb[:],
                      in_=wd[wname][h, :, :],
                      wait=[(s_pe, s_pe.n)], inc=(s_cl, 16))
            for u in range(2):
                pp = st.ps[u]
                t_pe = None
                for i in range(NCH):
                    t_pe = P.E("pe", "matmul", pp[:],
                               wk_sb[:, i * 128:(i + 1) * 128],
                               yown(st, i, u * 512, 512),
                               start=(i == 0), stop=(i == NCH - 1),
                               wait=[(s_cl, t_w), (sem_y, sem_y.n),
                                     (s_av, max(0, s_av.n - 1)),
                                     (s_dv, max(0, s_dv.n - 1))] if i == 0 else None,
                               inc=(s_pe, 1) if i == NCH - 1 else None)
                P.E("dve", "tensor_scalar",
                    dst[:, dst_off + u * 512: dst_off + u * 512 + 512],
                    pp[:], bia_sb[:, h * 3 + bcol: h * 3 + bcol + 1], None,
                    ALU.add,
                    wait=[(s_pe, t_pe), (s_ld, s_ld.n)], inc=(s_dv, 1))

        def elu1(src_ap, dst_ap, tmpo, first_wait=None, w=512, p=128):
            a1 = el_sb[0:p, tmpo:tmpo + w]
            a2 = el_sb[0:p, tmpo + 512:tmpo + 512 + w]
            P.E("act", "activation", a1, src_ap, AF.Relu, scale=-1.0, wait=first_wait)
            P.E("act", "activation", a1, a1, AF.Exp, scale=-1.0)
            t_a = P.E("act", "activation", a2, src_ap, AF.Relu, inc=(s_av, 1))
            return P.E("dve", "tensor_add", dst_ap, a1, a2,
                       wait=[(s_av, t_a)], inc=(s_dv, 1))

        for h in range(H):
            proj_head(h, "wk_l1", 1, kf_sb, 0)
            t_kf = s_dv.n
            for ts_ in range(8):
                pbk = st.psS if ts_ % 2 == 0 else st.ps[4]
                t_pe = P.E("pe", "matmul", pbk[:, 0:64],
                           kf_sb[:, ts_ * 128:(ts_ + 1) * 128], wmap_sb[:, 64:128],
                           start=True, stop=True,
                           wait=[(s_dv, t_kf), (s_cp, s_cp.n)] if ts_ == 0 else
                           ([(s_dv, s_dv.n - 1)] if ts_ >= 2 else None),
                           inc=(s_pe, 1))
                elu1(pbk[:, 0:64],
                     km_sb[:, ts_ * 64:(ts_ + 1) * 64], (ts_ % 2) * 1024,
                     first_wait=[(s_pe, t_pe)] if ts_ < 2 else
                     [(s_pe, t_pe), (s_dv, s_dv.n - 1)], w=64)
            proj_head(h, "wv_l1", 2, kf_sb, 0)
            for ts_ in range(8):
                pb = st.psT[ts_ % 2]
                col = ((ts_ // 2) % 4) * 128
                t_pe = P.E("pe", "transpose", pb[:, col:col + 128],
                           kf_sb[:, ts_ * 128:(ts_ + 1) * 128], st.ident_b[:],
                           wait=[(s_dv, s_dv.n)] if ts_ == 0 else
                           ([(s_dv, s_dv.n - 1)] if ts_ >= 2 else None), inc=(s_pe, 1))
                P.E("dve", "tensor_copy", vt1_sb[:, ts_ * 128:(ts_ + 1) * 128],
                    pb[:, col:col + 128],
                    wait=[(s_pe, t_pe)], inc=(s_dv, 1))
            t_pe = None
            for ts_ in range(8):
                t_pe = P.E("pe", "matmul", st.ps[2][0:64, 0:128],
                           km_sb[:, ts_ * 64:(ts_ + 1) * 64],
                           vt1_sb[:, ts_ * 128:(ts_ + 1) * 128],
                           start=(ts_ == 0), stop=(ts_ == 7),
                           wait=[(s_dv, s_dv.n), (s_av, s_av.n)] if ts_ == 0 else None,
                           inc=(s_pe, 1) if ts_ == 7 else None)
            for ts_ in range(8):
                t_pe2 = P.E("pe", "matmul", st.ps[3][0:64, 0:1],
                            km_sb[:, ts_ * 64:(ts_ + 1) * 64], onesb[:],
                            start=(ts_ == 0), stop=(ts_ == 7),
                            inc=(s_pe, 1) if ts_ == 7 else None)
            P.E("act", "activation", stg_sb[:, h * 129: h * 129 + 128],
                st.ps[2][0:64, 0:128], AF.Copy, wait=[(s_pe, s_pe.n)], inc=(s_av, 1))
            P.E("act", "activation", stg_sb[:, h * 129 + 128: h * 129 + 129],
                st.ps[3][0:64, 0:1], AF.Copy, inc=(s_av, 1))
        P.E("pool", "dma_start", out=ar_in[:, :], in_=stg_sb[:],
            wait=[(s_av, s_av.n)], inc=(s_cl, 16))
        P.W("pool", s_cl)
        P.E("pool", "collective_compute", "AllReduce", ALU.add,
            replica_groups=[list(range(NCORES))],
            ins=[ar_in.ap().opt()], outs=[ar_out.ap().opt()], inc=(s_cc, 1))
        t_bk = P.E("pool", "dma_start", out=stg_sb[:], in_=ar_out[:, :],
                   wait=[(s_cc, 1)], inc=(s_cl, 16))
        t_kv = P.E("act", "activation", kvb_sb[:, 0:16 * 129], stg_sb[:, 0:16 * 129],
                   AF.Copy, wait=[(s_cl, t_bk)], inc=(s_av, 1))
        for h in range(H):
            proj_head(h, "wq_l1", 0, kf_sb, 0)
            t_kf = s_dv.n
            t_qm = None
            for u in range(2):
                pp = st.ps[u]
                t_pe = P.E("pe", "matmul", pp[0:64, 0:512],
                           wmap_sb[:, 0:64],
                           kf_sb[:, u * 512:(u + 1) * 512],
                           start=True, stop=True,
                           wait=[(s_dv, t_kf), (s_dv, max(0, s_dv.n - 1))],
                           inc=(s_pe, 1))
                t_qm = elu1(pp[0:64, 0:512],
                            qm_sb[:, h * 1024 + u * 512: h * 1024 + u * 512 + 512],
                            1024 + u * 1024,
                            first_wait=[(s_pe, t_pe), (s_dv, s_dv.n)], p=64)
            t_nr = None
            for u in range(2):
                t_pe = P.E("pe", "matmul", st.ps[2 + u][0:1, 0:512],
                           kvb_sb[:, h * 129 + 128: h * 129 + 129],
                           qm_sb[:, h * 1024 + u * 512: h * 1024 + u * 512 + 512],
                           start=True, stop=True,
                           wait=[(s_av, t_kv), (s_dv, t_qm),
                                 (s_dv, max(0, s_dv.n - 1))], inc=(s_pe, 1))
                t_nr = P.E("dve", "tensor_scalar", rn_sb[0:1, u * 512:(u + 1) * 512],
                           st.ps[2 + u][0:1, 0:512],
                           st.consts[0:1, C_EPS6:C_EPS6 + 1], None, ALU.add,
                           wait=[(s_pe, t_pe)], inc=(s_dv, 1))
            t_rn = P.E("dve", "reciprocal", rn_sb[0:1, 1024:2048],
                       rn_sb[0:1, 0:1024],
                       wait=[(s_dv, t_nr)], inc=(s_dv, 1))
            for u in range(2):
                t_pe = P.E("pe", "matmul", st.ps[2 + u][:, 0:512], st.ones_row[:],
                           rn_sb[0:1, 1024 + u * 512:1024 + (u + 1) * 512],
                           start=True, stop=True,
                           wait=[(s_dv, t_rn)], inc=(s_pe, 1))
                P.E("dve", "tensor_copy", rnb_sb[:, u * 512:(u + 1) * 512],
                    st.ps[2 + u][:, 0:512], wait=[(s_pe, t_pe)], inc=(s_dv, 1))
            for u in range(2):
                t_pe = P.E("pe", "matmul", st.ps[u][:, 0:512],
                           kvb_sb[:, h * 129: h * 129 + 128],
                           qm_sb[:, h * 1024 + u * 512: h * 1024 + u * 512 + 512],
                           start=True, stop=True,
                           wait=[(s_av, t_kv), (s_dv, s_dv.n)], inc=(s_pe, 1))
                P.E("dve", "tensor_mul",
                    ctx1_sb[:, h * 1024 + u * 512: h * 1024 + u * 512 + 512],
                    st.ps[u][:, 0:512], rnb_sb[:, u * 512:(u + 1) * 512],
                    wait=[(s_pe, t_pe)], inc=(s_dv, 1))
        P.barrier()

    with ExitStack() as ph:
        wo_sb = [ph.enter_context(P.sbt(f"wo1_sb{b}", [128, 16 * 128], BF16))
                 for b in range(2)]
        bo_sb = ph.enter_context(P.sbt("bo1_sb", [128, 16], F32))
        P.E("sp", "dma_start", out=bo_sb[:], in_=wd["bo_pc"][:, :], inc=(s_ld, 16))
        t_bo = s_ld.n
        for do in range(16):
            b = do % 2
            t_w = P.E("pool", "dma_start", out=wo_sb[b][:],
                      in_=wd["wo_l1"][do, :, :],
                      wait=[(s_pe, max(0, s_pe.n - 1))] if do >= 2 else None,
                      inc=(s_cl, 16))
            for tt2 in range(2):
                pp = st.ps[(do * 2 + tt2) % 4]
                t_pe = None
                for i in range(NCH):
                    t_pe = P.E("pe", "matmul", pp[:],
                               wo_sb[b][:, i * 128:(i + 1) * 128],
                               ctx1_sb[:, i * 1024 + tt2 * 512: i * 1024 + tt2 * 512 + 512],
                               start=(i == 0), stop=(i == NCH - 1),
                               wait=[(s_cl, t_w), (s_dv, max(0, s_dv.n - 3))] if i == 0 else None,
                               inc=(s_pe, 1) if i == NCH - 1 else None)
                P.E("dve", "scalar_tensor_tensor", xch(st, do, tt2 * 512, 512),
                    pp[:], bo_sb[:, do:do + 1], xch(st, do, tt2 * 512, 512),
                    ALU.add, ALU.add,
                    wait=[(s_pe, t_pe), (s_ld, t_bo)], inc=(s_dv, 1))
    P.W("dve", s_dv)
    P.E("dve", "nop", inc=(sem_done, 1))
    _ph.close()



def _din(nc, name, shape, dtype):
    return nc.dram_tensor(name, list(shape), dtype, kind="ExternalInput").ap()


def build_prog(kind):
    scratch = 8192 if kind == "ffn0" else 16384
    nc = bass.Bass("TRN2", target_bir_lowering=False, debug=False,
                   dynamic_dma_scratch_size=scratch, detect_race_conditions=False)
    stack = ExitStack()
    with stack:
        P = Prog(nc, stack)
        x_in = _din(nc, "x_in", [TPC, 2048], F32)
        x_out = nc.dram_tensor("x_out", [TPC, 2048], F32, kind="ExternalOutput").ap()
        wd = {}
        if kind in ("ffn0", "mega"):
            wd["wg_pt"] = _din(nc, "wg_pt", [1024, NFF], U8)
            wd["wu_pt"] = _din(nc, "wu_pt", [1024, NFF], U8)
            wd["wd_pt"] = _din(nc, "wd_pt", [8192, 2048], U8)
            wd["sc_g"] = _din(nc, "sc_g", [64, NFF], F32)
            wd["sc_u"] = _din(nc, "sc_u", [64, NFF], F32)
            wd["sc_d"] = _din(nc, "sc_d", [512, 2048], F32)
        if kind in ("attn0", "mega"):
            for nm in ("wq_l0", "wk_l0", "wv_l0", "wo_l0", "wkm_l"):
                wd[nm] = _din(nc, nm, [16, 128, 2048], F32)
            wd["wvm_l"] = _din(nc, "wvm_l", [4, 128, 8192], F32)
            wd["gw_pc"] = _din(nc, "gw_pc", [128, 16], F32)
            wd["gb0"] = _din(nc, "gb0", [1, 1], F32)
            wd["memT_sl"] = _din(nc, "memT_sl", [2048, MSL], F32)
            if kind == "attn0":
                halo_in = _din(nc, "halo_in", [HALO, 2048], F32)
            else:
                sel_d = _din(nc, "sel_pc", [128, 8], F32)
        if kind in ("ffn1", "mega"):
            wd["bW_l"] = _din(nc, "bW_l", [16, 128, 2048], F32)
            wd["rw_pc"] = _din(nc, "rw_pc", [128, 512], F32)
            wd["at_pc"] = _din(nc, "at_pc", [128, 1024], F32)
            wd["BT1"] = _din(nc, "BT1", [64, 2048], F32)
            wd["bias1"] = _din(nc, "bias1", [64, 1], F32)
            wd["r64"] = _din(nc, "r64", [32, 64], F32)
        if kind in ("attn1", "mega"):
            for nm in ("wq_l1", "wk_l1", "wv_l1", "wo_l1"):
                wd[nm] = _din(nc, nm, [16, 128, 2048], F32)
            wd["bqkv_pc"] = _din(nc, "bqkv_pc", [128, 48], F32)
            wd["bo_pc"] = _din(nc, "bo_pc", [128, 16], F32)
            wd["wmqT"] = _din(nc, "wmqT", [128, 64], F32)
            wd["wmkT"] = _din(nc, "wmkT", [128, 64], F32)

        is_attn = kind in ("attn0", "attn1", "mega")
        st = setup_common(P, nc, text=TPC, want_identb=is_attn,
                          nps=5 if is_attn else 6, npsT=2 if is_attn else 0)
        s_x = P.sem("sx")
        s_o = P.sem("so")

        def attn0_block(sem_after_load, halo_src_fn):
            ph_byp = ExitStack()
            byp_sb = ph_byp.enter_context(P.sbt("byp_sb", [128, 8], F32))
            ph_yh = ExitStack()
            yh_sb = ph_yh.enter_context(
                P.sbt("yh_sb", [128, NCH * HALO], BF16))
            ph_h = ExitStack()
            halo_sb = ph_h.enter_context(
                P.sbt("halo_sb", [128, NCH * HALO], F32))
            s_h = P.sem("sh")
            halo_src_fn(halo_sb, s_h)
            P.barrier()
            s_y = P.sem("sy_a0")
            emit_ln(P, st, 0, TPC, sem_after_load, s_y, tag="a0m")
            P.barrier()
            emit_ln(P, st, 0, HALO, s_h, s_y,
                    x_src=lambda i, f0, w: halo_sb[:, i * HALO + f0: i * HALO + f0 + w],
                    dst_fn=lambda i, f0, w: yh_sb[:, i * HALO + f0: i * HALO + f0 + w],
                    tag="a0h")
            P.barrier()
            ph_h.close()
            emit_attn0(P, st, wd, s_y, P.sem("sf_a0"), stop_after=A0_STOP,
                       yh=lambda i, f0, w: yh_sb[:, i * HALO + f0: i * HALO + f0 + w],
                       yh_stack=ph_yh, byp_sb=byp_sb)
            ph_byp.close()
            P.barrier()

        if kind == "attn0":
            emit_load_x(P, st, x_in, s_x)
            P.barrier()

            def fill_halo(halo_sb, s_h):
                emit_load_x(P, st, halo_in, s_h, into_fn=lambda i, t0:
                            halo_sb[:, i * HALO + t0: i * HALO + t0 + 128])
            attn0_block(s_x, fill_halo)
        elif kind == "ffn0":
            emit_load_x(P, st, x_in, s_x)
            P.barrier()
            s_y = P.sem("sy")
            emit_ln(P, st, 0, TPC, s_x, s_y, tag="m")
            P.barrier()
            emit_ffn0(P, st, wd, s_y, P.sem("sf"))
            P.barrier()
        elif kind == "ffn1":
            emit_load_x(P, st, x_in, s_x)
            P.barrier()
            s_y = P.sem("sy")
            emit_ln(P, st, 0, TPC, s_x, s_y, tag="m")
            P.barrier()
            emit_ffn1(P, st, wd, s_y, P.sem("sf"))
            P.barrier()
        elif kind == "attn1":
            emit_load_x(P, st, x_in, s_x)
            P.barrier()
            s_y = P.sem("sy")
            emit_ln(P, st, 0, TPC, s_x, s_y, tag="m")
            P.barrier()
            emit_attn1(P, st, wd, s_y, P.sem("sf"))
            P.barrier()
        else:
            stop = MEGA_STOP or 6
            emit_load_x(P, st, x_in, s_x)
            P.barrier()
            s_y = P.sem("sy1")
            emit_ln(P, st, 0, TPC, s_x, s_y, tag="l1")
            P.barrier()
            sf1 = P.sem("sf1")
            emit_ffn0(P, st, wd, s_y, sf1)
            P.barrier()

            if stop >= 2:
                def fill_halo(halo_sb, s_h):
                    emit_halo_gather(P, st, sel_d, halo_sb, sf1, s_h)
                attn0_block(sf1, fill_halo)

            if stop >= 3:
                s_y3 = P.sem("sy3")
                emit_ln(P, st, 0, TPC, P.sem("dummy3"), s_y3, tag="l3")
                P.barrier()
                emit_ffn0(P, st, wd, s_y3, P.sem("sf3"))
                P.barrier()

            if stop >= 4:
                s_y4 = P.sem("sy4")
                emit_ln(P, st, 0, TPC, P.sem("dummy4"), s_y4, tag="l4")
                P.barrier()
                emit_ffn1(P, st, wd, s_y4, P.sem("sf4"))
                P.barrier()

            if stop >= 5:
                s_y5 = P.sem("sy5")
                emit_ln(P, st, 0, TPC, P.sem("dummy5"), s_y5, tag="l5")
                P.barrier()
                emit_attn1(P, st, wd, s_y5, P.sem("sf5"))
                P.barrier()

            if stop >= 6:
                s_y6 = P.sem("sy6")
                emit_ln(P, st, 0, TPC, P.sem("dummy6"), s_y6, tag="l6")
                P.barrier()
                emit_ffn1(P, st, wd, s_y6, P.sem("sf6"))
                P.barrier()

        emit_store_x(P, st, x_out, s_o)
        P.finalize()
    return nc


A0_STOP = None
MEGA_STOP = None
_PROGS = {}


def _get_prog(kind):
    if kind not in _PROGS:
        _PROGS[kind] = build_prog(kind)
    return _PROGS[kind]


MEGA = True


def kernel(**inputs):
    wd = host_prepare(inputs)
    cores = list(range(NCORES))
    common = {k: wd[k] for k in ("consts", "ones_row", "ident_f", "ident_b")}

    if MEGA:
        nc = _get_prog("mega")
        shared = {}
        for k in ("wg_pt", "wu_pt", "wd_pt", "sc_g", "sc_u", "sc_d",
                  "wq_l0", "wk_l0", "wv_l0", "wo_l0", "wkm_l", "wvm_l",
                  "gw_pc", "gb0",
                  "bW_l", "rw_pc", "at_pc", "BT1", "bias1", "r64",
                  "wq_l1", "wk_l1", "wv_l1", "wo_l1", "bqkv_pc", "bo_pc",
                  "wmqT", "wmkT"):
            shared[k] = wd[k]
        maps = []
        for c in cores:
            m = dict(common)
            m.update(shared)
            m["x_in"] = wd["x_sl"][c]
            m["memT_sl"] = wd["memT_sl"][c]
            m["masks"] = wd["masks"][c]
            m["sel_pc"] = wd["sel_pc"][c]
            maps.append(m)
        res = run_bass_kernel_spmd(nc, maps, core_ids=cores)
        x = [np.asarray(res.results[c]["x_out"]) for c in cores]
        return np.concatenate(x, axis=0).reshape(1, 8192, 2048).astype(np.float32)

    def launch(kind, x_cur, extra_per_core=None, shared=None):

        nc = _get_prog(kind)
        maps = []
        for c in cores:
            m = dict(common)
            m["x_in"] = x_cur[c]
            if shared:
                m.update(shared)
            if extra_per_core:
                m.update(extra_per_core[c])
            maps.append(m)
        res = run_bass_kernel_spmd(nc, maps, core_ids=cores)
        return [np.asarray(res.results[c]["x_out"]) for c in cores]

    sh_ffn0 = {k: wd[k] for k in ("wg_pt", "wu_pt", "wd_pt", "sc_g", "sc_u", "sc_d")}
    sh_attn0 = {k: wd[k] for k in ("wq_l0", "wk_l0", "wv_l0", "wo_l0", "wkm_l",
                                   "wvm_l", "gw_pc", "gb0")}
    sh_ffn1 = {k: wd[k] for k in ("bW_l", "rw_pc", "at_pc", "BT1", "bias1", "r64")}
    sh_attn1 = {k: wd[k] for k in ("wq_l1", "wk_l1", "wv_l1", "wo_l1",
                                   "bqkv_pc", "bo_pc", "wmqT", "wmkT")}

    x = wd["x_sl"]
    x = launch("ffn0", x, shared=sh_ffn0)
    halo = [np.zeros((HALO, 2048), np.float32) if c == 0 else
            np.ascontiguousarray(x[c - 1][-HALO:]) for c in cores]
    extra = [{"halo_in": halo[c], "memT_sl": wd["memT_sl"][c],
              "masks": wd["masks"][c]} for c in cores]
    x = launch("attn0", x, extra_per_core=extra, shared=sh_attn0)
    x = launch("ffn0", x, shared=sh_ffn0)
    x = launch("ffn1", x, shared=sh_ffn1)
    x = launch("attn1", x, shared=sh_attn1)
    x = launch("ffn1", x, shared=sh_ffn1)
    out = np.concatenate(x, axis=0).reshape(1, 8192, 2048).astype(np.float32)
    return out


# version 03
# speedup vs baseline: 1.6615x; 1.4016x over previous
import numpy as np
import ml_dtypes
import concourse.bass as bass
import concourse.mybir as mybir
from concourse.bass_utils import run_bass_kernel_spmd
from contextlib import ExitStack

F32 = mybir.dt.float32
BF16 = mybir.dt.bfloat16
U8 = mybir.dt.uint8
AF = mybir.ActivationFunctionType
ALU = mybir.AluOpType
AX = mybir.AxisListType.X

D = 2048
NCH = 16
TPC = 1024
HALO = 256
TEXT = TPC + HALO
NCORES = 8
M_MEM = 4096
MSL = M_MEM // NCORES
NFF = 16384
NSC = 32
SCW = NFF // NSC
H = 16
HD = 128
INV_SQRT_D = float(1.0 / np.sqrt(2048.0))
INV_SQRT_HD = float(1.0 / np.sqrt(128.0))

C_M4, C_M6, C_75, C_EPS5, C_EPS6, C_ONE, C_ZERO, C_SPARE = range(8)


def _pi(n):
    idx = np.arange(n)
    i, p = idx // 128, idx % 128
    return 256 * (i // 2) + 2 * p + (i % 2)


PI_D = _pi(2048)
PI_N = _pi(16384)


class S:
    def __init__(self, stack, nc, name):
        self.h = stack.enter_context(nc.semaphore(name))
        self.n = 0


class Prog:
    ENGS = ("pe", "act", "dve", "pool", "sp")

    def __init__(self, nc, stack):
        self.nc = nc
        self.stack = stack
        self.q = {e: [] for e in self.ENGS}
        self.bar = S(stack, nc, "bar")
        self.nbar = 0

    def sem(self, name):
        self._uid = getattr(self, "_uid", 0) + 1
        return S(self.stack, self.nc, f"{name}_{self._uid}")

    def sbt(self, name, shape, dtype):
        self._uid = getattr(self, "_uid", 0) + 1
        return self.nc.sbuf_tensor(f"{name}_{self._uid}", shape, dtype)

    def E(self, eng, method, *args, wait=None, inc=None, **kwargs):
        if wait:
            wait = [(s.h, t) for (s, t) in wait if t > 0]

        def f(h, method=method, args=args, kwargs=kwargs, wait=wait, inc=inc):
            if wait:
                for sh, t in wait:
                    h.wait_ge(sh, t)
            inst = getattr(h, method)(*args, **kwargs)
            if inc is not None:
                inst.then_inc(inc[0].h, inc[1])
        self.q[eng].append(f)
        if inc is not None:
            inc[0].n += inc[1]
            return inc[0].n
        return None

    def W(self, eng, sem, tok=None):
        t = sem.n if tok is None else tok
        if t > 0:
            self.q[eng].append(lambda h, sh=sem.h, t=t: h.wait_ge(sh, t))

    def barrier(self):
        self.nbar += 1
        tgt = 5 * self.nbar
        for e in self.ENGS:
            self.q[e].append(lambda h, sh=self.bar.h: h.sem_inc(sh, 1))
            self.q[e].append(lambda h, sh=self.bar.h, t=tgt: h.wait_ge(sh, t))
        self.bar.n = tgt

    def finalize(self):
        with self.nc.Block() as block:
            @block.tensor
            def _(h):
                for f in self.q["pe"]:
                    f(h)

            @block.scalar
            def _(h):
                for f in self.q["act"]:
                    f(h)

            @block.vector
            def _(h):
                for f in self.q["dve"]:
                    f(h)

            @block.gpsimd
            def _(h):
                for f in self.q["pool"]:
                    f(h)

            @block.sync
            def _(h):
                for f in self.q["sp"]:
                    f(h)


class Ctx:
    pass



def host_prepare(inp):
    out = {}
    asc = np.ascontiguousarray

    def bview(a):
        a = np.asarray(a)
        return a.view(np.uint8)[:, ::4]

    out["wg_pt"] = asc(bview(inp["L0_ffn_gate_W_packed"])[PI_N].T)
    out["wu_pt"] = asc(bview(inp["L0_ffn_up_W_packed"])[PI_N].T)
    out["wd_pt"] = asc(bview(inp["L0_ffn_down_W_packed"])[PI_D].T)
    out["sc_g"] = asc(np.asarray(inp["L0_ffn_gate_scales"], np.float32)[PI_N].T)
    out["sc_u"] = asc(np.asarray(inp["L0_ffn_up_scales"], np.float32)[PI_N].T)
    out["sc_d"] = asc(np.asarray(inp["L0_ffn_down_scales"], np.float32)[PI_D].T)

    def f32(name):
        return np.asarray(inp[name], np.float32)

    out["wqT0"] = asc(f32("L0_attn_wq").T[PI_D])
    out["wkT0"] = asc(f32("L0_attn_wk").T[PI_D])
    out["wvT0"] = asc(f32("L0_attn_wv").T[PI_D])
    out["woT0"] = asc(f32("L0_attn_wo").T[:, PI_D])
    out["wkTm"] = asc(f32("L0_attn_wk").T[:, PI_D])
    out["wvTm"] = asc(f32("L0_attn_wv").T)
    out["gwT0"] = asc(f32("L0_attn_gate_w")[0][PI_D].reshape(2048, 1))
    out["gb0"] = f32("L0_attn_gate_b").reshape(1, 1)
    memT = asc(f32("L0_attn_mem_bank").T)
    out["memT_sl"] = [asc(memT[:, c * MSL:(c + 1) * MSL]) for c in range(NCORES)]

    out["wqT1"] = asc(f32("L1_attn_wq").T[PI_D])
    out["wkT1"] = asc(f32("L1_attn_wk").T[PI_D])
    out["wvT1"] = asc(f32("L1_attn_wv").T[PI_D])
    out["woT1"] = asc(f32("L1_attn_wo").T[:, PI_D])
    out["bq1"] = asc(f32("L1_attn_bq").reshape(2048, 1))
    out["bk1"] = asc(f32("L1_attn_bk").reshape(2048, 1))
    out["bv1"] = asc(f32("L1_attn_bv").reshape(2048, 1))
    out["bo1"] = asc(f32("L1_attn_bo")[PI_D].reshape(2048, 1))
    out["wmqT"] = asc(f32("L1_attn_wmap_q").T)
    out["wmkT"] = asc(f32("L1_attn_wmap_k").T)

    out["bWT1"] = asc(f32("L1_ffn_base_W").T[PI_D][:, PI_D])
    out["rWT1"] = asc(f32("L1_ffn_router_W").T[PI_D])
    out["AT1"] = asc(f32("L1_ffn_A").T[PI_D])
    out["BT1"] = asc(f32("L1_ffn_B").T[:, PI_D])
    out["bias1"] = asc(f32("L1_ffn_biases").reshape(64, 1))

    def sliced(wT, ow):
        nO = wT.shape[1] // ow
        return asc(wT.reshape(16, 128, nO, ow).transpose(2, 1, 0, 3)
                   .reshape(nO, 128, 16 * ow))

    out["wq_l0"] = sliced(out["wqT0"], 128)
    out["wk_l0"] = sliced(out["wkT0"], 128)
    out["wv_l0"] = sliced(out["wvT0"], 128)
    out["wo_l0"] = sliced(out["woT0"], 128)
    out["wkm_l"] = sliced(out["wkTm"], 128)
    out["wvm_l"] = sliced(out["wvTm"], 512)
    out["gw_pc"] = asc(out["gwT0"].reshape(16, 128).T)
    out["bW_l"] = sliced(out["bWT1"], 128)
    out["rw_pc"] = asc(out["rWT1"].reshape(16, 128, 32).transpose(1, 0, 2)
                       .reshape(128, 512))
    out["at_pc"] = asc(out["AT1"].reshape(16, 128, 64).transpose(1, 0, 2)
                       .reshape(128, 1024))
    out["wq_l1"] = sliced(out["wqT1"], 128)
    out["wk_l1"] = sliced(out["wkT1"], 128)
    out["wv_l1"] = sliced(out["wvT1"], 128)
    out["wo_l1"] = sliced(out["woT1"], 128)
    bqkv = np.zeros((128, 48), np.float32)
    for j, nm in enumerate(("bq1", "bk1", "bv1")):
        bqkv[:, j::3] = out[nm].reshape(16, 128).T
    out["bqkv_pc"] = asc(bqkv)
    out["bo_pc"] = asc(out["bo1"].reshape(16, 128).T)

    x0 = np.asarray(inp["L0_attn_hidden_states"], np.float32).reshape(8192, 2048)
    out["x_sl"] = [asc(x0[c * TPC:(c + 1) * TPC]) for c in range(NCORES)]

    consts = np.zeros((128, 8), np.float32)
    consts[:, C_M4] = -4.0
    consts[:, C_M6] = -6.0
    consts[:, C_75] = 7.5
    consts[:, C_EPS5] = 1e-5
    consts[:, C_EPS6] = 1e-6
    consts[:, C_ONE] = 1.0
    out["consts"] = consts
    out["ones_row"] = np.ones((1, 128), np.float32)
    out["ident_f"] = np.eye(128, dtype=np.float32)
    out["ident_b"] = np.eye(128).astype(ml_dtypes.bfloat16)
    r64 = np.zeros((32, 64), np.float32)
    r64[np.arange(32), 2 * np.arange(32)] = 1.0
    r64[np.arange(32), 2 * np.arange(32) + 1] = 1.0
    out["r64"] = r64

    NEG = np.float32(-1.0e30)
    masks = np.zeros((NCORES, 8, 128, 384), np.float32)
    a = np.arange(128)[:, None]
    cc = np.arange(128)[None, :]
    tri_up = np.where(cc > a, 0.0, NEG).astype(np.float32)
    tri_lo = np.where(cc <= a, 0.0, NEG).astype(np.float32)
    for core in range(NCORES):
        for qt in range(8):
            m = masks[core, qt]
            m[:, 0:128] = tri_up
            m[:, 128:256] = 0.0
            m[:, 256:384] = tri_lo
            if core == 0 and qt == 0:
                m[:, 0:256] = NEG
            elif core == 0 and qt == 1:
                m[:, 0:128] = NEG
    out["masks"] = masks
    sel = np.zeros((NCORES, 128, 8), np.float32)
    for c in range(1, NCORES):
        sel[c, :, c - 1] = 1.0
    out["sel_pc"] = sel
    return out



def ap2(t, p0, pn, f0, fn, fstep=1):
    rowlen = t.shape[-1]
    return bass.AP(t, p0 * rowlen + f0, [[rowlen, pn], [fstep, fn]])


def cb(st, col):
    return st.consts[:, col:col + 1]


def xch(st, i, f0, fn):
    return st.x[:, i * TPC + f0: i * TPC + f0 + fn]


def ych(st, i, f0, fn):
    t = st.text
    return st.y[:, i * t + f0: i * t + f0 + fn]


def yown(st, i, f0, fn):
    return ych(st, i, (st.text - 1024) + f0, fn)


def setup_common(P, nc, *, text=TPC, want_identb=True, nps=6, npsT=0):
    st = Ctx()
    st.text = text
    st.consts_d = nc.dram_tensor("consts", [128, 8], F32, kind="ExternalInput").ap()
    st.ones_row_d = nc.dram_tensor("ones_row", [1, 128], F32, kind="ExternalInput").ap()
    st.ident_f_d = nc.dram_tensor("ident_f", [128, 128], F32, kind="ExternalInput").ap()
    ec = P.stack.enter_context
    st.consts = ec(P.sbt("consts_sb", [128, 8], F32))
    st.ones_row = ec(P.sbt("ones_row_sb", [1, 128], F32))
    st.ident_f = ec(P.sbt("ident_f_sb", [128, 128], F32))
    if want_identb:
        st.ident_b_d = nc.dram_tensor("ident_b", [128, 128], BF16,
                                      kind="ExternalInput").ap()
        st.ident_b = ec(P.sbt("ident_b_sb", [128, 128], BF16))

    st.ld = P.sem("ld_misc")
    P.E("sp", "dma_start", out=st.consts[:], in_=st.consts_d[:], inc=(st.ld, 16))
    P.E("sp", "dma_start", out=st.ones_row[:], in_=st.ones_row_d[:], inc=(st.ld, 16))
    P.E("sp", "dma_start", out=st.ident_f[:], in_=st.ident_f_d[:], inc=(st.ld, 16))
    if want_identb:
        P.E("sp", "dma_start", out=st.ident_b[:], in_=st.ident_b_d[:], inc=(st.ld, 16))

    st.x = ec(P.sbt("x_sb", [128, NCH * TPC], F32))
    st.y = ec(P.sbt("y_sb", [128, NCH * text], BF16))

    st.ps = [ec(nc.psum_tensor(f"ps{i}", [128, 512], F32)) for i in range(nps)]
    st.psS = ec(nc.psum_tensor("psS", [128, 512], F32))
    st.psT = [ec(nc.psum_tensor(f"psT{i}", [128, 1024], BF16)) for i in range(npsT)]
    return st


def emit_load_x(P, st, x_d, sem_done, into_fn=None):
    nc = P.nc
    _ph = ExitStack()
    xin = _ph.enter_context(P.sbt("xin_tm", [128, 2048], F32))
    ld = P.sem("ld_x")
    pv = P.sem("pe_xload")
    av = P.sem("act_xload")
    if into_fn is None:
        into_fn = lambda i, t0: xch(st, i, t0, 128)
    ntt = x_d.shape[0] // 128
    for tt in range(ntt):
        w = [(st.ld, st.ld.n)]
        if tt > 0:
            w.append((av, av.n))
        t_ld = P.E("sp", "dma_start", out=xin[:], in_=x_d[tt * 128:(tt + 1) * 128, :],
                   wait=w, inc=(ld, 16))
        for i in range(NCH):
            c, s_ = i // 2, i % 2
            src = ap2(xin, 0, 128, 256 * c + s_, 128, fstep=2)
            k = tt * NCH + i
            pbank = st.psS if k % 2 == 0 else st.ps[0]
            col = ((k // 2) % 2) * 128
            pw = [(ld, t_ld), (st.ld, st.ld.n)]
            if k >= 2:
                pw.append((av, av.n - 1))
            P.E("pe", "transpose", pbank[:, col:col + 128], src,
                st.ident_f[:], wait=pw, inc=(pv, 1))
            P.E("act", "copy", into_fn(i, tt * 128), pbank[:, col:col + 128],
                wait=[(pv, pv.n)], inc=(av, 1))
    P.W("act", av)
    P.E("act", "nop", inc=(sem_done, 1))
    _ph.close()


def emit_store_x(P, st, out_d, sem_done):
    nc = P.nc
    _ph = ExitStack()
    xout = _ph.enter_context(P.sbt("xout_tm", [128, 2048], F32))
    pv = P.sem("pe_xst")
    av = P.sem("act_xst")
    sv = P.sem("st_x")
    t_id = P.E("sp", "dma_start", out=st.ident_f[:], in_=st.ident_f_d[:],
               inc=(st.ld, 16))
    P.W("pe", st.ld, t_id)
    for tt in range(8):
        for i in range(NCH):
            c, s_ = i // 2, i % 2
            k = tt * NCH + i
            pbank = st.psS if k % 2 == 0 else st.ps[0]
            col = ((k // 2) % 2) * 128
            pw = []
            if k >= 2:
                pw.append((av, av.n - 1))
            if tt:
                pw.append((sv, tt * 16))
            P.E("pe", "transpose", pbank[:, col:col + 128],
                xch(st, i, tt * 128, 128), st.ident_f[:],
                wait=pw if pw else None, inc=(pv, 1))
            dst = ap2(xout, 0, 128, 256 * c + s_, 128, fstep=2)
            P.E("act", "copy", dst, pbank[:, col:col + 128],
                wait=[(pv, pv.n)], inc=(av, 1))
        P.E("sp", "dma_start", out=out_d[tt * 128:(tt + 1) * 128, :], in_=xout[:],
            wait=[(av, av.n)], inc=(sv, 16))
    P.W("sp", sv)
    P.E("sp", "nop", inc=(sem_done, 1))
    _ph.close()


def emit_ln(P, st, ext_f0, n, sem_ready, sem_done, x_src=None, tag="",
            dst_fn=None):
    nc = P.nc
    if x_src is None:
        x_src = lambda i, f0, w: xch(st, i, f0, w)
    if dst_fn is None:
        dst_fn = lambda i, f0, w: ych(st, i, ext_f0 + f0, w)
    _ph = ExitStack()
    ec = _ph.enter_context
    sq = ec(P.sbt(f"ln_sq{tag}", [128, 1024], F32))
    srow = ec(P.sbt(f"ln_srow{tag}", [1, 1024], F32))
    rr = ec(P.sbt(f"ln_rr{tag}", [1, 1024], F32))
    rb = ec(P.sbt(f"ln_rb{tag}", [128, 1024], F32))
    pv = P.sem(f"ln_pe{tag}")
    av = P.sem(f"ln_act{tag}")
    dv = P.sem(f"ln_dve{tag}")
    wid = min(n, 512)
    for u in range(max(1, n // 512)):
        f0 = u * wid
        for i in range(NCH):
            w = None
            if i == 0:
                w = [(sem_ready, sem_ready.n)]
                if u:
                    w.append((dv, dv.n))
            P.E("pe", "matmul", st.psS[0:1, 0:wid], cb(st, C_ONE),
                x_src(i, f0, wid), start=(i == 0), stop=(i == NCH - 1), wait=w,
                inc=(pv, 1) if i == NCH - 1 else None)
        for i in range(NCH):
            aw = [(sem_ready, sem_ready.n)] if i == 0 else []
            if u and i < 2:
                aw.append((dv, dv.n))
            if i >= 2:
                aw.append((pv, pv.n))
            P.E("act", "activation", sq[:, (i % 2) * 512:(i % 2) * 512 + wid],
                x_src(i, f0, wid), AF.Square, wait=aw if aw else None, inc=(av, 1))
            P.E("pe", "matmul", st.ps[4][0:1, 0:wid], cb(st, C_ONE),
                sq[:, (i % 2) * 512:(i % 2) * 512 + wid],
                start=(i == 0), stop=(i == NCH - 1),
                wait=[(av, av.n)], inc=(pv, 1))
        t_stats = pv.n
        P.E("dve", "tensor_scalar_mul", srow[0:1, 0:wid], st.psS[0:1, 0:wid],
            1.0 / 2048.0, wait=[(pv, t_stats)])
        P.E("dve", "tensor_scalar_mul", srow[0:1, 512:512 + wid], st.ps[4][0:1, 0:wid],
            1.0 / 2048.0)
        P.E("dve", "tensor_mul", rr[0:1, 512:512 + wid], srow[0:1, 0:wid],
            srow[0:1, 0:wid])
        t_d = P.E("dve", "tensor_sub", rr[0:1, 0:wid], srow[0:1, 512:512 + wid],
                  rr[0:1, 512:512 + wid], inc=(dv, 1))
        t_a = P.E("act", "activation", srow[0:1, 512:512 + wid], rr[0:1, 0:wid],
                  AF.Sqrt, bias=st.consts[0:1, C_EPS5:C_EPS5 + 1],
                  wait=[(dv, t_d)], inc=(av, 1))
        P.E("dve", "reciprocal", rr[0:1, 0:wid], srow[0:1, 512:512 + wid],
            wait=[(av, t_a)])
        t_d = P.E("dve", "scalar_tensor_tensor", rr[0:1, 512:512 + wid],
                  srow[0:1, 0:wid], -1.0, rr[0:1, 0:wid], ALU.mult, ALU.mult,
                  inc=(dv, 1))
        P.E("pe", "matmul", st.psS[:, 0:wid], st.ones_row[:], rr[0:1, 0:wid],
            start=True, stop=True, wait=[(dv, t_d)])
        t_pe = P.E("pe", "matmul", st.ps[4][:, 0:wid], st.ones_row[:],
                   rr[0:1, 512:512 + wid], start=True, stop=True, inc=(pv, 1))
        P.E("dve", "tensor_copy", rb[:, 0:wid], st.psS[:, 0:wid], wait=[(pv, t_pe)])
        t_d = P.E("dve", "tensor_copy", rb[:, 512:512 + wid], st.ps[4][:, 0:wid],
                  inc=(dv, 1))
        for i in range(NCH):
            P.E("dve", "tensor_mul", sq[:, (i % 2) * 512:(i % 2) * 512 + wid],
                x_src(i, f0, wid), rb[:, 0:wid],
                wait=[(dv, t_d), (av, av.n), (pv, pv.n)] if i < 2 else None)
            P.E("dve", "tensor_add", dst_fn(i, u * wid, wid),
                sq[:, (i % 2) * 512:(i % 2) * 512 + wid], rb[:, 512:512 + wid],
                inc=(dv, 1))
    P.W("dve", dv)
    P.E("dve", "nop", inc=(sem_done, 1))
    _ph.close()


def emit_halo_gather(P, st, sel_d, halo_sb, sem_x_ready, sem_done):
    nc = P.nc
    ag_in = nc.dram_tensor("hx_in", [2048, HALO], F32)
    ag_out = nc.dram_tensor("hx_out", [8 * 2048, HALO], F32)
    s_st = P.sem("hx_st")
    s_cc = P.sem("hx_cc")
    s_ld = P.sem("hx_ld")
    s_dv = P.sem("hx_dv")
    with ExitStack() as ph:
        stg = ph.enter_context(P.sbt("hx_stg", [128, NCH * HALO], F32))
        slab = [ph.enter_context(P.sbt(f"hx_sl{b}", [128, NCH * HALO], F32))
                for b in range(2)]
        sel_sb = ph.enter_context(P.sbt("hx_sel", [128, 8], F32))
        P.E("sp", "dma_start", out=sel_sb[:], in_=sel_d[:, :], inc=(s_ld, 16))
        t_d = None
        for i in range(NCH):
            t_d = P.E("dve", "tensor_copy", stg[:, i * HALO:(i + 1) * HALO],
                      xch(st, i, TPC - HALO, HALO),
                      wait=[(sem_x_ready, sem_x_ready.n)] if i == 0 else None,
                      inc=(s_dv, 1))
        t_st = P.E("sp", "dma_start",
                   out=ag_in.ap().rearrange("(c p) m -> p c m", p=128),
                   in_=stg[:], wait=[(s_dv, t_d)], inc=(s_st, 16))
        P.W("pool", s_st, t_st)
        P.E("pool", "collective_compute", "AllGather", ALU.bypass,
            replica_groups=[list(range(NCORES))],
            ins=[ag_in.ap().opt()], outs=[ag_out.ap().opt()], inc=(s_cc, 1))
        P.E("dve", "memset", halo_sb[:], 0.0, inc=(s_dv, 1))
        for s in range(8):
            b = s % 2
            t_l = P.E("sp", "dma_start", out=slab[b][:],
                      in_=ag_out.ap()[s * 2048:(s + 1) * 2048, :]
                      .rearrange("(c p) m -> p c m", p=128),
                      wait=[(s_cc, 1)] if s < 2 else [(s_dv, s_dv.n)],
                      inc=(s_ld, 16))
            t_d = P.E("dve", "scalar_tensor_tensor", halo_sb[:],
                      slab[b][:], sel_sb[:, s:s + 1], halo_sb[:],
                      ALU.mult, ALU.add,
                      wait=[(s_ld, t_l), (s_ld, s_ld.n if s == 0 else 0)],
                      inc=(s_dv, 1))
    P.W("dve", s_dv)
    P.E("dve", "nop", inc=(sem_done, 1))



def emit_ffn0(P, st, wd, sem_y, sem_done, scratch=None, mode="decode"):
    nc = P.nc
    _ph = ExitStack()
    ec = _ph.enter_context
    wg_sb = [ec(P.sbt(f"wg{b}", [128, NCH * 512], BF16)) for b in range(2)]
    wu_sb = [ec(P.sbt(f"wu{b}", [128, NCH * 512], BF16)) for b in range(2)]
    wdn_sb = [ec(P.sbt(f"wdn{b}", [128, 4 * 2048], BF16)) for b in range(2)]
    pk_sb = ec(P.sbt("pk_sb", [128, 2 * 512], U8))
    vdec = ec(P.sbt("vdec_sb", [128, 3 * 512], U8))
    scs_sb = ec(P.sbt("scs_sb", [128, 512], F32))
    tmp = ec(P.sbt("dec_tmp", [128, 5 * 512], BF16))
    gel = ec(P.sbt("gelu_sb", [128, 512], BF16))
    inter = ec(P.sbt("inter_sb", [128, 4 * 512], BF16))

    s_ld = P.sem("f0_ld")
    s_dd = P.sem("f0_dd")
    s_da = P.sem("f0_da")
    s_pe = P.sem("f0_pe")
    s_ga = P.sem("f0_ga")
    s_iv = P.sem("f0_iv")
    s_xa = P.sem("f0_xa")

    s_st = P.sem("f0_st")
    st_done = {}
    dec_done = {}
    pe_done = {}
    slot_free_tok = {}
    last_mphase_tok = {}
    last_sgread_tok = {}
    last_relu = [0]
    last_gelu_tok = {}
    last_inter_tok = {}
    down_xa_tok = {}
    gu_ctr = [0]

    def unit_list(j):
        nsl = j * SCW
        units = []
        for c in range(8):
            units.append(("g",
                          wd["wg_pt"][c * 128:(c + 1) * 128, nsl:nsl + 512],
                          bass.AP(wd["sc_g"].tensor, (8 * c) * NFF + nsl,
                                  [[NFF, 8], [0, 16], [1, 512]]),
                          wg_sb[j % 2], 2 * c, 512, 1.0))
            units.append(("u",
                          wd["wu_pt"][c * 128:(c + 1) * 128, nsl:nsl + 512],
                          bass.AP(wd["sc_u"].tensor, (8 * c) * NFF + nsl,
                                  [[NFF, 8], [0, 16], [1, 512]]),
                          wu_sb[j % 2], 2 * c, 512, 1.0))
        for cp in range(2):
            cpr = 2 * j + cp
            for strip in range(4):
                units.append(("d",
                              wd["wd_pt"][cpr * 128:(cpr + 1) * 128,
                                          strip * 512:(strip + 1) * 512],
                              bass.AP(wd["sc_d"].tensor,
                                      (8 * cpr) * 2048 + strip * 512,
                                      [[2048, 8], [0, 16], [1, 512]]),
                              wdn_sb[j % 2], 2 * cp, strip * 512, 0.5))
        return units

    def load_sc2(j):
        b = j % 2
        w = [(s_pe, pe_done[j - 2])] if j >= 2 else None
        P.E("sp", "dma_start", out=wg_sb[b][:], in_=scratch["g"][j, :, :],
            wait=w, inc=(s_ld, 16))
        P.E("sp", "dma_start", out=wu_sb[b][:], in_=scratch["u"][j, :, :],
            inc=(s_ld, 16))
        t = P.E("sp", "dma_start", out=wdn_sb[b][:], in_=scratch["d"][j, :, :],
                inc=(s_ld, 16))
        dec_done[j] = t

    def store_sc(j):
        b = j % 2
        P.E("sp", "dma_start", out=scratch["g"][j, :, :], in_=wg_sb[b][:],
            wait=[(s_dd, dec_done[j])], inc=(s_st, 16))
        P.E("sp", "dma_start", out=scratch["u"][j, :, :], in_=wu_sb[b][:],
            inc=(s_st, 16))
        P.E("sp", "dma_start", out=scratch["d"][j, :, :], in_=wdn_sb[b][:],
            inc=(s_st, 16))
        st_done[j] = s_st.n

    def stream_sc(j):
        first_dec_w = [(s_pe, pe_done[j - 2])] if j >= 2 else []
        if j >= 2 and scratch is not None and (j - 2) in st_done:
            first_dec_w.append((s_st, st_done[j - 2]))
        for iu, unit in enumerate(unit_list(j)):
            kind, pk_src, scs_src, dst, chb, extra, fold = unit
            gu = gu_ctr[0]
            gu_ctr[0] += 1
            slot = gu % 2
            so = slot * 512
            lw = [(s_dd, slot_free_tok[slot])] if slot in slot_free_tok else None
            P.E("sp", "dma_start", out=pk_sb[:, so:so + 512], in_=pk_src,
                wait=lw, inc=(s_ld, 16))
            sw = [(s_dd, last_mphase_tok[1])] if 1 in last_mphase_tok else None
            t_l2 = P.E("sp", "dma_start", out=scs_sb[:, 0:512], in_=scs_src,
                       wait=sw, inc=(s_ld, 16))
            up = gu % 2
            v_t = vdec[:, up * 512: up * 512 + 512]
            r_t = vdec[:, 1024:1536]
            rb_t = tmp[:, 0:512]
            m_t = tmp[:, 512:1024]
            for s_i in (0, 1):
                npar = s_i
                sg_t = tmp[:, 1024:1536]
                a_t = tmp[:, 1536:2048]
                b_t = tmp[:, 2048:2560]
                vw = [(s_ld, t_l2)]
                if s_i == 0:
                    vw += (first_dec_w if iu == 0 else [])
                    if up in last_sgread_tok:
                        vw.append((s_da, last_sgread_tok[up]))
                    P.E("dve", "tensor_single_scalar", v_t, pk_sb[:, so:so + 512],
                        15, ALU.bitwise_and, wait=vw)
                else:
                    if last_relu[0]:
                        vw.append((s_da, last_relu[0]))
                    P.E("dve", "tensor_single_scalar", v_t, pk_sb[:, so:so + 512],
                        4, ALU.logical_shift_right, wait=vw)
                rw = [(s_da, last_relu[0])] if last_relu[0] else None
                P.E("dve", "tensor_single_scalar", r_t, v_t, 7, ALU.bitwise_and,
                    wait=rw)
                t_r = P.E("dve", "tensor_copy", rb_t, r_t, inc=(s_dd, 1))
                aw = [(s_dd, t_r)]
                if last_mphase_tok:
                    aw.append((s_dd, max(last_mphase_tok.values())))
                P.E("act", "activation", sg_t, v_t, AF.Sign, scale=-1.0,
                    bias=cb(st, C_75), wait=aw)
                P.E("act", "activation", a_t, r_t, AF.Relu, bias=cb(st, C_M4))
                t_a = P.E("act", "activation", b_t, r_t, AF.Relu, bias=cb(st, C_M6),
                          inc=(s_da, 1))
                last_relu[0] = t_a
                if s_i == 1:
                    last_sgread_tok[up] = t_a
                P.E("dve", "tensor_add", m_t, rb_t, a_t, wait=[(s_da, t_a)])
                P.E("dve", "scalar_tensor_tensor", m_t, m_t, 0.5, b_t, ALU.mult, ALU.add)
                P.E("dve", "tensor_mul", m_t, m_t, sg_t)
                if kind == "d":
                    dsto = (chb + s_i) * 2048 + extra
                else:
                    dsto = (chb + s_i) * 512
                t_m = P.E("dve", "scalar_tensor_tensor", dst[:, dsto:dsto + 512],
                          scs_sb[:, 0:512], fold, m_t, ALU.mult, ALU.mult,
                          inc=(s_dd, 1))
                last_mphase_tok[npar] = t_m
            slot_free_tok[slot] = last_mphase_tok[1]
        dec_done[j] = s_dd.n

    def matmul_sc(j):
        b = j % 2
        for tt2 in range(2):
            for ln in range(4):
                un = tt2 * 4 + ln
                gup = (j * 8 + un) % 2
                gps, ups = st.ps[gup], st.ps[2 + gup]
                w0 = [((s_ld if mode == "load" else s_dd), dec_done[j]),
                      (sem_y, sem_y.n)]
                if gup in last_gelu_tok:
                    w0.append((s_ga, last_gelu_tok[gup]))
                if gup in last_inter_tok:
                    w0.append((s_iv, last_inter_tok[gup]))
                for i in range(NCH):
                    P.E("pe", "matmul", gps[:],
                        wg_sb[b][:, i * 512 + ln * 128: i * 512 + ln * 128 + 128],
                        yown(st, i, tt2 * 512, 512),
                        start=(i == 0), stop=(i == NCH - 1), wait=w0 if i == 0 else None)
                t_pe = None
                for i in range(NCH):
                    t_pe = P.E("pe", "matmul", ups[:],
                               wu_sb[b][:, i * 512 + ln * 128: i * 512 + ln * 128 + 128],
                               yown(st, i, tt2 * 512, 512),
                               start=(i == 0), stop=(i == NCH - 1),
                               inc=(s_pe, 1) if i == NCH - 1 else None)
                gw = [(s_pe, t_pe)]
                if last_inter_tok:
                    gw.append((s_iv, max(last_inter_tok.values())))
                t_g = P.E("act", "activation", gel[:, 0:512], gps[:],
                          AF.Gelu_apprx_tanh, wait=gw, inc=(s_ga, 1))
                last_gelu_tok[gup] = t_g
                t_i = P.E("dve", "tensor_mul",
                          inter[:, ln * 512:(ln + 1) * 512],
                          gel[:, 0:512], ups[:],
                          wait=[(s_ga, t_g), (s_pe, t_pe)], inc=(s_iv, 1))
                last_inter_tok[gup] = t_i
            t_inter_half = s_iv.n
            for dch in range(NCH):
                dp = (dch + tt2) % 2
                dps = st.ps[4] if dp == 0 else st.psS
                w0 = [(s_iv, t_inter_half)]
                if dp in down_xa_tok:
                    w0.append((s_xa, down_xa_tok[dp]))
                t_pe = None
                for ln in range(4):
                    t_pe = P.E("pe", "matmul", dps[:],
                               wdn_sb[b][:, ln * 2048 + dch * 128: ln * 2048 + dch * 128 + 128],
                               inter[:, ln * 512:(ln + 1) * 512],
                               start=(ln == 0), stop=(ln == 3),
                               wait=w0 if ln == 0 else None,
                               inc=(s_pe, 1) if ln == 3 else None)
                t_x = P.E("dve", "tensor_add", xch(st, dch, tt2 * 512, 512),
                          xch(st, dch, tt2 * 512, 512), dps[:],
                          wait=[(s_pe, t_pe)], inc=(s_xa, 1))
                down_xa_tok[dp] = t_x
        pe_done[j] = s_pe.n

    for j in range(NSC):
        if mode == "load":
            load_sc2(j)
        else:
            stream_sc(j)
            if scratch is not None:
                store_sc(j)
        matmul_sc(j)
    P.W("dve", s_xa)
    if mode != "load" and scratch is not None:
        P.W("sp", s_st)
    P.E("dve", "nop", inc=(sem_done, 1))
    _ph.close()



def emit_attn0(P, st, wd, sem_y, sem_done, stop_after=None, yh=None,
               yh_stack=None, byp_sb=None):
    nc = P.nc
    ec = P.stack.enter_context
    s_ld = P.sem("a0_ld")
    s_cl = P.sem("a0_cast")
    s_pe = P.sem("a0_pe")
    s_av = P.sem("a0_act")
    s_dv = P.sem("a0_dve")
    s_cc = P.sem("a0_cc")
    s_st = P.sem("a0_st")

    agk_in = nc.dram_tensor("agk_in", [2048, MSL], BF16)
    agk_out = nc.dram_tensor("agk_out", [8 * 2048, MSL], BF16)
    agv_in = nc.dram_tensor("agv_in", [MSL, 2048], BF16)
    agv_out = nc.dram_tensor("agv_out", [M_MEM, 2048], BF16)
    with ExitStack() as ph:
        memT_sb = ph.enter_context(P.sbt("memT_sb", [128, 16 * MSL], BF16))
        wk_sb = ph.enter_context(P.sbt("wkm_sb", [128, 16 * 128], BF16))
        kms = ph.enter_context(P.sbt("kms_sb", [128, 16 * MSL], BF16))
        wvm_sb = ph.enter_context(P.sbt("wvm_sb", [128, 16 * 512], BF16))
        vms = ph.enter_context(P.sbt("vms_sb", [128, 4 * 2048], BF16))
        t_mem = P.E("pool", "dma_start", out=memT_sb[:],
                    in_=wd["memT_sl"].rearrange("(c p) m -> p c m", p=128),
                    inc=(s_cl, 16))
        if stop_after == "p0ld":
            P.W("pool", s_cl)
            P.E("pool", "nop", inc=(sem_done, 1))
        for dk in range(0 if stop_after == 'p0ld' else 16):
            t_w = P.E("pool", "dma_start", out=wk_sb[:],
                      in_=wd["wkm_l"][dk, :, :],
                      wait=[(s_av, s_av.n)] if dk else None, inc=(s_cl, 16))
            if stop_after == "p0w":
                continue
            for u in range(MSL // 512):
                pp = st.ps[(dk + u) % 2]
                t_pe = None
                for i in range(NCH):
                    t_pe = P.E("pe", "matmul", pp[:],
                               wk_sb[:, i * 128:(i + 1) * 128],
                               memT_sb[:, i * MSL + u * 512: i * MSL + u * 512 + 512],
                               start=(i == 0), stop=(i == NCH - 1),
                               wait=[(s_cl, s_cl.n), (s_av, max(0, s_av.n - 1))] if i == 0 else None,
                               inc=(s_pe, 1) if i == NCH - 1 else None)
                P.E("act", "activation", kms[:, dk * MSL + u * 512: dk * MSL + u * 512 + 512],
                    pp[:], AF.Copy, scale=INV_SQRT_D, wait=[(s_pe, t_pe)], inc=(s_av, 1))
        t_kst = None
        if stop_after not in ('p0ld', 'p0k', 'p0w'):
            t_kst = P.E("sp", "dma_start",
                        out=agk_in.ap().rearrange("(c p) m -> p c m", p=128),
                        in_=kms[:], wait=[(s_av, s_av.n)], inc=(s_ld, 16))
        for dvs in range(4 if stop_after not in ('p0ld', 'p0k', 'p0w') else 0):
            t_w = P.E("pool", "dma_start", out=wvm_sb[:],
                      in_=wd["wvm_l"][dvs, :, :],
                      wait=[(s_av, s_av.n)] if dvs else None, inc=(s_cl, 16))
            for mb in range(4):
                pp = st.ps[2 + mb % 2]
                t_pe = None
                for i in range(NCH):
                    t_pe = P.E("pe", "matmul", pp[:],
                               memT_sb[:, i * MSL + mb * 128: i * MSL + mb * 128 + 128],
                               wvm_sb[:, i * 512:(i + 1) * 512],
                               start=(i == 0), stop=(i == NCH - 1),
                               wait=[(s_cl, s_cl.n), (s_av, max(0, s_av.n - 1))] if i == 0 else None,
                               inc=(s_pe, 1) if i == NCH - 1 else None)
                P.E("act", "activation",
                    vms[:, mb * 2048 + dvs * 512: mb * 2048 + dvs * 512 + 512],
                    pp[:], AF.Copy, wait=[(s_pe, t_pe)], inc=(s_av, 1))
        if stop_after not in ('p0ld', 'p0k', 'p0w'):
            t_vst = P.E("sp", "dma_start",
                        out=agv_in.ap().rearrange("(c p) m -> p c m", p=128),
                        in_=vms[:], wait=[(s_av, s_av.n)], inc=(s_ld, 16))
        P.W("pool", s_cl)
        P.W("pool", s_ld)
        if stop_after not in ('p0nc', 'p0ld', 'p0k', 'p0w'):
            P.E("pool", "collective_compute", "AllGather", ALU.bypass,
                replica_groups=[list(range(NCORES))],
                ins=[agk_in.ap().opt()], outs=[agk_out.ap().opt()], inc=(s_cc, 1))
            P.E("pool", "collective_compute", "AllGather", ALU.bypass,
                replica_groups=[list(range(NCORES))],
                ins=[agv_in.ap().opt()], outs=[agv_out.ap().opt()], inc=(s_cc, 1))
        P.barrier()
    if stop_after in ('p0nc', 'p0ld', 'p0k', 'p0w'):
        P.W("pool", s_cl)
        P.E("pool", "nop", inc=(sem_done, 1))
        return
    if stop_after == "p0":
        P.W("dve", s_cc)
        P.E("dve", "nop", inc=(sem_done, 1))
        return

    ph_ctx = ExitStack()
    ctx_sb = ph_ctx.enter_context(P.sbt("ctx_sb", [128, 16 * 1024], BF16))
    with ExitStack() as ph:
        st.masks_d = nc.dram_tensor("masks", [8, 128, 384], F32,
                                    kind="ExternalInput").ap()
        st.masks = ph.enter_context(P.sbt("masks_sb", [128, 8 * 384], F32))
        for qt in range(8):
            P.E("sp", "dma_start", out=st.masks[:, qt * 384:(qt + 1) * 384],
                in_=st.masks_d[qt, :, :], inc=(st.ld, 16))
        wq_sb = ph.enter_context(P.sbt("wq_sb", [128, 16 * 128], BF16))
        wk_sb = ph.enter_context(P.sbt("wk_sb", [128, 16 * 128], BF16))
        wv_sb = ph.enter_context(P.sbt("wv_sb", [128, 16 * 128], BF16))
        q_sb = ph.enter_context(P.sbt("q_sb", [128, 1024], BF16))
        k_sb = ph.enter_context(P.sbt("k_sb", [128, TEXT], BF16))
        v_sb = ph.enter_context(P.sbt("v_sb", [128, TEXT], BF16))
        vt_sb = ph.enter_context(P.sbt("vt_sb", [128, 10 * 128], BF16))
        sc_sb = ph.enter_context(P.sbt("sc_sb", [128, 2 * 384], F32))
        sm_sb = ph.enter_context(P.sbt("sm_sb", [128, 8], F32))
        w_sb = ph.enter_context(P.sbt("w_sb", [128, 2 * 384], BF16))
        wt_sb = ph.enter_context(P.sbt("wt_sb", [128, 2 * 384], BF16))
        for h in range(H):
            hs = slice(h * 128, (h + 1) * 128)
            t_w = None
            for nm, sb in (("wq_l0", wq_sb), ("wk_l0", wk_sb), ("wv_l0", wv_sb)):
                t_w = P.E("pool", "dma_start", out=sb[:],
                          in_=wd[nm][h, :, :],
                          wait=[(s_pe, s_pe.n)] if h else None, inc=(s_cl, 16))
            qu = [(lambda i, o=o: yown(st, i, o, 512), 512, 256 + o)
                  for o in (0, 512)]
            ku = [(lambda i: yh(i, 0, 256), 256, 0)] + \
                 [(lambda i, o=o: yown(st, i, o, 512), 512, 256 + o)
                  for o in (0, 512)]
            for (wsb, dst, units, scl) in ((wq_sb, q_sb, qu, INV_SQRT_HD),
                                           (wk_sb, k_sb, ku, None),
                                           (wv_sb, v_sb, ku, None)):
                for u, (src_fn, wid, dsto) in enumerate(units):
                    if dst is q_sb:
                        dsto -= 256
                    pp = st.ps[u % 2]
                    t_pe = None
                    for i in range(NCH):
                        t_pe = P.E("pe", "matmul", pp[:, 0:wid],
                                   wsb[:, i * 128:(i + 1) * 128],
                                   src_fn(i),
                                   start=(i == 0), stop=(i == NCH - 1),
                                   wait=[(s_cl, t_w), (sem_y, sem_y.n),
                                         (s_av, s_av.n)] if i == 0 else None,
                                   inc=(s_pe, 1) if i == NCH - 1 else None)
                    if scl is None:
                        P.E("act", "activation", dst[:, dsto:dsto + wid],
                            pp[:, 0:wid], AF.Copy, wait=[(s_pe, t_pe)], inc=(s_av, 1))
                    else:
                        P.E("act", "activation", dst[:, dsto:dsto + wid],
                            pp[:, 0:wid], AF.Copy, scale=scl,
                            wait=[(s_pe, t_pe)], inc=(s_av, 1))
            if stop_after == "p1q" and h == 0:
                cpv = P.sem("a0_dbgq")
                P.W("dve", s_av)
                P.E("dve", "tensor_copy", xch(st, 0, 0, 1024), q_sb[:, 0:1024],
                    inc=(cpv, 1))
                P.E("dve", "tensor_copy", xch(st, 1, 0, 1024), k_sb[:, 0:1024],
                    inc=(cpv, 1))
                P.E("dve", "tensor_copy", xch(st, 2, 0, 1024), k_sb[:, 256:1280],
                    inc=(cpv, 1))
                P.E("dve", "tensor_copy", xch(st, 3, 0, 1024), v_sb[:, 256:1280],
                    inc=(cpv, 1))
                P.W("dve", cpv)
                P.E("dve", "nop", inc=(sem_done, 1))
                break
            for ts_ in range(10):
                pb = st.psT[ts_ % 2]
                col = ((ts_ // 2) % 4) * 128
                w = [(s_av, s_av.n), (s_dv, s_dv.n)] if ts_ == 0 else (
                    [(s_dv, s_dv.n - 1)] if ts_ >= 2 else None)
                t_pe = P.E("pe", "transpose", pb[:, col:col + 128],
                           v_sb[:, ts_ * 128:(ts_ + 1) * 128], st.ident_b[:],
                           wait=w, inc=(s_pe, 1))
                P.E("dve", "tensor_copy", vt_sb[:, ts_ * 128:(ts_ + 1) * 128],
                    pb[:, col:col + 128],
                    wait=[(s_pe, t_pe)], inc=(s_dv, 1))
            if stop_after == "p1s" and h == 1:
                cpv = P.sem("a0_dbgs")
                P.W("dve", s_av)
                P.E("dve", "tensor_copy", xch(st, 0, 0, 1024),
                    ctx_sb[:, 0:1024], inc=(cpv, 1))
                P.E("dve", "tensor_copy", xch(st, 1, 0, 768),
                    st.x[:, 1 * TPC: 1 * TPC + 768] if False else sc_sb[:, 0:768],
                    inc=(cpv, 1))
                P.E("dve", "tensor_copy", xch(st, 2, 0, 768), w_sb[:, 0:768],
                    inc=(cpv, 1))
                P.E("dve", "tensor_copy", xch(st, 3, 0, 768), wt_sb[:, 0:768],
                    inc=(cpv, 1))
                P.W("dve", cpv)
                P.E("dve", "nop", inc=(sem_done, 1))
                break
            wt_tok = {}
            for qt in range(8):
                sp_ = qt % 2
                pp = st.ps[2 + sp_]
                t_pe = P.E("pe", "matmul", pp[:, 0:384],
                           q_sb[:, qt * 128:(qt + 1) * 128],
                           k_sb[:, qt * 128:qt * 128 + 384],
                           start=True, stop=True,
                           wait=[(s_av, s_av.n), (s_dv, s_dv.n)] if qt == 0 else
                           [(s_dv, s_dv.n - 4)],
                           inc=(s_pe, 1))
                P.E("dve", "tensor_add", sc_sb[:, sp_ * 384:sp_ * 384 + 384],
                    pp[:, 0:384], st.masks[:, qt * 384:(qt + 1) * 384],
                    wait=[(s_pe, t_pe), (st.ld, st.ld.n)])
                t_d = P.E("dve", "reduce_max", sm_sb[:, sp_:sp_ + 1],
                          sc_sb[:, sp_ * 384:sp_ * 384 + 384], axis=AX,
                          inc=(s_dv, 1))
                t_d = P.E("dve", "tensor_scalar_mul", sm_sb[:, 4 + sp_:5 + sp_],
                          sm_sb[:, sp_:sp_ + 1], -1.0, wait=[(s_dv, t_d)],
                          inc=(s_dv, 1))
                t_a = P.E("act", "activation", sc_sb[:, sp_ * 384:sp_ * 384 + 384],
                          sc_sb[:, sp_ * 384:sp_ * 384 + 384], AF.Exp,
                          bias=sm_sb[:, 4 + sp_:5 + sp_],
                          wait=[(s_dv, t_d)], inc=(s_av, 1))
                t_d = P.E("dve", "reduce_sum", sm_sb[:, 2 + sp_:3 + sp_],
                          sc_sb[:, sp_ * 384:sp_ * 384 + 384], axis=AX,
                          wait=[(s_av, t_a)], inc=(s_dv, 1))
                t_d = P.E("dve", "reciprocal", sm_sb[:, 6 + sp_:7 + sp_],
                          sm_sb[:, 2 + sp_:3 + sp_], wait=[(s_dv, t_d)],
                          inc=(s_dv, 1))
                ww = [(s_dv, t_d)]
                if sp_ in wt_tok:
                    ww.append((s_pe, wt_tok[sp_]))
                t_d = P.E("dve", "tensor_scalar_mul",
                          w_sb[:, sp_ * 384:sp_ * 384 + 384],
                          sc_sb[:, sp_ * 384:sp_ * 384 + 384],
                          sm_sb[:, 6 + sp_:7 + sp_],
                          wait=ww, inc=(s_dv, 1))
                for kt in range(3):
                    pb = st.psT[kt % 2]
                    col = 512 + (kt // 2) * 128
                    t_pe = P.E("pe", "transpose", pb[:, col:col + 128],
                               w_sb[:, sp_ * 384 + kt * 128: sp_ * 384 + kt * 128 + 128],
                               st.ident_b[:],
                               wait=[(s_dv, t_d)] if kt == 0 else
                               ([(s_dv, s_dv.n - 1)] if kt >= 2 else None),
                               inc=(s_pe, 1))
                    P.E("dve", "tensor_copy",
                        wt_sb[:, sp_ * 384 + kt * 128: sp_ * 384 + kt * 128 + 128],
                        pb[:, col:col + 128],
                        wait=[(s_pe, t_pe)], inc=(s_dv, 1))
                    wt_tok[sp_] = t_pe
                t_pe = None
                for kt in range(3):
                    t_pe = P.E("pe", "matmul", st.psS[:, 0:128],
                               vt_sb[:, (qt + kt) * 128:(qt + kt) * 128 + 128],
                               wt_sb[:, sp_ * 384 + kt * 128: sp_ * 384 + kt * 128 + 128],
                               start=(kt == 0), stop=(kt == 2),
                               wait=[(s_dv, s_dv.n), (s_av, max(0, s_av.n - 0))] if kt == 0 else None,
                               inc=(s_pe, 1) if kt == 2 else None)
                P.E("act", "activation", ctx_sb[:, h * 1024 + qt * 128: h * 1024 + qt * 128 + 128],
                    st.psS[:, 0:128], AF.Copy, wait=[(s_pe, t_pe)], inc=(s_av, 1))
        P.barrier()
    if stop_after == "p1s":
        ph_ctx.close()
        return
    if stop_after == "p1q":
        ph_ctx.close()
        return
    if stop_after == "p1":
        P.W("dve", s_av)
        cpv = P.sem("a0_dbgcp")
        for i in range(NCH):
            P.E("dve", "tensor_copy", xch(st, i, 0, 1024),
                ctx_sb[:, i * 1024:(i + 1) * 1024], inc=(cpv, 1))
        P.W("dve", cpv)
        P.E("dve", "nop", inc=(sem_done, 1))
        ph_ctx.close()
        return

    ph_ol = ExitStack()
    ol_sb = ph_ol.enter_context(P.sbt("ol_sb", [128, 16 * 1024], BF16))
    gw_sb = ph_ol.enter_context(P.sbt("gw_sb", [128, 16], BF16))
    gwf_sb = ph_ol.enter_context(P.sbt("gwf_sb", [128, 16], F32))
    gb_sb = ph_ol.enter_context(P.sbt("gb_sb", [128, 1], F32))
    with ExitStack() as ph:
        wo_sb = [ph.enter_context(P.sbt(f"wo_sb{b}", [128, 16 * 128], BF16))
                 for b in range(2)]
        for do in range(16):
            b = do % 2
            t_w = P.E("pool", "dma_start", out=wo_sb[b][:],
                      in_=wd["wo_l0"][do, :, :],
                      wait=[(s_pe, max(0, s_pe.n - 1))] if do >= 2 else None,
                      inc=(s_cl, 16))
            for tt2 in range(2):
                pp = st.ps[(do * 2 + tt2) % 4]
                t_pe = None
                for i in range(NCH):
                    t_pe = P.E("pe", "matmul", pp[:],
                               wo_sb[b][:, i * 128:(i + 1) * 128],
                               ctx_sb[:, i * 1024 + tt2 * 512: i * 1024 + tt2 * 512 + 512],
                               start=(i == 0), stop=(i == NCH - 1),
                               wait=[(s_cl, t_w), (s_dv, max(0, s_dv.n - 2)),
                                     (s_av, max(0, s_av.n - 2))] if i == 0 else None,
                               inc=(s_pe, 1) if i == NCH - 1 else None)
                t_x = P.E("dve", "tensor_add", xch(st, do, tt2 * 512, 512),
                          xch(st, do, tt2 * 512, 512), pp[:],
                          wait=[(s_pe, t_pe)], inc=(s_dv, 1))
                P.E("act", "activation",
                    ol_sb[:, do * 1024 + tt2 * 512: do * 1024 + tt2 * 512 + 512],
                    pp[:], AF.Copy, wait=[(s_pe, t_pe), (s_dv, t_x)], inc=(s_av, 1))
        P.barrier()
    if stop_after == "p2":
        P.W("dve", s_dv)
        P.E("dve", "nop", inc=(sem_done, 1))
        ph_ol.close()
        ph_ctx.close()
        return

    s_cp = P.sem("a0_cp")
    t_gwf = P.E("sp", "dma_start", out=gwf_sb[:], in_=wd["gw_pc"][:, :],
                inc=(s_ld, 16))
    P.E("pool", "tensor_copy", gw_sb[:], gwf_sb[:],
        wait=[(s_ld, t_gwf)], inc=(s_cp, 1))
    P.E("sp", "dma_start", out=gb_sb[:],
        in_=bass.AP(wd["gb0"].tensor, 0, [[0, 128], [1, 1]]), inc=(s_ld, 16))
    t_gw = s_cp.n
    t_gb = s_ld.n
    for tt in range(8):
        t_pe = None
        for i in range(NCH):
            t_pe = P.E("pe", "matmul", st.psS[0:128, 0:1],
                       ol_sb[:, i * 1024 + tt * 128: i * 1024 + tt * 128 + 128],
                       gw_sb[:, i:i + 1],
                       start=(i == 0), stop=(i == NCH - 1),
                       wait=[(s_cp, t_gw), (s_av, s_av.n)] if i == 0 and tt == 0 else
                       ([(s_av, s_av.n)] if i == 0 else None),
                       inc=(s_pe, 1) if i == NCH - 1 else None)
        P.E("act", "activation", byp_sb[:, tt:tt + 1], st.psS[0:128, 0:1],
            AF.Sigmoid, bias=gb_sb[:, 0:1],
            wait=[(s_pe, t_pe), (st.ld, st.ld.n), (s_ld, t_gb)], inc=(s_av, 1))
    P.barrier()
    ph_ol.close()
    ph_ctx.close()
    if yh_stack is not None:
        yh_stack.close()
    if stop_after == "byp":
        P.W("dve", s_av)
        P.E("dve", "nop", inc=(sem_done, 1))
        return

    with ExitStack() as ph:
        km_sb = [ph.enter_context(P.sbt(f"km_sb{b}", [128, 16 * 512], BF16))
                 for b in range(2)]
        vm_sb = [ph.enter_context(P.sbt(f"vm_sb{b}", [128, 512], BF16))
                 for b in range(2)]
        wo2_sb = [ph.enter_context(P.sbt(f"wo2_sb{b}", [128, 16 * 128], BF16))
                  for b in range(2)]
        sim_sb = ph.enter_context(P.sbt("sim_sb", [128, 4096], F32))
        wn_sb = ph.enter_context(P.sbt("wn_sb", [128, 4096], BF16))
        wnt_sb = ph.enter_context(P.sbt("wnt_sb", [128, 32 * 256], BF16))
        t8_sb = ph.enter_context(P.sbt("t8_sb", [128, 16], F32))
        opre_sb = ph.enter_context(P.sbt("opre_sb", [128, 16 * 256], BF16))
        t_k_ld = {}
        for grp in range(4):
            for lt in range(2):
                tt = grp * 2 + lt
                for ms in range(8):
                    b = ms % 2
                    t_k = P.E("sp", "dma_start", out=km_sb[b][:],
                              in_=agk_out.ap()[ms * 2048:(ms + 1) * 2048, :]
                              .rearrange("(c p) m -> p c m", p=128),
                              wait=[(s_cc, 1), (s_pe, max(0, s_pe.n - 1))],
                              inc=(s_ld, 16))
                    t_k_ld[ms] = t_k
                    pp = st.ps[ms % 2]
                    t_pe = None
                    for i in range(NCH):
                        t_pe = P.E("pe", "matmul", pp[:],
                                   yown(st, i, tt * 128, 128),
                                   km_sb[b][:, i * 512:(i + 1) * 512],
                                   start=(i == 0), stop=(i == NCH - 1),
                                   wait=[(s_ld, t_k_ld[ms]),
                                         (s_av, max(0, s_av.n - 1))] if i == 0 else None,
                                   inc=(s_pe, 1) if i == NCH - 1 else None)
                    P.E("act", "activation", sim_sb[:, ms * 512:(ms + 1) * 512],
                        pp[:], AF.Copy, wait=[(s_pe, t_pe)], inc=(s_av, 1))
                t_sim = s_av.n
                t_d = P.E("dve", "max", t8_sb[:, 0:8], sim_sb[:],
                          wait=[(s_av, t_sim)], inc=(s_dv, 1))
                t_d = P.E("dve", "tensor_scalar_mul", t8_sb[:, 8:9], t8_sb[:, 0:1],
                          -1.0, wait=[(s_dv, t_d)], inc=(s_dv, 1))
                t_d = P.E("dve", "tensor_add", t8_sb[:, 9:10], t8_sb[:, 7:8],
                          t8_sb[:, 8:9], wait=[(s_dv, t_d)], inc=(s_dv, 1))
                t_a = P.E("act", "activation", t8_sb[:, 10:11], t8_sb[:, 9:10],
                          AF.Exp, wait=[(s_dv, t_d)], inc=(s_av, 1))
                t_a2 = P.E("act", "activation", sim_sb[:], sim_sb[:], AF.Exp,
                           bias=t8_sb[:, 8:9], wait=[(s_dv, t_d)], inc=(s_av, 1))
                t_d = P.E("dve", "scalar_tensor_tensor",
                          wn_sb[:, 0:4096], sim_sb[:],
                          t8_sb[:, 10:11], sim_sb[:], ALU.is_ge, ALU.mult,
                          wait=[(s_av, s_av.n)], inc=(s_dv, 1))
                t_d = P.E("dve", "reduce_sum", t8_sb[:, 11:12],
                          wn_sb[:, 0:4096], axis=AX, inc=(s_dv, 1))
                t_d = P.E("dve", "reciprocal", t8_sb[:, 12:13], t8_sb[:, 11:12],
                          wait=[(s_dv, t_d)], inc=(s_dv, 1))
                t_d = P.E("dve", "tensor_scalar",
                          wn_sb[:, 0:4096],
                          wn_sb[:, 0:4096],
                          t8_sb[:, 12:13], byp_sb[:, tt:tt + 1],
                          ALU.mult, ALU.mult, wait=[(s_dv, t_d)], inc=(s_dv, 1))
                for ms2 in range(32):
                    pb = st.psT[ms2 % 2]
                    col = ((ms2 // 2) % 4) * 128
                    t_pe = P.E("pe", "transpose", pb[:, col:col + 128],
                               wn_sb[:, ms2 * 128: ms2 * 128 + 128],
                               st.ident_b[:],
                               wait=[(s_dv, t_d)] if ms2 == 0 else
                               ([(s_dv, s_dv.n - 1)] if ms2 >= 2 else None),
                               inc=(s_pe, 1))
                    P.E("dve", "tensor_copy",
                        wnt_sb[:, ms2 * 256 + lt * 128: ms2 * 256 + lt * 128 + 128],
                        pb[:, col:col + 128],
                        wait=[(s_pe, t_pe)], inc=(s_dv, 1))
            t_wnt = s_dv.n
            for pass_ in range(4):
                t_pe_last = {}
                mc_tok = {}
                for mc in range(32):
                    vw = [(s_cc, 2)]
                    if mc - 2 in mc_tok:
                        vw.append((s_pe, mc_tok[mc - 2]))
                    elif pass_ > 0:
                        vw.append((s_pe, s_pe.n))
                    t_v = P.E("sp", "dma_start", out=vm_sb[mc % 2][:],
                              in_=agv_out.ap()[mc * 128:(mc + 1) * 128,
                                               pass_ * 512:(pass_ + 1) * 512],
                              wait=vw, inc=(s_ld, 16))
                    for ldv in range(4):
                        dv_ = pass_ * 4 + ldv
                        pp = st.ps[ldv]
                        w = None
                        if mc == 0:
                            w = [(s_dv, t_wnt), (s_ld, t_v),
                                 (s_av, s_av.n)]
                        elif ldv == 0:
                            w = [(s_ld, t_v)]
                        t_pe_last[ldv] = P.E("pe", "matmul", pp[:, 0:256],
                                             vm_sb[mc % 2][:, ldv * 128:(ldv + 1) * 128],
                                             wnt_sb[:, mc * 256:(mc + 1) * 256],
                                             start=(mc == 0), stop=(mc == 31), wait=w,
                                             inc=(s_pe, 1))
                        mc_tok[mc] = t_pe_last[ldv]
                for ldv in range(4):
                    dv_ = pass_ * 4 + ldv
                    P.E("act", "activation", opre_sb[:, dv_ * 256:(dv_ + 1) * 256],
                        st.ps[ldv][:, 0:256], AF.Copy,
                        wait=[(s_pe, t_pe_last[ldv])], inc=(s_av, 1))
            t_opre = s_av.n
            for do in range(16):
                b2 = do % 2
                t_w = P.E("pool", "dma_start", out=wo2_sb[b2][:],
                          in_=wd["wo_l0"][do, :, :],
                          wait=[(s_pe, max(0, s_pe.n - 1))] if (grp or do >= 2) else None,
                          inc=(s_cl, 16))
                pp = st.ps[4] if do % 2 == 0 else st.psS
                t_pe = None
                for i in range(NCH):
                    t_pe = P.E("pe", "matmul", pp[:, 0:256],
                               wo2_sb[b2][:, i * 128:(i + 1) * 128],
                               opre_sb[:, i * 256:(i + 1) * 256],
                               start=(i == 0), stop=(i == NCH - 1),
                               wait=[(s_av, t_opre), (s_cl, t_w),
                                     (s_dv, max(0, s_dv.n - 1))] if i == 0 else None,
                               inc=(s_pe, 1) if i == NCH - 1 else None)
                P.E("dve", "tensor_add", xch(st, do, grp * 256, 256),
                    xch(st, do, grp * 256, 256), pp[:, 0:256],
                    wait=[(s_pe, t_pe)], inc=(s_dv, 1))
        P.W("dve", s_dv)
        P.E("dve", "nop", inc=(sem_done, 1))



def emit_ffn1(P, st, wd, sem_y, sem_done):
    nc = P.nc
    ec = P.stack.enter_context
    s_ld = P.sem("f1_ld")
    s_cl = P.sem("f1_cl")
    s_cp = P.sem("f1_cp")
    s_pe = P.sem("f1_pe")
    s_av = P.sem("f1_av")
    s_dv = P.sem("f1_dv")
    with ExitStack() as ph:
        rw_sb = ph.enter_context(P.sbt("rw_sb", [128, 16 * 32], BF16))
        at_sb = ph.enter_context(P.sbt("at_sb", [128, 16 * 64], BF16))
        bt_sb = ph.enter_context(P.sbt("bt_sb", [64, 2048], BF16))
        r64_sb = ph.enter_context(P.sbt("r64_sb", [32, 64], F32))
        bia_sb = ph.enter_context(P.sbt("bia_sb", [64, 1], F32))
        gl_sb = ph.enter_context(P.sbt("gl_sb", [64, 1024], F32))
        gt_sb = ph.enter_context(P.sbt("gt_sb", [128, 8 * 64], F32))
        tk_sb = ph.enter_context(P.sbt("tk_sb", [128, 12], F32))
        gat_sb = ph.enter_context(P.sbt("gat_sb", [64, 1024], BF16))
        mid_sb = ph.enter_context(P.sbt("mid_sb", [64, 1024], BF16))
        bw_sb = [ph.enter_context(P.sbt(f"bw_sb{b}", [128, 16 * 128], BF16))
                 for b in range(2)]
        with ExitStack() as ph2:
            rwf_sb = ph2.enter_context(P.sbt("rwf_sb", [128, 512], F32))
            atf_sb = ph2.enter_context(P.sbt("atf_sb", [128, 1024], F32))
            t_f = P.E("sp", "dma_start", out=rwf_sb[:], in_=wd["rw_pc"][:, :],
                      inc=(s_ld, 16))
            P.E("pool", "tensor_copy", rw_sb[:], rwf_sb[:], wait=[(s_ld, t_f)],
                inc=(s_cp, 1))
            t_f = P.E("sp", "dma_start", out=atf_sb[:], in_=wd["at_pc"][:, :],
                      inc=(s_ld, 16))
            P.E("pool", "tensor_copy", at_sb[:], atf_sb[:], wait=[(s_ld, t_f)],
                inc=(s_cp, 1))
        P.E("pool", "dma_start", out=bt_sb[:], in_=wd["BT1"][:, :], inc=(s_cl, 16))
        P.E("sp", "dma_start", out=r64_sb[:], in_=wd["r64"][:], inc=(s_ld, 16))
        P.E("sp", "dma_start", out=bia_sb[:], in_=wd["bias1"][:], inc=(s_ld, 16))
        t_cl = s_cl.n
        t_cp = s_cp.n
        t_ld = s_ld.n
        for u in range(2):
            pp = st.ps[u]
            t_pe = None
            for i in range(NCH):
                t_pe = P.E("pe", "matmul", pp[0:32, 0:512],
                           rw_sb[:, i * 32:(i + 1) * 32],
                           yown(st, i, u * 512, 512),
                           start=(i == 0), stop=(i == NCH - 1),
                           wait=[(s_cl, t_cl), (s_cp, t_cp),
                                 (sem_y, sem_y.n)] if i == 0 else None,
                           inc=(s_pe, 1) if i == NCH - 1 else None)
            P.E("act", "activation", gl_sb[0:32, u * 512:(u + 1) * 512],
                pp[0:32, 0:512],
                AF.Sigmoid, wait=[(s_pe, t_pe)], inc=(s_av, 1))
        for u in range(2):
            pp = st.ps[2 + u]
            t_pe = P.E("pe", "matmul", pp[0:64, 0:512], r64_sb[:],
                       gl_sb[0:32, u * 512:(u + 1) * 512], start=True, stop=True,
                       wait=[(s_av, s_av.n), (st.ld, st.ld.n), (s_ld, t_ld)],
                       inc=(s_pe, 1))
            P.E("dve", "tensor_scalar", gl_sb[:, u * 512:(u + 1) * 512],
                pp[0:64, 0:512], bia_sb[:, 0:1], None, ALU.add,
                wait=[(s_pe, t_pe), (s_ld, t_ld)], inc=(s_dv, 1))
        for tt in range(8):
            pb = st.psS if tt % 2 == 0 else st.ps[4]
            col = ((tt // 2) % 2) * 64
            t_pe = P.E("pe", "transpose", pb[:, col:col + 64],
                       gl_sb[:, tt * 128:(tt + 1) * 128], st.ident_f[0:64, 0:64],
                       wait=[(s_av, s_av.n), (s_dv, s_dv.n)] if tt == 0 else
                       ([(s_dv, s_dv.n - 1)] if tt >= 2 else None), inc=(s_pe, 1))
            P.E("dve", "tensor_copy", gt_sb[:, tt * 64:(tt + 1) * 64],
                pb[:, col:col + 64],
                wait=[(s_pe, t_pe)], inc=(s_dv, 1))
        for tt in range(8):
            g_ap = gt_sb[:, tt * 64:(tt + 1) * 64]
            t_d = P.E("dve", "max", tk_sb[:, 0:8], g_ap, wait=[(s_dv, s_dv.n)],
                      inc=(s_dv, 1))
            t_d = P.E("dve", "scalar_tensor_tensor", g_ap, g_ap, tk_sb[:, 7:8],
                      g_ap, ALU.is_ge, ALU.mult, wait=[(s_dv, t_d)], inc=(s_dv, 1))
            t_d = P.E("dve", "reduce_sum", tk_sb[:, 8:9], g_ap, axis=AX,
                      wait=[(s_dv, t_d)], inc=(s_dv, 1))
            t_d = P.E("dve", "reciprocal", tk_sb[:, 9:10], tk_sb[:, 8:9],
                      wait=[(s_dv, t_d)], inc=(s_dv, 1))
            t_d = P.E("dve", "tensor_scalar", g_ap, g_ap, tk_sb[:, 9:10], 16.0,
                      ALU.mult, ALU.mult, wait=[(s_dv, t_d)], inc=(s_dv, 1))
            pb2 = st.psS if tt % 2 == 0 else st.ps[4]
            t_pe = P.E("pe", "transpose", pb2[0:64, 256:384], g_ap, st.ident_f[:],
                       wait=[(s_dv, t_d), (s_av, max(0, s_av.n - 1))], inc=(s_pe, 1))
            P.E("act", "activation", gat_sb[:, tt * 128:(tt + 1) * 128],
                pb2[0:64, 256:384], AF.Copy,
                wait=[(s_pe, t_pe)], inc=(s_av, 1))
        for u in range(2):
            pp = st.ps[u]
            t_pe = None
            for i in range(NCH):
                t_pe = P.E("pe", "matmul", pp[0:64, 0:512],
                           at_sb[:, i * 64:(i + 1) * 64],
                           yown(st, i, u * 512, 512),
                           start=(i == 0), stop=(i == NCH - 1),
                           wait=[(s_av, s_av.n), (s_cp, t_cp)] if i == 0 else None,
                           inc=(s_pe, 1) if i == NCH - 1 else None)
            P.E("act", "activation", mid_sb[:, u * 512:(u + 1) * 512], pp[0:64, 0:512],
                AF.Copy, wait=[(s_pe, t_pe)], inc=(s_av, 1))
        t_d = P.E("dve", "tensor_mul", mid_sb[:], mid_sb[:], gat_sb[:],
                  wait=[(s_av, s_av.n)], inc=(s_dv, 1))
        for nchk in range(NCH):
            b = nchk % 2
            t_w = P.E("pool", "dma_start", out=bw_sb[b][:],
                      in_=wd["bW_l"][nchk, :, :],
                      wait=[(s_pe, max(0, s_pe.n - 1))] if nchk >= 2 else None,
                      inc=(s_cl, 16))
            for tt2 in range(2):
                pp = st.ps[2 + (nchk * 2 + tt2) % 2]
                for i in range(NCH):
                    P.E("pe", "matmul", pp[:],
                        bw_sb[b][:, i * 128:(i + 1) * 128],
                        yown(st, i, tt2 * 512, 512),
                        start=(i == 0), stop=False,
                        wait=[(s_cl, t_w), (s_dv, t_d),
                              (s_dv, max(0, s_dv.n - 1))] if i == 0 else None)
                t_pe = P.E("pe", "matmul", pp[:],
                           bt_sb[:, nchk * 128:(nchk + 1) * 128],
                           mid_sb[:, tt2 * 512:(tt2 + 1) * 512],
                           start=False, stop=True, inc=(s_pe, 1))
                P.E("dve", "scalar_tensor_tensor", xch(st, nchk, tt2 * 512, 512),
                    pp[:], 0.5, xch(st, nchk, tt2 * 512, 512), ALU.mult, ALU.add,
                    wait=[(s_pe, t_pe)], inc=(s_dv, 1))
        P.W("dve", s_dv)
        P.E("dve", "nop", inc=(sem_done, 1))



def emit_attn1(P, st, wd, sem_y, sem_done):
    nc = P.nc
    _ph = ExitStack()
    ec = _ph.enter_context
    s_ld = P.sem("a1_ld")
    s_cl = P.sem("a1_cl")
    s_cp = P.sem("a1_cp")
    s_pe = P.sem("a1_pe")
    s_av = P.sem("a1_av")
    s_dv = P.sem("a1_dv")
    s_cc = P.sem("a1_cc")
    ar_in = nc.dram_tensor("kv_in", [64, 16 * 129], F32)
    ar_out = nc.dram_tensor("kv_out", [64, 16 * 129], F32)

    qm_sb = ec(P.sbt("qm_sb", [64, 16 * 1024], BF16))
    ctx1_sb = ec(P.sbt("ctx1_sb", [128, 16 * 1024], BF16))
    stg_sb = ec(P.sbt("stg_sb", [64, 16 * 129], F32))
    kvb_sb = ec(P.sbt("kvb_sb", [64, 16 * 130], BF16))
    rn_sb = ec(P.sbt("rn_sb", [1, 2048], F32))
    rnb_sb = ec(P.sbt("rnb_sb", [128, 1024], F32))
    with ExitStack() as ph:
        wk_sb = ph.enter_context(P.sbt("wk1_sb", [128, 16 * 128], BF16))
        wmap_sb = ph.enter_context(P.sbt("wmap_sb", [128, 2 * 64], BF16))
        bia_sb = ph.enter_context(P.sbt("bia1_sb", [128, 16 * 3], F32))
        kf_sb = ph.enter_context(P.sbt("kf_sb", [128, 1024], BF16))
        vt1_sb = ph.enter_context(P.sbt("vt1_sb", [128, 8 * 128], BF16))
        km_sb = ph.enter_context(P.sbt("km1_sb", [128, 8 * 64], BF16))
        el_sb = ph.enter_context(P.sbt("el_sb", [128, 6 * 512], F32))
        onesb = ph.enter_context(P.sbt("onesb", [128, 1], BF16))

        wmf_sb = ph.enter_context(P.sbt("wmf_sb", [128, 128], F32))
        t_f = P.E("sp", "dma_start", out=wmf_sb[:, 0:64], in_=wd["wmqT"][:, :],
                  inc=(s_ld, 16))
        t_f = P.E("sp", "dma_start", out=wmf_sb[:, 64:128], in_=wd["wmkT"][:, :],
                  inc=(s_ld, 16))
        P.E("pool", "tensor_copy", wmap_sb[:], wmf_sb[:], wait=[(s_ld, t_f)],
            inc=(s_cp, 1))
        P.E("sp", "dma_start", out=bia_sb[:, 0:48], in_=wd["bqkv_pc"][:, :],
            inc=(s_ld, 16))
        P.E("act", "activation", onesb[:], cb(st, C_ONE), AF.Copy,
            wait=[(st.ld, st.ld.n)], inc=(s_av, 1))

        def proj_head(h, wname, bcol, dst, dst_off, scale_sb=None):
            t_w = P.E("pool", "dma_start", out=wk_sb[:],
                      in_=wd[wname][h, :, :],
                      wait=[(s_pe, s_pe.n)], inc=(s_cl, 16))
            for u in range(2):
                pp = st.ps[u]
                t_pe = None
                for i in range(NCH):
                    t_pe = P.E("pe", "matmul", pp[:],
                               wk_sb[:, i * 128:(i + 1) * 128],
                               yown(st, i, u * 512, 512),
                               start=(i == 0), stop=(i == NCH - 1),
                               wait=[(s_cl, t_w), (sem_y, sem_y.n),
                                     (s_av, max(0, s_av.n - 1)),
                                     (s_dv, max(0, s_dv.n - 1))] if i == 0 else None,
                               inc=(s_pe, 1) if i == NCH - 1 else None)
                P.E("dve", "tensor_scalar",
                    dst[:, dst_off + u * 512: dst_off + u * 512 + 512],
                    pp[:], bia_sb[:, h * 3 + bcol: h * 3 + bcol + 1], None,
                    ALU.add,
                    wait=[(s_pe, t_pe), (s_ld, s_ld.n)], inc=(s_dv, 1))

        def elu1(src_ap, dst_ap, tmpo, first_wait=None, w=512, p=128):
            a1 = el_sb[0:p, tmpo:tmpo + w]
            a2 = el_sb[0:p, tmpo + 512:tmpo + 512 + w]
            P.E("act", "activation", a1, src_ap, AF.Relu, scale=-1.0, wait=first_wait)
            P.E("act", "activation", a1, a1, AF.Exp, scale=-1.0)
            t_a = P.E("act", "activation", a2, src_ap, AF.Relu, inc=(s_av, 1))
            return P.E("dve", "tensor_add", dst_ap, a1, a2,
                       wait=[(s_av, t_a)], inc=(s_dv, 1))

        for h in range(H):
            proj_head(h, "wk_l1", 1, kf_sb, 0)
            t_kf = s_dv.n
            for ts_ in range(8):
                pbk = st.psS if ts_ % 2 == 0 else st.ps[4]
                t_pe = P.E("pe", "matmul", pbk[:, 0:64],
                           kf_sb[:, ts_ * 128:(ts_ + 1) * 128], wmap_sb[:, 64:128],
                           start=True, stop=True,
                           wait=[(s_dv, t_kf), (s_cp, s_cp.n)] if ts_ == 0 else
                           ([(s_dv, s_dv.n - 1)] if ts_ >= 2 else None),
                           inc=(s_pe, 1))
                elu1(pbk[:, 0:64],
                     km_sb[:, ts_ * 64:(ts_ + 1) * 64], (ts_ % 2) * 1024,
                     first_wait=[(s_pe, t_pe)] if ts_ < 2 else
                     [(s_pe, t_pe), (s_dv, s_dv.n - 1)], w=64)
            proj_head(h, "wv_l1", 2, kf_sb, 0)
            for ts_ in range(8):
                pb = st.psT[ts_ % 2]
                col = ((ts_ // 2) % 4) * 128
                t_pe = P.E("pe", "transpose", pb[:, col:col + 128],
                           kf_sb[:, ts_ * 128:(ts_ + 1) * 128], st.ident_b[:],
                           wait=[(s_dv, s_dv.n)] if ts_ == 0 else
                           ([(s_dv, s_dv.n - 1)] if ts_ >= 2 else None), inc=(s_pe, 1))
                P.E("dve", "tensor_copy", vt1_sb[:, ts_ * 128:(ts_ + 1) * 128],
                    pb[:, col:col + 128],
                    wait=[(s_pe, t_pe)], inc=(s_dv, 1))
            t_pe = None
            for ts_ in range(8):
                t_pe = P.E("pe", "matmul", st.ps[2][0:64, 0:128],
                           km_sb[:, ts_ * 64:(ts_ + 1) * 64],
                           vt1_sb[:, ts_ * 128:(ts_ + 1) * 128],
                           start=(ts_ == 0), stop=(ts_ == 7),
                           wait=[(s_dv, s_dv.n), (s_av, s_av.n)] if ts_ == 0 else None,
                           inc=(s_pe, 1) if ts_ == 7 else None)
            for ts_ in range(8):
                t_pe2 = P.E("pe", "matmul", st.ps[3][0:64, 0:1],
                            km_sb[:, ts_ * 64:(ts_ + 1) * 64], onesb[:],
                            start=(ts_ == 0), stop=(ts_ == 7),
                            inc=(s_pe, 1) if ts_ == 7 else None)
            P.E("act", "activation", stg_sb[:, h * 129: h * 129 + 128],
                st.ps[2][0:64, 0:128], AF.Copy, wait=[(s_pe, s_pe.n)], inc=(s_av, 1))
            P.E("act", "activation", stg_sb[:, h * 129 + 128: h * 129 + 129],
                st.ps[3][0:64, 0:1], AF.Copy, inc=(s_av, 1))
        P.E("pool", "dma_start", out=ar_in[:, :], in_=stg_sb[:],
            wait=[(s_av, s_av.n)], inc=(s_cl, 16))
        P.W("pool", s_cl)
        P.E("pool", "collective_compute", "AllReduce", ALU.add,
            replica_groups=[list(range(NCORES))],
            ins=[ar_in.ap().opt()], outs=[ar_out.ap().opt()], inc=(s_cc, 1))
        t_bk = P.E("pool", "dma_start", out=stg_sb[:], in_=ar_out[:, :],
                   wait=[(s_cc, 1)], inc=(s_cl, 16))
        t_kv = P.E("act", "activation", kvb_sb[:, 0:16 * 129], stg_sb[:, 0:16 * 129],
                   AF.Copy, wait=[(s_cl, t_bk)], inc=(s_av, 1))
        for h in range(H):
            proj_head(h, "wq_l1", 0, kf_sb, 0)
            t_kf = s_dv.n
            t_qm = None
            for u in range(2):
                pp = st.ps[u]
                t_pe = P.E("pe", "matmul", pp[0:64, 0:512],
                           wmap_sb[:, 0:64],
                           kf_sb[:, u * 512:(u + 1) * 512],
                           start=True, stop=True,
                           wait=[(s_dv, t_kf), (s_dv, max(0, s_dv.n - 1))],
                           inc=(s_pe, 1))
                t_qm = elu1(pp[0:64, 0:512],
                            qm_sb[:, h * 1024 + u * 512: h * 1024 + u * 512 + 512],
                            1024 + u * 1024,
                            first_wait=[(s_pe, t_pe), (s_dv, s_dv.n)], p=64)
            t_nr = None
            for u in range(2):
                t_pe = P.E("pe", "matmul", st.ps[2 + u][0:1, 0:512],
                           kvb_sb[:, h * 129 + 128: h * 129 + 129],
                           qm_sb[:, h * 1024 + u * 512: h * 1024 + u * 512 + 512],
                           start=True, stop=True,
                           wait=[(s_av, t_kv), (s_dv, t_qm),
                                 (s_dv, max(0, s_dv.n - 1))], inc=(s_pe, 1))
                t_nr = P.E("dve", "tensor_scalar", rn_sb[0:1, u * 512:(u + 1) * 512],
                           st.ps[2 + u][0:1, 0:512],
                           st.consts[0:1, C_EPS6:C_EPS6 + 1], None, ALU.add,
                           wait=[(s_pe, t_pe)], inc=(s_dv, 1))
            t_rn = P.E("dve", "reciprocal", rn_sb[0:1, 1024:2048],
                       rn_sb[0:1, 0:1024],
                       wait=[(s_dv, t_nr)], inc=(s_dv, 1))
            for u in range(2):
                t_pe = P.E("pe", "matmul", st.ps[2 + u][:, 0:512], st.ones_row[:],
                           rn_sb[0:1, 1024 + u * 512:1024 + (u + 1) * 512],
                           start=True, stop=True,
                           wait=[(s_dv, t_rn)], inc=(s_pe, 1))
                P.E("dve", "tensor_copy", rnb_sb[:, u * 512:(u + 1) * 512],
                    st.ps[2 + u][:, 0:512], wait=[(s_pe, t_pe)], inc=(s_dv, 1))
            for u in range(2):
                t_pe = P.E("pe", "matmul", st.ps[u][:, 0:512],
                           kvb_sb[:, h * 129: h * 129 + 128],
                           qm_sb[:, h * 1024 + u * 512: h * 1024 + u * 512 + 512],
                           start=True, stop=True,
                           wait=[(s_av, t_kv), (s_dv, s_dv.n)], inc=(s_pe, 1))
                P.E("dve", "tensor_mul",
                    ctx1_sb[:, h * 1024 + u * 512: h * 1024 + u * 512 + 512],
                    st.ps[u][:, 0:512], rnb_sb[:, u * 512:(u + 1) * 512],
                    wait=[(s_pe, t_pe)], inc=(s_dv, 1))
        P.barrier()

    with ExitStack() as ph:
        wo_sb = [ph.enter_context(P.sbt(f"wo1_sb{b}", [128, 16 * 128], BF16))
                 for b in range(2)]
        bo_sb = ph.enter_context(P.sbt("bo1_sb", [128, 16], F32))
        P.E("sp", "dma_start", out=bo_sb[:], in_=wd["bo_pc"][:, :], inc=(s_ld, 16))
        t_bo = s_ld.n
        for do in range(16):
            b = do % 2
            t_w = P.E("pool", "dma_start", out=wo_sb[b][:],
                      in_=wd["wo_l1"][do, :, :],
                      wait=[(s_pe, max(0, s_pe.n - 1))] if do >= 2 else None,
                      inc=(s_cl, 16))
            for tt2 in range(2):
                pp = st.ps[(do * 2 + tt2) % 4]
                t_pe = None
                for i in range(NCH):
                    t_pe = P.E("pe", "matmul", pp[:],
                               wo_sb[b][:, i * 128:(i + 1) * 128],
                               ctx1_sb[:, i * 1024 + tt2 * 512: i * 1024 + tt2 * 512 + 512],
                               start=(i == 0), stop=(i == NCH - 1),
                               wait=[(s_cl, t_w), (s_dv, max(0, s_dv.n - 3))] if i == 0 else None,
                               inc=(s_pe, 1) if i == NCH - 1 else None)
                P.E("dve", "scalar_tensor_tensor", xch(st, do, tt2 * 512, 512),
                    pp[:], bo_sb[:, do:do + 1], xch(st, do, tt2 * 512, 512),
                    ALU.add, ALU.add,
                    wait=[(s_pe, t_pe), (s_ld, t_bo)], inc=(s_dv, 1))
    P.W("dve", s_dv)
    P.E("dve", "nop", inc=(sem_done, 1))
    _ph.close()



def _din(nc, name, shape, dtype):
    return nc.dram_tensor(name, list(shape), dtype, kind="ExternalInput").ap()


def build_prog(kind):
    scratch = 8192 if kind == "ffn0" else 16384
    nc = bass.Bass("TRN2", target_bir_lowering=False, debug=False,
                   dynamic_dma_scratch_size=scratch, detect_race_conditions=False)
    stack = ExitStack()
    with stack:
        P = Prog(nc, stack)
        x_in = _din(nc, "x_in", [TPC, 2048], F32)
        x_out = nc.dram_tensor("x_out", [TPC, 2048], F32, kind="ExternalOutput").ap()
        wd = {}
        if kind in ("ffn0", "mega"):
            wd["wg_pt"] = _din(nc, "wg_pt", [1024, NFF], U8)
            wd["wu_pt"] = _din(nc, "wu_pt", [1024, NFF], U8)
            wd["wd_pt"] = _din(nc, "wd_pt", [8192, 2048], U8)
            wd["sc_g"] = _din(nc, "sc_g", [64, NFF], F32)
            wd["sc_u"] = _din(nc, "sc_u", [64, NFF], F32)
            wd["sc_d"] = _din(nc, "sc_d", [512, 2048], F32)
        if kind in ("attn0", "mega"):
            for nm in ("wq_l0", "wk_l0", "wv_l0", "wo_l0", "wkm_l"):
                wd[nm] = _din(nc, nm, [16, 128, 2048], F32)
            wd["wvm_l"] = _din(nc, "wvm_l", [4, 128, 8192], F32)
            wd["gw_pc"] = _din(nc, "gw_pc", [128, 16], F32)
            wd["gb0"] = _din(nc, "gb0", [1, 1], F32)
            wd["memT_sl"] = _din(nc, "memT_sl", [2048, MSL], F32)
            if kind == "attn0":
                halo_in = _din(nc, "halo_in", [HALO, 2048], F32)
            else:
                sel_d = _din(nc, "sel_pc", [128, 8], F32)
        if kind in ("ffn1", "mega"):
            wd["bW_l"] = _din(nc, "bW_l", [16, 128, 2048], F32)
            wd["rw_pc"] = _din(nc, "rw_pc", [128, 512], F32)
            wd["at_pc"] = _din(nc, "at_pc", [128, 1024], F32)
            wd["BT1"] = _din(nc, "BT1", [64, 2048], F32)
            wd["bias1"] = _din(nc, "bias1", [64, 1], F32)
            wd["r64"] = _din(nc, "r64", [32, 64], F32)
        if kind in ("attn1", "mega"):
            for nm in ("wq_l1", "wk_l1", "wv_l1", "wo_l1"):
                wd[nm] = _din(nc, nm, [16, 128, 2048], F32)
            wd["bqkv_pc"] = _din(nc, "bqkv_pc", [128, 48], F32)
            wd["bo_pc"] = _din(nc, "bo_pc", [128, 16], F32)
            wd["wmqT"] = _din(nc, "wmqT", [128, 64], F32)
            wd["wmkT"] = _din(nc, "wmkT", [128, 64], F32)

        is_attn = kind in ("attn0", "attn1", "mega")
        st = setup_common(P, nc, text=TPC, want_identb=is_attn,
                          nps=5 if is_attn else 6, npsT=2 if is_attn else 0)
        s_x = P.sem("sx")
        s_o = P.sem("so")

        def attn0_block(sem_after_load, halo_src_fn):
            ph_byp = ExitStack()
            byp_sb = ph_byp.enter_context(P.sbt("byp_sb", [128, 8], F32))
            ph_yh = ExitStack()
            yh_sb = ph_yh.enter_context(
                P.sbt("yh_sb", [128, NCH * HALO], BF16))
            ph_h = ExitStack()
            halo_sb = ph_h.enter_context(
                P.sbt("halo_sb", [128, NCH * HALO], F32))
            s_h = P.sem("sh")
            halo_src_fn(halo_sb, s_h)
            P.barrier()
            s_y = P.sem("sy_a0")
            emit_ln(P, st, 0, TPC, sem_after_load, s_y, tag="a0m")
            P.barrier()
            emit_ln(P, st, 0, HALO, s_h, s_y,
                    x_src=lambda i, f0, w: halo_sb[:, i * HALO + f0: i * HALO + f0 + w],
                    dst_fn=lambda i, f0, w: yh_sb[:, i * HALO + f0: i * HALO + f0 + w],
                    tag="a0h")
            P.barrier()
            ph_h.close()
            emit_attn0(P, st, wd, s_y, P.sem("sf_a0"), stop_after=A0_STOP,
                       yh=lambda i, f0, w: yh_sb[:, i * HALO + f0: i * HALO + f0 + w],
                       yh_stack=ph_yh, byp_sb=byp_sb)
            ph_byp.close()
            P.barrier()

        if kind == "attn0":
            emit_load_x(P, st, x_in, s_x)
            P.barrier()

            def fill_halo(halo_sb, s_h):
                emit_load_x(P, st, halo_in, s_h, into_fn=lambda i, t0:
                            halo_sb[:, i * HALO + t0: i * HALO + t0 + 128])
            attn0_block(s_x, fill_halo)
        elif kind == "ffn0":
            emit_load_x(P, st, x_in, s_x)
            P.barrier()
            s_y = P.sem("sy")
            emit_ln(P, st, 0, TPC, s_x, s_y, tag="m")
            P.barrier()
            emit_ffn0(P, st, wd, s_y, P.sem("sf"))
            P.barrier()
        elif kind == "ffn1":
            emit_load_x(P, st, x_in, s_x)
            P.barrier()
            s_y = P.sem("sy")
            emit_ln(P, st, 0, TPC, s_x, s_y, tag="m")
            P.barrier()
            emit_ffn1(P, st, wd, s_y, P.sem("sf"))
            P.barrier()
        elif kind == "attn1":
            emit_load_x(P, st, x_in, s_x)
            P.barrier()
            s_y = P.sem("sy")
            emit_ln(P, st, 0, TPC, s_x, s_y, tag="m")
            P.barrier()
            emit_attn1(P, st, wd, s_y, P.sem("sf"))
            P.barrier()
        else:
            stop = MEGA_STOP or 6
            emit_load_x(P, st, x_in, s_x)
            P.barrier()
            s_y = P.sem("sy1")
            emit_ln(P, st, 0, TPC, s_x, s_y, tag="l1")
            P.barrier()
            scr = {
                "g": nc.dram_tensor("scr_g", [NSC, 128, NCH * 512], BF16).ap(),
                "u": nc.dram_tensor("scr_u", [NSC, 128, NCH * 512], BF16).ap(),
                "d": nc.dram_tensor("scr_d", [NSC, 128, 4 * 2048], BF16).ap(),
            }
            sf1 = P.sem("sf1")
            emit_ffn0(P, st, wd, s_y, sf1, scratch=scr, mode="decode")
            P.barrier()

            if stop >= 2:
                def fill_halo(halo_sb, s_h):
                    emit_halo_gather(P, st, sel_d, halo_sb, sf1, s_h)
                attn0_block(sf1, fill_halo)

            if stop >= 3:
                s_y3 = P.sem("sy3")
                emit_ln(P, st, 0, TPC, P.sem("dummy3"), s_y3, tag="l3")
                P.barrier()
                emit_ffn0(P, st, wd, s_y3, P.sem("sf3"), scratch=scr,
                          mode="load")
                P.barrier()

            if stop >= 4:
                s_y4 = P.sem("sy4")
                emit_ln(P, st, 0, TPC, P.sem("dummy4"), s_y4, tag="l4")
                P.barrier()
                emit_ffn1(P, st, wd, s_y4, P.sem("sf4"))
                P.barrier()

            if stop >= 5:
                s_y5 = P.sem("sy5")
                emit_ln(P, st, 0, TPC, P.sem("dummy5"), s_y5, tag="l5")
                P.barrier()
                emit_attn1(P, st, wd, s_y5, P.sem("sf5"))
                P.barrier()

            if stop >= 6:
                s_y6 = P.sem("sy6")
                emit_ln(P, st, 0, TPC, P.sem("dummy6"), s_y6, tag="l6")
                P.barrier()
                emit_ffn1(P, st, wd, s_y6, P.sem("sf6"))
                P.barrier()

        emit_store_x(P, st, x_out, s_o)
        P.finalize()
    return nc


A0_STOP = None
MEGA_STOP = None
_PROGS = {}


def _get_prog(kind):
    if kind not in _PROGS:
        _PROGS[kind] = build_prog(kind)
    return _PROGS[kind]


MEGA = True


def kernel(**inputs):
    wd = host_prepare(inputs)
    cores = list(range(NCORES))
    common = {k: wd[k] for k in ("consts", "ones_row", "ident_f", "ident_b")}

    if MEGA:
        nc = _get_prog("mega")
        shared = {}
        for k in ("wg_pt", "wu_pt", "wd_pt", "sc_g", "sc_u", "sc_d",
                  "wq_l0", "wk_l0", "wv_l0", "wo_l0", "wkm_l", "wvm_l",
                  "gw_pc", "gb0",
                  "bW_l", "rw_pc", "at_pc", "BT1", "bias1", "r64",
                  "wq_l1", "wk_l1", "wv_l1", "wo_l1", "bqkv_pc", "bo_pc",
                  "wmqT", "wmkT"):
            shared[k] = wd[k]
        maps = []
        for c in cores:
            m = dict(common)
            m.update(shared)
            m["x_in"] = wd["x_sl"][c]
            m["memT_sl"] = wd["memT_sl"][c]
            m["masks"] = wd["masks"][c]
            m["sel_pc"] = wd["sel_pc"][c]
            maps.append(m)
        res = run_bass_kernel_spmd(nc, maps, core_ids=cores)
        x = [np.asarray(res.results[c]["x_out"]) for c in cores]
        return np.concatenate(x, axis=0).reshape(1, 8192, 2048).astype(np.float32)

    def launch(kind, x_cur, extra_per_core=None, shared=None):

        nc = _get_prog(kind)
        maps = []
        for c in cores:
            m = dict(common)
            m["x_in"] = x_cur[c]
            if shared:
                m.update(shared)
            if extra_per_core:
                m.update(extra_per_core[c])
            maps.append(m)
        res = run_bass_kernel_spmd(nc, maps, core_ids=cores)
        return [np.asarray(res.results[c]["x_out"]) for c in cores]

    sh_ffn0 = {k: wd[k] for k in ("wg_pt", "wu_pt", "wd_pt", "sc_g", "sc_u", "sc_d")}
    sh_attn0 = {k: wd[k] for k in ("wq_l0", "wk_l0", "wv_l0", "wo_l0", "wkm_l",
                                   "wvm_l", "gw_pc", "gb0")}
    sh_ffn1 = {k: wd[k] for k in ("bW_l", "rw_pc", "at_pc", "BT1", "bias1", "r64")}
    sh_attn1 = {k: wd[k] for k in ("wq_l1", "wk_l1", "wv_l1", "wo_l1",
                                   "bqkv_pc", "bo_pc", "wmqT", "wmkT")}

    x = wd["x_sl"]
    x = launch("ffn0", x, shared=sh_ffn0)
    halo = [np.zeros((HALO, 2048), np.float32) if c == 0 else
            np.ascontiguousarray(x[c - 1][-HALO:]) for c in cores]
    extra = [{"halo_in": halo[c], "memT_sl": wd["memT_sl"][c],
              "masks": wd["masks"][c]} for c in cores]
    x = launch("attn0", x, extra_per_core=extra, shared=sh_attn0)
    x = launch("ffn0", x, shared=sh_ffn0)
    x = launch("ffn1", x, shared=sh_ffn1)
    x = launch("attn1", x, shared=sh_attn1)
    x = launch("ffn1", x, shared=sh_ffn1)
    out = np.concatenate(x, axis=0).reshape(1, 8192, 2048).astype(np.float32)
    return out
